# version 09
# baseline (speedup 1.0000x reference)
import numpy as np
import ml_dtypes
from contextlib import ExitStack

import concourse.bass as bass
import concourse.mybir as mybir
from concourse.bass_utils import run_bass_kernel_spmd

F32 = mybir.dt.float32
BF16 = mybir.dt.bfloat16
AF = mybir.ActivationFunctionType
ALU = mybir.AluOpType
AX = mybir.AxisListType
BF = ml_dtypes.bfloat16

D = 2048
T = 2048
KD = 16
NQ = 4
H = 16
KF = 44
EPS_LN = 1e-5
NEG_BIG = -3.0e38
FFN_CNT = 5632 * 2048

ENGINES = ("tensor", "vector", "scalar", "gpsimd", "sync")


class Ctr:
    __slots__ = ("h", "count", "name")

    def __init__(self, h, name):
        self.h = h
        self.count = 0
        self.name = name


class Sched:
    def __init__(self, nc, st):
        self.nc = nc
        self.st = st
        self.q = {e: [] for e in ENGINES}
        self.sems = {}

    def sem(self, name):
        if name not in self.sems:
            h = self.st.enter_context(self.nc.semaphore(name))
            self.sems[name] = Ctr(h, name)
        return self.sems[name]

    def wait(self, eng, ctr, val=None):
        v = ctr.count if val is None else val
        if v <= 0:
            return
        self.q[eng].append(lambda e, h=ctr.h, v=v: e.wait_ge(h, v))

    def emit(self, eng, fn, inc=None, inc_n=1):
        if inc is not None:
            inc.count += inc_n

        def run(e, fn=fn, inc=inc, n=inc_n):
            ins = fn(e)
            if inc is not None:
                ins.then_inc(inc.h, n)

        self.q[eng].append(run)

    def raw(self, eng, fn):
        self.q[eng].append(fn)

    def fence(self):
        for name, c in list(self.sems.items()):
            if name.startswith("dma_"):
                self.wait("sync", c)
        for eng in ENGINES:
            f = self.sem("fence_" + eng)
            if eng in ("tensor", "sync"):
                self.emit(eng, lambda e: e.nop(), inc=f)
            else:
                self.emit(eng, lambda e: e.drain(fusable=False), inc=f)
        for eng in ENGINES:
            for other in ENGINES:
                if other != eng:
                    self.wait(eng, self.sem("fence_" + other))

    def build(self):
        block = self.st.enter_context(self.nc.Block())

        def mk(eng):
            def body(e):
                for f in self.q[eng]:
                    f(e)
            return body

        block.tensor(mk("tensor"))
        block.vector(mk("vector"))
        block.scalar(mk("scalar"))
        block.gpsimd(mk("gpsimd"))
        block.sync(mk("sync"))


def tile_lhsT(Wt, groups=1):
    Fd, Cd = Wt.shape
    nf, kc = Fd // 128, Cd // 128
    L = Wt.reshape(nf, 128, kc, 128).transpose(0, 3, 2, 1)
    if groups == 1:
        return np.ascontiguousarray(L.reshape(nf, 128, kc * 128))
    kg = kc // groups
    Lg = L.reshape(nf, 128, groups, kg, 128).transpose(2, 0, 1, 3, 4)
    return np.ascontiguousarray(Lg.reshape(groups, nf, 128, kg * 128))


def chunk_vec(b):
    return np.ascontiguousarray(np.asarray(b).reshape(-1, 128).T)


def rope_tables():
    pos = np.arange(T, dtype=np.float64)
    idx = np.arange(64, dtype=np.float64)
    theta = (10000.0 ** (-2.0 * idx / 128.0))[:, None] * pos[None, :]
    C = np.repeat(np.cos(theta), 2, axis=0).astype(BF)
    S = np.empty((128, T), np.float64)
    S[0::2] = -np.sin(theta)
    S[1::2] = np.sin(theta)
    return C, S.astype(BF)


def build_program(stop_after="full"):
    nc = bass.Bass("TRN2", target_bir_lowering=False)

    def ein(name, shape):
        return nc.dram_tensor(name, shape, F32, kind="ExternalInput").ap()

    xt = ein("xt", [D, T])
    win_t = ein("win_t", [64, 128, 2048])
    wout_t = ein("wout_t", [2, 16, 128, 2048])
    wm1_t = ein("wm1_t", [64, 128, 2048])
    wm2_t = ein("wm2_t", [4, 16, 128, 2048])
    w1_t = ein("w1_t", [KF, 128, 2048])
    w2_t = ein("w2_t", [KF, 128, 2048])
    w3_t = ein("w3_t", [4, 16, 128, 1408])
    wdkv_t = ein("wdkv_t", [4, 128, 2048])
    wuk_t = ein("wuk_t", [16, 128, 512])
    wuv_t = ein("wuv_t", [16, 128, 512])
    wdq_t = ein("wdq_t", [8, 128, 2048])
    wuq_t = ein("wuq_t", [16, 128, 1024])
    wkr_t = ein("wkr_t", [16, 128, 2048])
    wo_t = ein("wo_t", [16, 128, 2048])
    convw_d = ein("convw", [128, 32, 4])
    convb_d = ein("convb", [128, 32])
    bdkv_d = ein("bdkv", [128, 4])
    buk_d = ein("buk", [128, 16])
    buv_row_d = ein("buv_row", [1, 2048])
    bdq_d = ein("bdq", [128, 8])
    buq_d = ein("buq", [128, 16])
    bkr_d = ein("bkr", [128, 16])
    bo_d = ein("bo", [128, 16])

    import os as _os
    _dbg = set(filter(None, _os.environ.get("MK_DBG", "").split(",")))

    def _scratch(name, shape, dt):
        kind = "ExternalOutput" if name in _dbg else "Internal"
        return nc.dram_tensor(name, shape, dt, kind=kind).ap()

    outT = nc.dram_tensor("outT", [KD, 128, T], F32, kind="ExternalOutput").ap()
    Rm = _scratch("Rm", [KD, 128, T], F32)
    ysp = _scratch("ysp", [32, 128, T], BF16)
    usp = _scratch("usp", [64, 128, T], BF16)
    sgsp = _scratch("sgsp", [KF, 128, T], BF16)
    hidsp = _scratch("hidsp", [KF, 128, T], BF16)
    osp = _scratch("osp", [16, 128, T], BF16)
    wq1sp = _scratch("wq1sp", [KF, 128, 2048], BF16)
    wq2sp = _scratch("wq2sp", [KF, 128, 2048], BF16)
    wq3sp = _scratch("wq3sp", [4, 16, 128, 1408], BF16)
    woutbf = _scratch("woutbf", [2, 16, 128, 2048], BF16)
    wm2bf = _scratch("wm2bf", [4, 16, 128, 2048], BF16)
    hsp = _scratch("hsp", [KD, 128, T], BF16) if "hsp" in _dbg else None
    scdbg = _scratch("scdbg", [4, 128, 512], F32) if "scdbg" in _dbg else None

    ident_c = nc.inline_tensor(np.eye(128, dtype=BF), "ident_c")
    onescol_c = nc.inline_tensor(np.ones((128, 1), dtype=BF), "onescol_c")
    onesrow_f_c = nc.inline_tensor(np.ones((1, 128), np.float32), "onesrow_f_c")
    onescolf_c = nc.inline_tensor(np.ones((128, 1), np.float32), "onescolf_c")
    epsc_np = np.zeros((128, 2), np.float32)
    epsc_np[:, 0] = EPS_LN
    epsc_np[:, 1] = 0.5e-8
    epsc_c = nc.inline_tensor(epsc_np, "epsc_c")
    swapP_np = np.zeros((128, 128), dtype=BF)
    for i in range(64):
        swapP_np[2 * i, 2 * i + 1] = 1
        swapP_np[2 * i + 1, 2 * i] = 1
    swapP_c = nc.inline_tensor(swapP_np, "swapP_c")
    tril_np = np.where(np.arange(128)[None, :] <= np.arange(128)[:, None],
                       np.float32(0.0), np.float32(NEG_BIG)).astype(np.float32)
    tril_c = nc.inline_tensor(tril_np, "tril_c")
    ropeC_np, ropeS_np = rope_tables()
    ropeC_c = nc.inline_tensor(ropeC_np, "ropeC_c")
    ropeS_c = nc.inline_tensor(ropeS_np, "ropeS_c")

    st = ExitStack()
    with st:
        RES = st.enter_context(nc.sbuf_tensor("RES", [128, KD, T], BF16))
        STG = [st.enter_context(nc.sbuf_tensor(f"STG{i}", [128, 2048], F32)) for i in range(2)]
        PAN = [st.enter_context(nc.sbuf_tensor(f"PAN{i}", [128, 2048], BF16)) for i in range(2)]
        EVB = [st.enter_context(nc.sbuf_tensor(f"EVB{i}", [128, 2048], BF16)) for i in range(2)]
        ident = st.enter_context(nc.sbuf_tensor("ident", [128, 128], BF16))
        onescol = st.enter_context(nc.sbuf_tensor("onescol", [128, 1], BF16))
        onesrow_f = st.enter_context(nc.sbuf_tensor("onesrow_f", [1, 128], F32))
        convw = st.enter_context(nc.sbuf_tensor("convw_sb", [128, 32, 4], F32))
        convb = st.enter_context(nc.sbuf_tensor("convb_sb", [128, 32], F32))
        bdkv = st.enter_context(nc.sbuf_tensor("bdkv_sb", [128, 4], F32))
        buk = st.enter_context(nc.sbuf_tensor("buk_sb", [128, 16], F32))
        buv_row = st.enter_context(nc.sbuf_tensor("buv_row_sb", [1, 2048], F32))
        bdq = st.enter_context(nc.sbuf_tensor("bdq_sb", [128, 8], F32))
        buq16 = st.enter_context(nc.sbuf_tensor("buq16", [128, 16], F32))
        bkr = st.enter_context(nc.sbuf_tensor("bkr_sb", [128, 16], F32))
        bo = st.enter_context(nc.sbuf_tensor("bo_sb", [128, 16], F32))
        gacc = st.enter_context(nc.sbuf_tensor("gacc", [128, 4], F32))
        epsc = st.enter_context(nc.sbuf_tensor("epsc", [128, 2], F32))
        gtmp = st.enter_context(nc.sbuf_tensor("gtmp", [128, 1], F32))
        thrv = st.enter_context(nc.sbuf_tensor("thrv", [128, 4], F32))
        nthrv = st.enter_context(nc.sbuf_tensor("nthrv", [128, 4], F32))

        s = Sched(nc, st)
        d_stg = [s.sem("dma_stg0"), s.sem("dma_stg1")]
        d_grp = [s.sem("dma_grp0"), s.sem("dma_grp1")]
        d_res = s.sem("dma_res")
        d_xb = s.sem("dma_xb")
        d_stv = s.sem("dma_stv")
        d_ev = s.sem("dma_ev")
        d_acc = s.sem("dma_acc")
        d_wq = s.sem("dma_wq")
        d_side = s.sem("dma_side")
        d_sta = s.sem("dma_sta")
        d_misc = s.sem("dma_misc")
        d_g = [s.sem("dma_g0"), s.sem("dma_g1")]
        c_pool = s.sem("c_pool")
        c_pan = [s.sem("c_pan0"), s.sem("c_pan1")]
        c_mm = [s.sem("c_mm0"), s.sem("c_mm1")]
        c_evv = s.sem("c_evv")
        c_eva = s.sem("c_eva")
        c_pe = s.sem("c_pe")
        c_g = s.sem("c_g")

        for dst, src in ((epsc, epsc_c.ap()), (ident, ident_c.ap()), (onescol, onescol_c.ap()),
                         (onesrow_f, onesrow_f_c.ap()), (convw, convw_d),
                         (convb, convb_d), (bdkv, bdkv_d), (buk, buk_d),
                         (buv_row, buv_row_d), (bdq, bdq_d), (bkr, bkr_d),
                         (bo, bo_d), (buq16, buq_d)):
            s.emit("sync", lambda e, dst=dst, src=src: e.dma_start(out=dst[:], in_=src),
                   inc=d_misc, inc_n=16)
        s.wait("vector", d_misc)
        s.emit("vector", lambda e: e.tensor_scalar(
            out=buq16[:], in0=buq16[:], scalar1=1.0 / 16.0, scalar2=None, op0=ALU.mult),
            inc=c_evv)
        s.emit("gpsimd", lambda e: e.memset(gacc[:], 0.0), inc=c_pool)
        s.wait("tensor", d_misc)

        uid = [0]

        def mktile(scope, shape, dt):
            uid[0] += 1
            return scope.enter_context(nc.sbuf_tensor(f"t{uid[0]}", shape, dt))

        def mkpsum(scope, shape, dt=F32):
            uid[0] += 1
            return scope.enter_context(nc.psum_tensor(f"p{uid[0]}", shape, dt))

        def load_res_stats(scope, src_chunks, mode):
            SQ = [mktile(scope, [128, T], BF16) for _ in range(2)]
            SR1 = mktile(scope, [1, T], F32)
            SR2 = mktile(scope, [1, T], F32)
            nbt = 3 if mode == "A" else 2
            BT = [mktile(scope, [128, T], BF16) for _ in range(nbt)]
            BTS = mktile(scope, [128, T], BF16)
            cast_c = []
            sq_c = []
            with ExitStack() as ps_sc:
                PS_S = mkpsum(ps_sc, [1, T])
                PS_Q = mkpsum(ps_sc, [1, T])
                for j in range(KD):
                    pr = j % 2
                    if j >= 2:
                        s.wait("sync", c_pan[pr], cast_c[j - 2])
                    s.emit("sync", lambda e, j=j, pr=pr: e.dma_start(
                        out=STG[pr][:], in_=src_chunks[j]), inc=d_stg[pr], inc_n=16)
                    ceng = "scalar" if pr == 0 else "vector"
                    s.wait(ceng, d_stg[pr])
                    if pr == 0:
                        s.emit("scalar", lambda e, j=j, pr=pr: e.activation(
                            out=RES[:, j, :], in_=STG[pr][:], func=AF.Identity), inc=c_pan[0])
                    else:
                        s.emit("vector", lambda e, j=j, pr=pr: e.tensor_copy(
                            out=RES[:, j, :], in_=STG[pr][:]), inc=c_pan[1])
                    cast_c.append(c_pan[pr].count)
                    s.wait("scalar", c_pan[pr], cast_c[j])
                    if j >= 2:
                        s.wait("scalar", c_pe, sq_c[j - 2])
                    s.emit("scalar", lambda e, j=j, pr=pr: e.activation(
                        out=SQ[pr][:], in_=RES[:, j, :], func=AF.Square), inc=c_eva)
                    sq_done = c_eva.count
                    s.wait("tensor", c_pan[pr], cast_c[j])
                    for q in range(NQ):
                        s.raw("tensor", lambda e, j=j, q=q: e.matmul(
                            PS_S[:, q * 512:(q + 1) * 512], onescol[:],
                            RES[:, j, q * 512:(q + 1) * 512],
                            start=(j == 0), stop=(j == KD - 1)))
                    s.wait("tensor", c_eva, sq_done)
                    for q in range(NQ):
                        fn = lambda e, pr=pr, q=q, j=j: e.matmul(
                            PS_Q[:, q * 512:(q + 1) * 512], onescol[:],
                            SQ[pr][:, q * 512:(q + 1) * 512],
                            start=(j == 0), stop=(j == KD - 1))
                        if q == NQ - 1:
                            s.emit("tensor", fn, inc=c_pe)
                        else:
                            s.raw("tensor", fn)
                    sq_c.append(c_pe.count)
                s.wait("vector", c_pe)
                s.emit("vector", lambda e: e.tensor_copy(out=SR1[:], in_=PS_S[:, :]), inc=c_evv)
                s.emit("vector", lambda e: e.tensor_copy(out=SR2[:], in_=PS_Q[:, :]), inc=c_evv)
                rows_done = c_evv.count
            with ExitStack() as ps_sc:
                PSB = mkpsum(ps_sc, [128, 2048])
                s.wait("tensor", c_evv, rows_done)
                for q in range(NQ):
                    fn = lambda e, q=q: e.matmul(
                        PSB[:, q * 512:(q + 1) * 512], onesrow_f[:],
                        SR1[:, q * 512:(q + 1) * 512], start=True, stop=True)
                    s.emit("tensor", fn, inc=c_pe) if q == NQ - 1 else s.raw("tensor", fn)
                s.wait("scalar", c_pe)
                s.emit("scalar", lambda e: e.activation(
                    out=BT[0][:], in_=PSB[:, :], func=AF.Identity, scale=1.0 / D), inc=c_eva)
                s.wait("tensor", c_eva)
                for q in range(NQ):
                    fn = lambda e, q=q: e.matmul(
                        PSB[:, q * 512:(q + 1) * 512], onesrow_f[:],
                        SR2[:, q * 512:(q + 1) * 512], start=True, stop=True)
                    s.emit("tensor", fn, inc=c_pe) if q == NQ - 1 else s.raw("tensor", fn)
                s.wait("scalar", c_pe)
                s.emit("scalar", lambda e: e.activation(
                    out=BT[1][:], in_=PSB[:, :], func=AF.Identity, scale=1.0 / D), inc=c_eva)
            s.wait("vector", c_eva)
            s.emit("vector", lambda e: e.tensor_mul(out=BTS[:], in0=BT[0][:], in1=BT[0][:]), inc=c_evv)
            s.emit("vector", lambda e: e.tensor_sub(out=BT[1][:], in0=BT[1][:], in1=BTS[:]), inc=c_evv)
            s.wait("scalar", c_evv)
            if mode == "A":
                s.emit("scalar", lambda e: e.activation(
                    out=BT[2][:], in_=BT[1][:], func=AF.Ln, bias=epsc[:, 0:1]), inc=c_eva)
                s.emit("scalar", lambda e: e.activation(
                    out=BT[1][:], in_=BT[1][:], func=AF.Ln), inc=c_eva)
                s.wait("vector", c_eva)
                s.emit("vector", lambda e: e.tensor_sub(out=BT[2][:], in0=BT[1][:], in1=BT[2][:]), inc=c_evv)
                s.wait("scalar", c_evv)
                s.emit("scalar", lambda e: e.activation(
                    out=BT[2][:], in_=BT[2][:], func=AF.Exp, scale=0.5), inc=c_eva)
                s.emit("scalar", lambda e: e.activation(
                    out=BT[1][:], in_=BT[1][:], func=AF.Exp, scale=-0.5), inc=c_eva)
                return BT[0], BT[1], BT[2], BTS
            else:
                s.emit("scalar", lambda e: e.activation(
                    out=BT[1][:], in_=BT[1][:], func=AF.Ln, bias=epsc[:, 0:1]), inc=c_eva)
                s.emit("scalar", lambda e: e.activation(
                    out=BT[1][:], in_=BT[1][:], func=AF.Exp, scale=-0.5), inc=c_eva)
                return BT[0], BT[1], BTS

        def center_res(Bmu, Bscale, scr):
            s.wait("vector", c_eva)
            for j in range(KD):
                s.emit("vector", lambda e, j=j: e.tensor_sub(
                    out=scr[:], in0=RES[:, j, :], in1=Bmu[:]), inc=c_evv)
                s.emit("vector", lambda e, j=j: e.tensor_mul(
                    out=RES[:, j, :], in0=scr[:], in1=Bscale[:]), inc=c_evv)

        def stream_panels(panels, kc, psA, psB, evac_fn, quant=None,
                          pool_extra=None, bf16_src=False, rhs=None, rhs_kc=None,
                          spills=None):
            if rhs is None:
                rhs, rhs_kc = RES, KD
            for qq in ("sync", "gpsimd"):
                s.wait(qq, c_mm[0])
                s.wait(qq, c_mm[1])
                s.wait(qq, c_pan[0])
                s.wait(qq, c_pan[1])
            wid = kc * 128
            n = len(panels)
            cast_c = [None] * n
            mm_c = [None] * n
            free_c = [None] * n
            for i in range(n):
                pr = i % 2
                if bf16_src:
                    if i >= 2:
                        s.wait("sync", c_mm[pr], mm_c[i - 2])
                    s.emit("sync", lambda e, i=i, pr=pr: e.dma_start(
                        out=PAN[pr][:, 0:wid], in_=panels[i]), inc=d_stg[pr], inc_n=16)
                    s.wait("tensor", d_stg[pr])
                else:
                    if i >= 2:
                        s.wait("sync", c_pan[pr], cast_c[i - 2])
                    s.emit("sync", lambda e, i=i, pr=pr: e.dma_start(
                        out=STG[pr][:, 0:wid], in_=panels[i]), inc=d_stg[pr], inc_n=16)
                    ceng = ("scalar" if pr == 0 else "vector") if quant is None else "vector"
                    s.wait(ceng, d_stg[pr])
                    if i >= 2:
                        s.wait(ceng, c_mm[pr], mm_c[i - 2])
                    if spills is not None and i >= 2 and spills[i - 2] is not None:
                        s.wait(ceng, d_wq)
                    if quant is None:
                        if ceng == "scalar":
                            s.emit("scalar", lambda e, pr=pr: e.activation(
                                out=PAN[pr][:, 0:wid], in_=STG[pr][:, 0:wid],
                                func=AF.Identity), inc=c_pan[pr])
                        else:
                            s.emit("vector", lambda e, pr=pr: e.tensor_copy(
                                out=PAN[pr][:, 0:wid], in_=STG[pr][:, 0:wid]), inc=c_pan[pr])
                    else:
                        m, QA, QB = quant
                        s.raw("vector", lambda e, pr=pr: e.tensor_scalar(
                            out=QA[:, 0:wid], in0=STG[pr][:, 0:wid],
                            scalar1=thrv[:, m:m + 1], scalar2=1.0,
                            op0=ALU.is_gt, op1=ALU.mult))
                        s.raw("vector", lambda e, pr=pr: e.tensor_scalar(
                            out=QB[:, 0:wid], in0=STG[pr][:, 0:wid],
                            scalar1=nthrv[:, m:m + 1], scalar2=1.0,
                            op0=ALU.is_ge, op1=ALU.mult))
                        s.emit("vector", lambda e, pr=pr: e.scalar_tensor_tensor(
                            out=PAN[pr][:, 0:wid], in0=QA[:, 0:wid], scalar=-1.0,
                            in1=QB[:, 0:wid], op0=ALU.add, op1=ALU.add), inc=c_pan[pr])
                    cast_c[i] = c_pan[pr].count
                    if spills is not None and spills[i] is not None:
                        s.wait("scalar", c_pan[pr], cast_c[i])
                        s.emit("scalar", lambda e, i=i, pr=pr: e.dma_start(
                            out=spills[i], in_=PAN[pr][:, 0:wid]), inc=d_wq, inc_n=16)
                    s.wait("tensor", c_pan[pr], cast_c[i])
                if pool_extra is not None:
                    pool_extra(i)
                if i >= 2:
                    s.wait("tensor", *free_c[i - 2])
                ps = psA if pr == 0 else psB
                for t in range(NQ):
                    for k in range(kc):
                        fn = lambda e, ps=ps, pr=pr, k=k, t=t: e.matmul(
                            ps[:, t * 512:(t + 1) * 512],
                            PAN[pr][:, k * 128:(k + 1) * 128],
                            rhs[:, k, t * 512:(t + 1) * 512],
                            start=(k == 0), stop=(k == kc - 1))
                        if t == NQ - 1 and k == kc - 1:
                            s.emit("tensor", fn, inc=c_mm[pr])
                        else:
                            s.raw("tensor", fn)
                mm_c[i] = c_mm[pr].count
                free_c[i] = evac_fn(i, ps, pr)

        def acc_stage(qg_panels, kc_g, ngrp, grp_src_fn, ACC, GRP, evac_q_fn,
                      bf16_pan=False, side_fn=None):
            wid = kc_g * 128
            PQ = None
            with ExitStack() as ps_sc:
                PQ = [mkpsum(ps_sc, [128, 512]) for _ in range(2)]
                pi = 0
                grp_mm = {}
                pan_cast = {}
                pan_mm = {}
                ps_free = {}
                for q in range(NQ):
                    for g in range(ngrp):
                        gpr = g % 2
                        prev = (q, g - 2) if g >= 2 else ((q - 1, ngrp - 2 + g) if q >= 1 else None)
                        if prev in grp_mm:
                            s.wait("sync", c_mm[0], grp_mm[prev])
                        s.emit("sync", lambda e, q=q, g=g, gpr=gpr: e.dma_start(
                            out=GRP[gpr][:], in_=grp_src_fn(q, g)), inc=d_grp[gpr], inc_n=16)
                        s.wait("tensor", d_grp[gpr])
                        for f in range(16):
                            pr = pi % 2
                            if bf16_pan:
                                if pi >= 2:
                                    s.wait("sync", c_mm[0], pan_mm.get(pi - 2, 0))
                                s.emit("sync", lambda e, q=q, g=g, f=f, pr=pr: e.dma_start(
                                    out=PAN[pr][:, 0:wid], in_=qg_panels[q][g][f]),
                                    inc=d_stg[pr], inc_n=16)
                                s.wait("tensor", d_stg[pr])
                            else:
                                if pi >= 2:
                                    s.wait("sync", c_pan[pr], pan_cast.get(pi - 2, 0))
                                s.emit("sync", lambda e, q=q, g=g, f=f, pr=pr: e.dma_start(
                                    out=STG[pr][:, 0:wid], in_=qg_panels[q][g][f]),
                                    inc=d_stg[pr], inc_n=16)
                                s.wait("gpsimd", d_stg[pr])
                                if pi >= 2:
                                    s.wait("gpsimd", c_mm[0], pan_mm.get(pi - 2, 0))
                                s.emit("gpsimd", lambda e, pr=pr: e.tensor_copy(
                                    out=PAN[pr][:, 0:wid], in_=STG[pr][:, 0:wid]),
                                    inc=c_pan[pr])
                                pan_cast[pi] = c_pan[pr].count
                                s.wait("tensor", c_pan[pr], pan_cast[pi])
                            if pi >= 2:
                                s.wait("tensor", c_evv, ps_free.get(pi - 2, 0))
                            ps = PQ[pi % 2]
                            for k in range(kc_g):
                                fn = lambda e, ps=ps, pr=pr, k=k, gpr=gpr: e.matmul(
                                    ps[:, :], PAN[pr][:, k * 128:(k + 1) * 128],
                                    GRP[gpr][:, k, :],
                                    start=(k == 0), stop=(k == kc_g - 1))
                                if k == kc_g - 1:
                                    s.emit("tensor", fn, inc=c_mm[0])
                                else:
                                    s.raw("tensor", fn)
                            pan_mm[pi] = c_mm[0].count
                            if side_fn is not None:
                                side_fn(pi)
                            s.wait("vector", c_mm[0], pan_mm[pi])
                            if g == 0:
                                if q > 0:
                                    s.wait("vector", d_acc)
                                s.emit("vector", lambda e, f=f, ps=ps: e.tensor_copy(
                                    out=ACC[:, f, :], in_=ps[:, :]), inc=c_evv)
                            else:
                                s.emit("vector", lambda e, f=f, ps=ps: e.tensor_add(
                                    out=ACC[:, f, :], in0=ACC[:, f, :], in1=ps[:, :]), inc=c_evv)
                            ps_free[pi] = c_evv.count
                            pi += 1
                        grp_mm[(q, g)] = c_mm[0].count
                    evac_q_fn(q)

        def bcast_const_row(scope, src_row_ap, out_tile, scale=1.0, extra=None):
            with ExitStack() as ps_sc:
                PSB = mkpsum(ps_sc, [128, 512])
                s.wait("tensor", c_evv)
                s.wait("tensor", c_eva)
                s.emit("tensor", lambda e: e.matmul(
                    PSB[:, 0:src_row_ap.shape[-1]], onesrow_f[:], src_row_ap,
                    start=True, stop=True), inc=c_pe)
                s.wait("scalar", c_pe)
                s.emit("scalar", lambda e: e.activation(
                    out=out_tile, in_=PSB[:, 0:src_row_ap.shape[-1]],
                    func=AF.Identity, scale=scale), inc=c_eva)

        side_jobs = ([("c", wout_t[g, f], woutbf[g, f], 2048)
                      for g in range(2) for f in range(16)]
                     + [("g", w1_t[i], 2048, 0) for i in range(KF)]
                     + [("g", w2_t[i], 2048, 1) for i in range(KF)]
                     + [("g", w3_t[g, f], 1408, 2) for g in range(4) for f in range(16)]
                     + [("c", wm2_t[g, f], wm2bf[g, f], 2048)
                        for g in range(4) for f in range(16)])
        N_WOUT_JOBS = 32
        N_GAMMA_END = 32 + 152
        side_i = [0]
        side_done = {}

        def side_pump(GSTb, SBTb, n=1):
            for _ in range(n):
                gi = side_i[0]
                if gi >= len(side_jobs):
                    return
                side_i[0] += 1
                job = side_jobs[gi]
                gpr = gi % 2
                if gi >= 2:
                    s.wait("sync", c_g, side_done[gi - 2])
                src = job[1]
                wid = job[2] if job[0] == "g" else job[3]
                s.emit("sync", lambda e, src=src, wid=wid, gpr=gpr: e.dma_start(
                    out=GSTb[gpr][:, 0:wid], in_=src), inc=d_g[gpr], inc_n=16)
                if job[0] == "g":
                    m = job[3]
                    s.wait("vector", d_g[gpr])
                    qw = wid // 4
                    for qq in range(4):
                        s.emit("vector", lambda e, qq=qq, qw=qw, gpr=gpr, GSTb=GSTb: e.tensor_reduce(
                            out=gtmp[:], in_=GSTb[gpr][:, qq * qw:(qq + 1) * qw],
                            axis=AX.X, op=ALU.add, apply_absolute_value=True), inc=c_evv)
                        s.wait("vector", c_evv)
                        fn = lambda e, m=m: e.tensor_add(
                            out=gacc[:, m:m + 1], in0=gacc[:, m:m + 1], in1=gtmp[:])
                        if qq == 3:
                            s.emit("vector", fn, inc=c_g)
                        else:
                            s.emit("vector", fn, inc=c_evv)
                        s.wait("vector", c_evv if qq < 3 else c_g)
                else:
                    dst = job[2]
                    ceng = "scalar" if gpr == 0 else "vector"
                    s.wait(ceng, d_g[gpr])
                    s.wait(ceng, d_side)
                    if ceng == "scalar":
                        s.emit("scalar", lambda e, gpr=gpr, GSTb=GSTb, SBTb=SBTb: e.activation(
                            out=SBTb[gpr][:], in_=GSTb[gpr][:], func=AF.Identity), inc=c_g)
                        s.wait("scalar", c_g)
                        s.emit("scalar", lambda e, dst=dst, gpr=gpr, SBTb=SBTb: e.dma_start(
                            out=dst, in_=SBTb[gpr][:]), inc=d_side, inc_n=16)
                    else:
                        s.emit("vector", lambda e, gpr=gpr, GSTb=GSTb, SBTb=SBTb: e.tensor_copy(
                            out=SBTb[gpr][:], in_=GSTb[gpr][:]), inc=c_g)
                        s.wait("scalar", c_g)
                        s.emit("scalar", lambda e, dst=dst, gpr=gpr, SBTb=SBTb: e.dma_start(
                            out=dst, in_=SBTb[gpr][:]), inc=d_side, inc_n=16)
                side_done[gi] = c_g.count

        xt_r = xt.rearrange("(j p) t -> p j t", p=128)
        with ExitStack() as scA:
            Bsig0 = mktile(scA, [128, T], BF16)
            Brho2 = mktile(scA, [128, T], BF16)

            with ExitStack() as sc:
                Bmu, Brho, Bsig, scr = load_res_stats(
                    sc, [xt_r[:, j, :] for j in range(KD)], "A")
                s.wait("vector", c_eva)
                s.emit("vector", lambda e: e.tensor_copy(out=Bsig0[:], in_=Bsig[:]), inc=c_evv)
                center_res(Bmu, Brho, scr)

            s.fence()
            with ExitStack() as sc:
                XPS = [mktile(sc, [128, 3 + T], F32) for _ in range(2)]
                XC = mktile(sc, [128, T], F32)
                XCS = mktile(sc, [128, T], BF16)
                ZSIL = mktile(sc, [128, T], BF16)
                GST = [mktile(sc, [128, 2048], F32) for _ in range(2)]
                SBT = [mktile(sc, [128, 2048], BF16) for _ in range(2)]
                PSA = mkpsum(sc, [128, 2048])
                PSB = mkpsum(sc, [128, 2048])
                s.wait("gpsimd", c_pool)
                s.emit("gpsimd", lambda e: e.memset(XPS[0][:, 0:3], 0.0), inc=c_pool)
                s.emit("gpsimd", lambda e: e.memset(XPS[1][:, 0:3], 0.0), inc=c_pool)

                def pool_extra(i):
                    side_pump(GST, SBT, 2 if i % 2 == 0 else 1)

                state = {"xc_sil": 0, "ytt": 0, "xpcopy": {}, "zsil": 0}

                def evac_a1(i, ps, pr):
                    pair = i // 2
                    j = pair
                    if pr == 0:
                        s.wait("scalar", c_mm[0])
                        if pair >= 2:
                            s.wait("scalar", c_evv, state["xpcopy"].get(pair - 2, (0, 0))[1])
                        s.emit("scalar", lambda e, pair=pair: e.activation(
                            out=XPS[pair % 2][:, 3:3 + T], in_=ps[:, :],
                            func=AF.Identity), inc=c_eva)
                        free = (c_eva, c_eva.count)
                        s.wait("vector", c_eva)
                        if pair >= 1:
                            s.wait("vector", c_eva, state["xc_sil"])
                        s.raw("vector", lambda e, pair=pair, j=j: e.tensor_scalar(
                            out=XC[:], in0=XPS[pair % 2][:, 0:T],
                            scalar1=convw[:, j, 0:1], scalar2=0.0,
                            op0=ALU.mult, op1=ALU.add))
                        for k in range(1, 3):
                            s.raw("vector", lambda e, pair=pair, j=j, k=k: e.scalar_tensor_tensor(
                                out=XC[:], in0=XPS[pair % 2][:, k:k + T],
                                scalar=convw[:, j, k:k + 1], in1=XC[:],
                                op0=ALU.mult, op1=ALU.add))
                        s.emit("vector", lambda e, pair=pair, j=j: e.scalar_tensor_tensor(
                            out=XC[:], in0=XPS[pair % 2][:, 3:3 + T],
                            scalar=convw[:, j, 3:4], in1=XC[:],
                            op0=ALU.mult, op1=ALU.add), inc=c_evv)
                        state["xpcopy"][pair] = (c_evv, c_evv.count)
                        s.wait("scalar", c_evv)
                        if pair >= 1:
                            s.wait("scalar", c_evv, state["ytt"])
                        s.emit("scalar", lambda e, j=j: e.activation(
                            out=XCS[:], in_=XC[:], func=AF.Silu,
                            bias=convb[:, j:j + 1]), inc=c_eva)
                        state["xc_sil"] = c_eva.count
                        return free
                    else:
                        s.wait("scalar", c_mm[1])
                        if pair >= 1:
                            s.wait("scalar", c_evv, state["ytt"])
                        s.emit("scalar", lambda e: e.activation(
                            out=ZSIL[:], in_=ps[:, :], func=AF.Silu), inc=c_eva)
                        free = (c_eva, c_eva.count)
                        s.wait("vector", c_eva)
                        s.wait("vector", d_ev)
                        s.emit("vector", lambda e, pair=pair: e.tensor_mul(
                            out=EVB[pair % 2][:], in0=XCS[:], in1=ZSIL[:]), inc=c_evv)
                        state["ytt"] = c_evv.count
                        s.wait("scalar", c_evv, state["ytt"])
                        s.emit("scalar", lambda e, j=j, pair=pair: e.dma_start(
                            out=ysp[j], in_=EVB[pair % 2][:]), inc=d_ev, inc_n=16)
                        return free

                order = []
                for j in range(32):
                    order.append(win_t[j])
                    order.append(win_t[j + 32])
                stream_panels(order, 16, PSA, PSB, evac_a1, pool_extra=pool_extra)

            s.fence()
            with ExitStack() as sc:
                GRP = [mktile(sc, [128, 16, 512], BF16) for _ in range(2)]
                ACC = mktile(sc, [128, 16, 512], F32)
                T1 = mktile(sc, [128, 512], BF16)
                SQ2 = T1
                SR = mktile(sc, [1, T], F32)
                rho2_row_done = [0]

                ps_q2 = mkpsum(sc, [1, T])

                def grp_src(q, g):
                    return ysp[g * 16:(g + 1) * 16, :, q * 512:(q + 1) * 512].rearrange(
                        "c p t -> p c t")

                def evac_a2(q):
                    s.wait("vector", c_evv)
                    for j in range(KD):
                        s.wait("vector", c_pe)
                        s.emit("vector", lambda e, j=j, q=q, T1=T1: e.tensor_mul(
                            out=T1[:], in0=RES[:, j, q * 512:(q + 1) * 512],
                            in1=Bsig0[:, q * 512:(q + 1) * 512]), inc=c_evv)
                        s.emit("vector", lambda e, j=j, q=q, T1=T1, ACC=ACC: e.tensor_add(
                            out=RES[:, j, q * 512:(q + 1) * 512], in0=T1[:],
                            in1=ACC[:, j, :]), inc=c_evv)
                        x1_done = c_evv.count
                        s.wait("scalar", c_evv, x1_done)
                        if j >= 1 or q >= 1:
                            s.wait("scalar", c_pe)
                        s.emit("scalar", lambda e, j=j, q=q, SQ2=SQ2: e.activation(
                            out=SQ2[:], in_=RES[:, j, q * 512:(q + 1) * 512],
                            func=AF.Square), inc=c_eva)
                        s.wait("tensor", c_eva)
                        fn = lambda e, j=j, q=q, SQ2=SQ2, ps_q2=ps_q2: e.matmul(
                            ps_q2[:, q * 512:(q + 1) * 512], onescol[:], SQ2[:],
                            start=(j == 0), stop=(j == KD - 1))
                        s.emit("tensor", fn, inc=c_pe)

                GST2 = [mktile(sc, [128, 2048], F32) for _ in range(2)]
                _sbt2 = mktile(sc, [128, 2048], BF16)
                SBT2 = [_sbt2, _sbt2]

                def side_a2(pi):
                    if pi % 8 < 3:
                        side_pump(GST2, SBT2, 1)

                panels = [[[woutbf[g, f] for f in range(16)] for g in range(2)]
                          for q in range(NQ)]
                acc_stage(panels, 16, 2, grp_src, ACC, GRP, evac_a2,
                          bf16_pan=True, side_fn=side_a2)
                s.wait("vector", c_pe)
                s.emit("vector", lambda e, SR=SR, ps_q2=ps_q2: e.tensor_copy(out=SR[:], in_=ps_q2[:, :]), inc=c_evv)
                with ExitStack() as ps_sc:
                    PSB2 = mkpsum(ps_sc, [128, 2048])
                    s.wait("tensor", c_evv)
                    for q in range(NQ):
                        fn = lambda e, q=q, PSB2=PSB2, SR=SR: e.matmul(
                            PSB2[:, q * 512:(q + 1) * 512], onesrow_f[:],
                            SR[:, q * 512:(q + 1) * 512], start=True, stop=True)
                        s.emit("tensor", fn, inc=c_pe) if q == NQ - 1 else s.raw("tensor", fn)
                    s.wait("scalar", c_pe)
                    s.emit("scalar", lambda e, PSB2=PSB2: e.activation(
                        out=Brho2[:], in_=PSB2[:, :], func=AF.Ln), inc=c_eva)
                    s.emit("scalar", lambda e: e.activation(
                        out=Brho2[:], in_=Brho2[:], func=AF.Exp, scale=-0.5), inc=c_eva)
                    s.emit("scalar", lambda e: e.activation(
                        out=Brho2[:], in_=Brho2[:], func=AF.Identity,
                        scale=float(np.sqrt(D))), inc=c_eva)

            s.fence()

            with ExitStack() as sc:
                PSA = mkpsum(sc, [128, 2048])
                PSB = mkpsum(sc, [128, 2048])

                GST3 = [mktile(sc, [128, 2048], F32) for _ in range(2)]
                SBT3 = [mktile(sc, [128, 2048], BF16) for _ in range(2)]

                def pool_extra_a3(i):
                    side_pump(GST3, SBT3, 2)

                def evac_a3(i, ps, pr):
                    s.wait("scalar", c_mm[pr])
                    s.wait("scalar", d_ev)
                    s.emit("scalar", lambda e, i=i, ps=ps: e.activation(
                        out=EVB[i % 2][:], in_=ps[:, :], func=AF.Relu), inc=c_eva)
                    free = (c_eva, c_eva.count)
                    s.wait("vector", c_eva)
                    s.emit("vector", lambda e, i=i: e.tensor_mul(
                        out=EVB[i % 2][:], in0=EVB[i % 2][:], in1=Brho2[:]), inc=c_evv)
                    s.wait("scalar", c_evv)
                    s.emit("scalar", lambda e, i=i: e.dma_start(
                        out=usp[i], in_=EVB[i % 2][:]), inc=d_ev, inc_n=16)
                    return free

                stream_panels([wm1_t[i] for i in range(64)], 16, PSA, PSB, evac_a3,
                              pool_extra=pool_extra_a3)

            s.fence()
            with ExitStack() as sc:
                GRP = [mktile(sc, [128, 16, 512], BF16) for _ in range(2)]
                ACC = mktile(sc, [128, 16, 512], F32)
                XB = mktile(sc, [128, 4, 512], F32)

                def grp_src(q, g):
                    return usp[g * 16:(g + 1) * 16, :, q * 512:(q + 1) * 512].rearrange(
                        "c p t -> p c t")

                def evac_a4(q):
                    for j in range(KD):
                        xslot = j % 4
                        s.wait("sync", c_evv)
                        s.emit("sync", lambda e, j=j, q=q, xslot=xslot, XB=XB: e.dma_start(
                            out=XB[:, xslot, :], in_=xt_r[:, j, q * 512:(q + 1) * 512]),
                            inc=d_xb, inc_n=16)
                        s.wait("vector", d_xb)
                        s.wait("vector", c_evv)
                        s.emit("vector", lambda e, j=j, xslot=xslot, ACC=ACC, XB=XB: e.tensor_add(
                            out=ACC[:, j, :], in0=ACC[:, j, :], in1=XB[:, xslot, :]),
                            inc=c_evv)
                        s.emit("vector", lambda e, j=j, q=q, ACC=ACC: e.tensor_add(
                            out=ACC[:, j, :], in0=ACC[:, j, :],
                            in1=RES[:, j, q * 512:(q + 1) * 512]), inc=c_evv)
                        add_done = c_evv.count
                        s.wait("scalar", c_evv, add_done)
                        s.emit("scalar", lambda e, j=j, q=q, ACC=ACC: e.dma_start(
                            out=Rm[j][:, q * 512:(q + 1) * 512], in_=ACC[:, j, :]),
                            inc=d_acc, inc_n=16)

                assert side_i[0] >= 0
                panels = [[[wm2bf[g, f] for f in range(16)] for g in range(4)]
                          for q in range(NQ)]
                acc_stage(panels, 16, 4, grp_src, ACC, GRP, evac_a4, bf16_pan=True)

        s.fence()
        if stop_after == "A":
            emit_tail(nc, s, Rm, outT, STG)
            s.build()
            return nc

        with ExitStack() as sc:
            SG = mktile(sc, [1, 512], F32)
            with ExitStack() as ps_sc:
                PSG = mkpsum(ps_sc, [1, 512])
                onescol_f = mktile(sc, [128, 1], F32)
                s.emit("sync", lambda e: e.dma_start(
                    out=onescol_f[:], in_=onescolf_c.ap()), inc=d_misc, inc_n=16)
                s.wait("tensor", d_misc)
                s.wait("tensor", c_g)
                s.emit("tensor", lambda e: e.matmul(
                    PSG[:, 0:4], onescol_f[:], gacc[:], start=True, stop=True), inc=c_pe)
                s.wait("scalar", c_pe)
                s.emit("scalar", lambda e: e.activation(
                    out=SG[:, 0:4], in_=PSG[:, 0:4], func=AF.Identity,
                    scale=0.5 / FFN_CNT, bias=epsc[0:1, 1:2]), inc=c_eva)
            with ExitStack() as ps_sc:
                PSG2 = mkpsum(ps_sc, [128, 512])
                s.wait("tensor", c_eva)
                s.emit("tensor", lambda e: e.matmul(
                    PSG2[:, 0:4], onesrow_f[:], SG[:, 0:4], start=True, stop=True), inc=c_pe)
                s.wait("scalar", c_pe)
                s.emit("scalar", lambda e: e.activation(
                    out=thrv[:], in_=PSG2[:, 0:4], func=AF.Identity), inc=c_eva)
                s.emit("scalar", lambda e: e.activation(
                    out=nthrv[:], in_=PSG2[:, 0:4], func=AF.Identity, scale=-1.0), inc=c_eva)
        s.fence()

        def emit_ffn(app, final_to_out):
            with ExitStack() as scF:
                with ExitStack() as sc:
                    Bmu, Bs, scr = load_res_stats(
                        sc, [Rm[j] for j in range(KD)], "LN")
                    center_res(Bmu, Bs, scr)
                    if hsp is not None and app == 0:
                        s.wait("sync", c_evv)
                        for j in range(KD):
                            s.emit("sync", lambda e, j=j: e.dma_start(
                                out=hsp[j], in_=RES[:, j, :]), inc=d_stv, inc_n=16)
                s.fence()

                with ExitStack() as sc:
                    PSA = mkpsum(sc, [128, 2048])
                    PSB = mkpsum(sc, [128, 2048])
                    if app == 0:
                        QA = mktile(sc, [128, 2048], F32)
                        QB = mktile(sc, [128, 2048], F32)
                        WQS = [mktile(sc, [128, 1408], F32) for _ in range(2)]
                        WQB = mktile(sc, [128, 1408], BF16)
                        w3cur = [0]
                        w3q_done = []

                        def pool_extra_w3(i):
                            gi = w3cur[0]
                            if gi >= 64:
                                return
                            w3cur[0] += 1
                            gpr = gi % 2
                            g, f = gi // 16, gi % 16
                            if gi >= 2:
                                s.wait("sync", c_g, w3q_done[gi - 2])
                            s.emit("sync", lambda e, g=g, f=f, gpr=gpr: e.dma_start(
                                out=WQS[gpr][:], in_=w3_t[g, f]), inc=d_g[gpr], inc_n=16)
                            s.wait("vector", d_g[gpr])
                            s.wait("vector", d_stv)
                            s.raw("vector", lambda e, gpr=gpr: e.tensor_scalar(
                                out=QA[:, 0:1408], in0=WQS[gpr][:],
                                scalar1=thrv[:, 2:3], scalar2=1.0,
                                op0=ALU.is_gt, op1=ALU.mult))
                            s.raw("vector", lambda e, gpr=gpr: e.tensor_scalar(
                                out=QB[:, 0:1408], in0=WQS[gpr][:],
                                scalar1=nthrv[:, 2:3], scalar2=1.0,
                                op0=ALU.is_ge, op1=ALU.mult))
                            s.emit("vector", lambda e: e.scalar_tensor_tensor(
                                out=WQB[:], in0=QA[:, 0:1408], scalar=-1.0,
                                in1=QB[:, 0:1408], op0=ALU.add, op1=ALU.add), inc=c_g)
                            w3q_done.append(c_g.count)
                            s.wait("sync", c_g, w3q_done[-1])
                            s.emit("sync", lambda e, g=g, f=f: e.dma_start(
                                out=wq3sp[g, f], in_=WQB[:]), inc=d_stv, inc_n=16)
                    else:
                        QA = QB = None
                        pool_extra_w3 = None

                    def evac_f1a(i, ps, pr):
                        s.wait("scalar", c_mm[pr])
                        s.wait("scalar", d_sta)
                        s.wait("scalar", d_ev)
                        s.emit("scalar", lambda e, i=i, ps=ps: e.activation(
                            out=EVB[i % 2][:], in_=ps[:, :], func=AF.Sigmoid), inc=c_eva)
                        free = (c_eva, c_eva.count)
                        s.wait("scalar", c_eva)
                        s.emit("scalar", lambda e, i=i: e.dma_start(
                            out=sgsp[i], in_=EVB[i % 2][:]), inc=d_sta, inc_n=16)
                        return free

                    if app == 0:
                        stream_panels([w1_t[i] for i in range(KF)], 16, PSA, PSB,
                                      evac_f1a, quant=(0, QA, QB),
                                      pool_extra=pool_extra_w3,
                                      spills=[wq1sp[i] for i in range(KF)])
                    else:
                        stream_panels([wq1sp[i] for i in range(KF)], 16, PSA, PSB,
                                      evac_f1a, bf16_src=True)

                    SGB = [mktile(sc, [128, 2048], BF16) for _ in range(2)]
                    sg_ld = {}

                    def evac_f1b(i, ps, pr):
                        s.wait("sync", c_evv)
                        s.emit("sync", lambda e, i=i: e.dma_start(
                            out=SGB[i % 2][:], in_=sgsp[i]), inc=d_grp[i % 2], inc_n=16)
                        s.wait("vector", d_grp[i % 2])
                        s.wait("vector", c_mm[pr])
                        s.wait("vector", d_ev)
                        s.wait("vector", d_sta)
                        s.emit("vector", lambda e, i=i, ps=ps: e.tensor_mul(
                            out=EVB[i % 2][:], in0=SGB[i % 2][:], in1=ps[:, :]), inc=c_evv)
                        free = (c_evv, c_evv.count)
                        s.wait("scalar", c_evv)
                        s.emit("scalar", lambda e, i=i: e.dma_start(
                            out=hidsp[i], in_=EVB[i % 2][:]), inc=d_ev, inc_n=16)
                        return free

                    if app == 0:
                        stream_panels([w2_t[i] for i in range(KF)], 16, PSA, PSB,
                                      evac_f1b, quant=(1, QA, QB),
                                      pool_extra=pool_extra_w3,
                                      spills=[wq2sp[i] for i in range(KF)])
                    else:
                        stream_panels([wq2sp[i] for i in range(KF)], 16, PSA, PSB,
                                      evac_f1b, bf16_src=True)
                s.fence()

                with ExitStack() as sc:
                    GRP = [mktile(sc, [128, 11, 512], BF16) for _ in range(2)]
                    ACC = mktile(sc, [128, 16, 512], F32)
                    RB = mktile(sc, [128, 4, 512], F32)

                    def grp_src(q, g):
                        return hidsp[g * 11:(g + 1) * 11, :, q * 512:(q + 1) * 512].rearrange(
                            "c p t -> p c t")

                    def evac_f2(q):
                        for j in range(KD):
                            xslot = j % 4
                            s.wait("sync", c_evv)
                            s.emit("sync", lambda e, j=j, q=q, xslot=xslot: e.dma_start(
                                out=RB[:, xslot, :], in_=Rm[j][:, q * 512:(q + 1) * 512]),
                                inc=d_xb, inc_n=16)
                            s.wait("vector", d_xb)
                            s.wait("vector", c_evv)
                            s.emit("vector", lambda e, j=j, xslot=xslot: e.tensor_add(
                                out=ACC[:, j, :], in0=ACC[:, j, :], in1=RB[:, xslot, :]),
                                inc=c_evv)
                            add_done = c_evv.count
                            s.wait("scalar", c_evv, add_done)
                            dst = outT if final_to_out else Rm
                            s.emit("scalar", lambda e, j=j, q=q, dst=dst: e.dma_start(
                                out=dst[j][:, q * 512:(q + 1) * 512], in_=ACC[:, j, :]),
                                inc=d_acc, inc_n=16)

                    panels = [[[wq3sp[g, f] for f in range(16)] for g in range(4)]
                              for q in range(NQ)]
                    acc_stage(panels, 11, 4, grp_src, ACC, GRP, evac_f2, bf16_pan=True)
            s.fence()

        emit_ffn(0, False)
        if stop_after == "F1":
            emit_tail(nc, s, Rm, outT, STG)
            s.build()
            return nc

        with ExitStack() as scB:
            ckv = mktile(scB, [128, 4, T], BF16)
            cq = mktile(scB, [128, 8, T], BF16)

            with ExitStack() as sc:
                Bmu, Bs, scr = load_res_stats(sc, [Rm[j] for j in range(KD)], "LN")
                center_res(Bmu, Bs, scr)
            s.fence()

            with ExitStack() as sc:
                PSA = mkpsum(sc, [128, 2048])
                PSB = mkpsum(sc, [128, 2048])

                def evac_ckv(i, ps, pr):
                    s.wait("scalar", c_mm[pr])
                    s.emit("scalar", lambda e, i=i, ps=ps: e.activation(
                        out=ckv[:, i, :], in_=ps[:, :], func=AF.Identity,
                        bias=bdkv[:, i:i + 1]), inc=c_eva)
                    return (c_eva, c_eva.count)

                def evac_cq(i, ps, pr):
                    s.wait("scalar", c_mm[pr])
                    s.emit("scalar", lambda e, i=i, ps=ps: e.activation(
                        out=cq[:, i, :], in_=ps[:, :], func=AF.Identity,
                        bias=bdq[:, i:i + 1]), inc=c_eva)
                    return (c_eva, c_eva.count)

                stream_panels([wdkv_t[i] for i in range(4)], 16, PSA, PSB, evac_ckv)
                s.wait("tensor", c_eva)
                stream_panels([wdq_t[i] for i in range(8)], 16, PSA, PSB, evac_cq)
            s.fence()

            with ExitStack() as sc:
                kCb = mktile(sc, [128, T], BF16)
                kRb = mktile(sc, [128, T], BF16)
                qCb = mktile(sc, [128, T], BF16)
                krp = qCb
                qRb = mktile(sc, [128, T], BF16)
                Vtmb = mktile(sc, [128, 16, 132], BF16)
                ptile = [mktile(sc, [128, 512], BF16) for _ in range(2)]
                pTs = [mktile(sc, [128, 512], BF16) for _ in range(2)]
                oacc = [mktile(sc, [128, 132], F32) for _ in range(2)]
                otm = mktile(sc, [128, 128], BF16)
                ropeC = mktile(sc, [128, T], BF16)
                ropeS = mktile(sc, [128, T], BF16)
                tril = mktile(sc, [128, 128], F32)
                rows = mktile(sc, [128, 16], F32)
                HP = [mktile(sc, [128, 4096], BF16) for _ in range(2)]

                dbgt = mktile(sc, [128, 512], F32) if scdbg is not None else None
                PBLD = mkpsum(sc, [128, 2048])
                PS0 = mkpsum(sc, [128, 512])
                PS1 = mkpsum(sc, [128, 512])
                PO = mkpsum(sc, [128, 512])
                PT = mkpsum(sc, [128, 1024], BF16)

                s.emit("sync", lambda e: e.dma_start(out=ropeC[:], in_=ropeC_c.ap()),
                       inc=d_misc, inc_n=16)
                s.emit("sync", lambda e: e.dma_start(out=ropeS[:], in_=ropeS_c.ap()),
                       inc=d_misc, inc_n=16)
                s.emit("sync", lambda e: e.dma_start(out=tril[:], in_=tril_c.ap()),
                       inc=d_misc, inc_n=16)
                s.wait("vector", d_misc)
                s.wait("tensor", d_misc)

                hp_cast = [0, 0]
                build_done = {}

                def load_head(h, prev_attend_pe):
                    pr = h % 2
                    if h >= 2:
                        s.wait("sync", c_pe, build_done[h - 2])
                    if h >= 1:
                        s.wait("sync", c_pool, hp_cast[1 - pr])
                    for (dst0, src) in ((0, wuk_t[h]), (512, wuv_t[h]), (1024, wuq_t[h])):
                        wdt = src.shape[-1]
                        s.emit("sync", lambda e, dst0=dst0, src=src, wdt=wdt: e.dma_start(
                            out=STG[0][:, dst0:dst0 + wdt], in_=src), inc=d_stg[0], inc_n=16)
                    s.emit("sync", lambda e, h=h: e.dma_start(
                        out=STG[1][:], in_=wkr_t[h]), inc=d_stg[1], inc_n=16)
                    s.wait("gpsimd", d_stg[0])
                    s.wait("gpsimd", d_stg[1])
                    if h >= 2:
                        s.wait("gpsimd", c_pe, build_done[h - 2])
                    s.raw("gpsimd", lambda e, pr=pr: e.tensor_copy(
                        out=HP[pr][:, 0:2048], in_=STG[0][:]))
                    s.emit("gpsimd", lambda e, pr=pr: e.tensor_copy(
                        out=HP[pr][:, 2048:4096], in_=STG[1][:]), inc=c_pool)
                    hp_cast[pr] = c_pool.count

                def build_head(h, prev_attend_pe, prev_ropeq_evv):
                    pr = h % 2
                    W = HP[pr]
                    s.wait("tensor", c_pool, hp_cast[pr])
                    if prev_ropeq_evv:
                        s.wait("tensor", c_evv, prev_ropeq_evv)
                    if prev_attend_pe:
                        s.wait("scalar", c_pe, prev_attend_pe)
                        s.wait("vector", c_pe, prev_attend_pe)
                    ev = c_eva.count
                    for t in range(NQ):
                        for c in range(4):
                            fn = lambda e, t=t, c=c, W=W: e.matmul(
                                PBLD[:, t * 512:(t + 1) * 512],
                                W[:, c * 128:(c + 1) * 128],
                                ckv[:, c, t * 512:(t + 1) * 512],
                                start=(c == 0), stop=(c == 3))
                            if c == 3:
                                s.emit("tensor", fn, inc=c_pe)
                            else:
                                s.raw("tensor", fn)
                        s.wait("scalar", c_pe)
                        s.emit("scalar", lambda e, t=t, h=h: e.activation(
                            out=kCb[:, t * 512:(t + 1) * 512],
                            in_=PBLD[:, t * 512:(t + 1) * 512], func=AF.Identity,
                            bias=buk[:, h:h + 1]), inc=c_eva)
                    for t in range(NQ):
                        s.wait("tensor", c_eva, ev + 1 + t)
                        for c in range(KD):
                            fn = lambda e, t=t, c=c, W=W: e.matmul(
                                PBLD[:, t * 512:(t + 1) * 512],
                                W[:, 2048 + c * 128:2048 + (c + 1) * 128],
                                RES[:, c, t * 512:(t + 1) * 512],
                                start=(c == 0), stop=(c == KD - 1))
                            if c == KD - 1:
                                s.emit("tensor", fn, inc=c_pe)
                            else:
                                s.raw("tensor", fn)
                        s.wait("scalar", c_pe)
                        s.emit("scalar", lambda e, t=t, h=h: e.activation(
                            out=krp[:, t * 512:(t + 1) * 512],
                            in_=PBLD[:, t * 512:(t + 1) * 512], func=AF.Identity,
                            bias=bkr[:, h:h + 1]), inc=c_eva)
                    ev2 = c_eva.count
                    for t in range(NQ):
                        s.wait("tensor", c_eva, ev2 - NQ + 1 + t)
                        fn = lambda e, t=t: e.matmul(
                            PBLD[:, t * 512:(t + 1) * 512], swapP_sb[:],
                            krp[:, t * 512:(t + 1) * 512], start=True, stop=True)
                        s.emit("tensor", fn, inc=c_pe)
                    swap_done = c_pe.count
                    s.wait("vector", c_eva, ev2)
                    s.wait("vector", c_pe, swap_done)
                    s.emit("vector", lambda e: e.tensor_mul(
                        out=kRb[:], in0=krp[:], in1=ropeC[:]), inc=c_evv)
                    s.emit("vector", lambda e: e.tensor_mul(
                        out=qRb[:], in0=PBLD[:, :], in1=ropeS[:]), inc=c_evv)
                    s.emit("vector", lambda e: e.tensor_add(
                        out=kRb[:], in0=kRb[:], in1=qRb[:]), inc=c_evv)
                    rope_k_done = c_evv.count
                    s.wait("tensor", c_evv, rope_k_done)
                    for kt in range(16):
                        for c in range(4):
                            s.raw("tensor", lambda e, kt=kt, c=c, W=W: e.matmul(
                                PBLD[:, kt * 128:(kt + 1) * 128],
                                ckv[:, c, kt * 128:(kt + 1) * 128],
                                W[:, 512 + c * 128:512 + (c + 1) * 128],
                                start=(c == 0), stop=False))
                        fn = lambda e, kt=kt, h=h: e.matmul(
                            PBLD[:, kt * 128:(kt + 1) * 128], onesrow_f[:],
                            buv_row[0:1, h * 128:(h + 1) * 128], start=False, stop=True)
                        if kt == 15:
                            s.emit("tensor", fn, inc=c_pe)
                        else:
                            s.raw("tensor", fn)
                    s.wait("scalar", c_pe)
                    s.emit("scalar", lambda e: e.activation(
                        out=Vtmb[:, :, 0:128], in_=PBLD[:, :].rearrange(
                            "p (a b) -> p a b", b=128), func=AF.Identity), inc=c_eva)
                    v_done = c_eva.count
                    s.wait("tensor", c_eva, v_done)
                    s.wait("scalar", c_evv, rope_k_done)
                    evq0 = c_eva.count
                    for t in range(NQ):
                        for c in range(8):
                            fn = lambda e, t=t, c=c, W=W: e.matmul(
                                PBLD[:, t * 512:(t + 1) * 512],
                                W[:, 1024 + c * 128:1024 + (c + 1) * 128],
                                cq[:, c, t * 512:(t + 1) * 512],
                                start=(c == 0), stop=(c == 7))
                            if c == 7:
                                s.emit("tensor", fn, inc=c_pe)
                            else:
                                s.raw("tensor", fn)
                        s.wait("scalar", c_pe)
                        s.emit("scalar", lambda e, t=t, h=h: e.activation(
                            out=qCb[:, t * 512:(t + 1) * 512],
                            in_=PBLD[:, t * 512:(t + 1) * 512], func=AF.Identity,
                            scale=1.0 / 16.0, bias=buq16[:, h:h + 1]), inc=c_eva)
                    evq = c_eva.count
                    for t in range(NQ):
                        s.wait("tensor", c_eva, evq - NQ + 1 + t)
                        fn = lambda e, t=t: e.matmul(
                            PBLD[:, t * 512:(t + 1) * 512], swapP_sb[:],
                            qCb[:, t * 512:(t + 1) * 512], start=True, stop=True)
                        s.emit("tensor", fn, inc=c_pe)
                    swap_q = c_pe.count
                    s.wait("vector", c_eva, evq)
                    s.wait("vector", c_pe, swap_q)
                    for t in range(NQ):
                        sl = slice(t * 512, (t + 1) * 512)
                        s.raw("vector", lambda e, sl=sl: e.tensor_mul(
                            out=qRb[:, sl], in0=qCb[:, sl], in1=ropeC[:, sl]))
                        s.raw("vector", lambda e, sl=sl: e.tensor_mul(
                            out=ptile[0][:], in0=PBLD[:, sl], in1=ropeS[:, sl]))
                        s.emit("vector", lambda e, sl=sl: e.tensor_add(
                            out=qRb[:, sl], in0=qRb[:, sl], in1=ptile[0][:]), inc=c_evv)
                    ropeq_evv = c_evv.count
                    ready = c_evv.count
                    build_done[h] = c_pe.count
                    return ready, build_done[h], ropeq_evv

                dbgslot = [0]

                def attend_head(h, ready_c):
                    pr = h % 2
                    s.wait("tensor", c_evv, ready_c)
                    s.wait("tensor", c_eva)
                    PS = [PS0, PS1]
                    ps_free = [0, 0]
                    pt_user = [0]
                    pt2_user = [0]
                    po_user = [0]

                    def chain_init(c, qi):
                        ro = 8 * c
                        s.wait("vector", c_eva)
                        s.emit("vector", lambda e, ro=ro: e.memset(rows[:, ro:ro + 1], 3.0e38), inc=c_evv)
                        s.emit("vector", lambda e, c=c: e.memset(oacc[c][:], 0.0), inc=c_evv)
                        s.wait("vector", c_evv)
                        s.wait("tensor", c_evv)

                    def emit_scores(st, kb):
                        c, qi = st["c"], st["qi"]
                        nkb = min(512, (qi + 1) * 128 - kb * 512)
                        if ps_free[c]:
                            s.wait("tensor", c_eva, ps_free[c])
                        s.raw("tensor", lambda e, c=c, qi=qi, kb=kb, nkb=nkb: e.matmul(
                            PS[c][:, 0:nkb], qCb[:, qi * 128:(qi + 1) * 128],
                            kCb[:, kb * 512:kb * 512 + nkb], start=True, stop=False))
                        s.emit("tensor", lambda e, c=c, qi=qi, kb=kb, nkb=nkb: e.matmul(
                            PS[c][:, 0:nkb], qRb[:, qi * 128:(qi + 1) * 128],
                            kRb[:, kb * 512:kb * 512 + nkb], start=False, stop=True), inc=c_pe)
                        st["sc"][kb] = c_pe.count

                    def emit_soft_pv(st, kb):
                        c, qi, nblk = st["c"], st["qi"], st["nblk"]
                        ro = 8 * c
                        nkb = min(512, (qi + 1) * 128 - kb * 512)
                        ps = PS[c]
                        nmo = ro + (0 if kb % 2 == 0 else 1)
                        nmn = ro + (1 if kb % 2 == 0 else 0)
                        s.wait("vector", c_pe, st["sc"][kb])
                        if kb == nblk - 1:
                            s.emit("vector", lambda e, ps=ps, nkb=nkb: e.tensor_add(
                                out=ps[:, nkb - 128:nkb], in0=ps[:, nkb - 128:nkb],
                                in1=tril[:]), inc=c_evv)
                        s.emit("vector", lambda e, ps=ps, nkb=nkb, ro=ro: e.reduce_max(
                            out=rows[:, ro + 4:ro + 5], in_=ps[:, 0:nkb], axis=AX.X,
                            negate=True), inc=c_evv)
                        s.wait("vector", c_evv)
                        s.emit("vector", lambda e, nmo=nmo, nmn=nmn, ro=ro: e.tensor_tensor(
                            out=rows[:, nmn:nmn + 1], in0=rows[:, nmo:nmo + 1],
                            in1=rows[:, ro + 4:ro + 5], op=ALU.min), inc=c_evv)
                        s.wait("vector", c_evv)
                        nm_done = c_evv.count
                        s.wait("scalar", c_evv, nm_done)
                        if st["tr"]:
                            s.wait("scalar", c_pe, st["tr"])
                        s.emit("scalar", lambda e, ps=ps, nkb=nkb, nmn=nmn, c=c: e.activation(
                            out=ptile[c][:, 0:nkb], in_=ps[:, 0:nkb], func=AF.Exp,
                            bias=rows[:, nmn:nmn + 1]), inc=c_eva)
                        ps_free[c] = c_eva.count
                        s.emit("scalar", lambda e, nmo=nmo, nmn=nmn, ro=ro: e.activation(
                            out=rows[:, ro + 2:ro + 3], in_=rows[:, nmo:nmo + 1], func=AF.Exp,
                            scale=-1.0, bias=rows[:, nmn:nmn + 1]), inc=c_eva)
                        exp_done = c_eva.count
                        nkt = (nkb + 127) // 128
                        s.wait("tensor", c_eva, exp_done - 1)
                        if pt_user[0]:
                            s.wait("tensor", c_evv, pt_user[0])
                        if pt2_user[0]:
                            s.wait("tensor", c_eva, pt2_user[0])
                        for kt in range(nkt):
                            w = min(128, nkb - kt * 128)
                            fn = lambda e, kt=kt, w=w, c=c: e.transpose(
                                PT[:, kt * 128:kt * 128 + w],
                                ptile[c][:, kt * 128:kt * 128 + w], ident[:])
                            if kt == nkt - 1:
                                s.emit("tensor", fn, inc=c_pe)
                            else:
                                s.raw("tensor", fn)
                        st["tr"] = c_pe.count
                        s.wait("vector", c_pe, st["tr"])
                        if st["pv"]:
                            s.wait("vector", c_pe, st["pv"])
                        s.emit("vector", lambda e, nkb=nkb, c=c: e.tensor_copy(
                            out=pTs[c][:, 0:nkb], in_=PT[:, 0:nkb]), inc=c_evv)
                        pt_user[0] = c_evv.count
                        s.wait("tensor", c_evv, pt_user[0])
                        if po_user[0]:
                            s.wait("tensor", c_evv, po_user[0])
                        for kt in range(nkt):
                            fn = lambda e, kt=kt, kb=kb, c=c: e.matmul(
                                PO[:, 0:129], pTs[c][:, kt * 128:(kt + 1) * 128],
                                Vtmb[:, kb * 4 + kt, 0:129],
                                start=(kt == 0), stop=(kt == nkt - 1))
                            if kt == nkt - 1:
                                s.emit("tensor", fn, inc=c_pe)
                            else:
                                s.raw("tensor", fn)
                        st["pv"] = c_pe.count
                        s.wait("vector", c_pe, st["pv"])
                        s.emit("vector", lambda e, c=c, ro=ro: e.scalar_tensor_tensor(
                            out=oacc[c][:, 0:129], in0=oacc[c][:, 0:129],
                            scalar=rows[:, ro + 2:ro + 3], in1=PO[:, 0:129],
                            op0=ALU.mult, op1=ALU.add), inc=c_evv)
                        s.wait("vector", c_evv)
                        po_user[0] = c_evv.count

                    def finalize(st):
                        c, qi = st["c"], st["qi"]
                        ro = 8 * c
                        s.emit("vector", lambda e, c=c, ro=ro: e.reciprocal(
                            out=rows[:, ro + 3:ro + 4], in_=oacc[c][:, 128:129]), inc=c_evv)
                        s.wait("vector", c_evv)
                        s.emit("vector", lambda e, c=c, ro=ro: e.tensor_scalar(
                            out=otm[:], in0=oacc[c][:, 0:128], scalar1=rows[:, ro + 3:ro + 4],
                            scalar2=1.0, op0=ALU.mult, op1=ALU.mult), inc=c_evv)
                        fin = c_evv.count
                        s.wait("tensor", c_evv, fin)
                        if pt_user[0]:
                            s.wait("tensor", c_evv, pt_user[0])
                        if pt2_user[0]:
                            s.wait("tensor", c_eva, pt2_user[0])
                        s.emit("tensor", lambda e: e.transpose(
                            PT[:, 512:640], otm[:], ident[:]), inc=c_pe)
                        ot_done = c_pe.count
                        s.wait("scalar", c_pe, ot_done)
                        s.wait("scalar", d_ev)
                        s.emit("scalar", lambda e, qi=qi, pr=pr: e.activation(
                            out=EVB[pr][:, qi * 128:(qi + 1) * 128], in_=PT[:, 512:640],
                            func=AF.Identity), inc=c_eva)
                        pt2_user[0] = c_eva.count

                    for qp in range(8):
                        ca = {"c": 0, "qi": 2 * qp, "nblk": (2 * qp) // 4 + 1,
                              "sc": {}, "tr": 0, "pv": 0}
                        cb = {"c": 1, "qi": 2 * qp + 1, "nblk": (2 * qp + 1) // 4 + 1,
                              "sc": {}, "tr": 0, "pv": 0}
                        chain_init(0, ca["qi"])
                        chain_init(1, cb["qi"])
                        nmax = max(ca["nblk"], cb["nblk"])
                        for step in range(nmax + 1):
                            for st in (ca, cb):
                                if 0 <= step - 1 < st["nblk"]:
                                    emit_soft_pv(st, step - 1)
                                if step < st["nblk"]:
                                    emit_scores(st, step)
                        for st in (ca, cb):
                            if st["nblk"] == nmax and nmax - 1 not in range(0, 0):
                                pass
                        finalize(ca)
                        finalize(cb)
                    s.wait("scalar", c_eva)
                    s.emit("scalar", lambda e, h=h, pr=pr: e.dma_start(
                        out=osp[h], in_=EVB[pr][:]), inc=d_ev, inc_n=16)
                    return c_pe.count

                swapP_sb = mktile(sc, [128, 128], BF16)
                s.emit("sync", lambda e: e.dma_start(out=swapP_sb[:], in_=swapP_c.ap()),
                       inc=d_misc, inc_n=16)
                s.wait("tensor", d_misc)

                s.wait("vector", d_misc)
                s.emit("vector", lambda e: e.memset(Vtmb[:, :, 128:129], 1.0), inc=c_evv)
                s.wait("tensor", c_evv)
                att_pe = 0
                load_head(0, 0)
                ready, bpe, ropeq = build_head(0, 0, 0)
                for h in range(1, H):
                    load_head(h, att_pe)
                    att_pe = attend_head(h - 1, ready)
                    ready, bpe, ropeq = build_head(h, att_pe, ropeq)
                att_pe = attend_head(H - 1, ready)
            s.fence()

            with ExitStack() as sc:
                EVF = [mktile(sc, [128, 2048], F32) for _ in range(2)]
                RST = [mktile(sc, [128, 2048], F32) for _ in range(2)]
                PSA = mkpsum(sc, [128, 2048])
                PSB = mkpsum(sc, [128, 2048])
                for j in range(16):
                    s.emit("sync", lambda e, j=j: e.dma_start(
                        out=RES[:, j, :], in_=osp[j]), inc=d_res, inc_n=16)
                s.wait("tensor", d_res)
                s.wait("gpsimd", d_res)

                def evac_b3(i, ps, pr):
                    s.wait("sync", c_evv)
                    s.emit("sync", lambda e, i=i, pr=pr: e.dma_start(
                        out=RST[pr][:], in_=Rm[i]), inc=d_xb, inc_n=16)
                    s.wait("scalar", c_mm[pr])
                    s.wait("scalar", c_evv)
                    s.emit("scalar", lambda e, i=i, ps=ps, pr=pr: e.activation(
                        out=EVF[pr][:], in_=ps[:, :], func=AF.Identity,
                        bias=bo[:, i:i + 1]), inc=c_eva)
                    free = (c_eva, c_eva.count)
                    s.wait("vector", c_eva)
                    s.wait("vector", d_xb)
                    s.wait("vector", d_acc)
                    s.emit("vector", lambda e, pr=pr: e.tensor_add(
                        out=EVF[pr][:], in0=EVF[pr][:], in1=RST[pr][:]), inc=c_evv)
                    s.wait("scalar", c_evv)
                    s.emit("scalar", lambda e, i=i, pr=pr: e.dma_start(
                        out=Rm[i], in_=EVF[pr][:]), inc=d_acc, inc_n=16)
                    return free

                stream_panels([wo_t[i] for i in range(16)], 16, PSA, PSB, evac_b3)
        s.fence()
        if stop_after == "B":
            emit_tail(nc, s, Rm, outT, STG)
            s.build()
            return nc

        emit_ffn(1, True)
        for name, c in list(s.sems.items()):
            if name.startswith("dma_"):
                s.wait("sync", c)
        s.build()
    return nc


def emit_tail(nc, s, Rm, outT, STG):
    d_tail = s.sem("dma_tail")
    for j in range(KD):
        pr = j % 2
        s.emit("sync", lambda e, j=j, pr=pr: e.dma_start(
            out=STG[pr][:], in_=Rm[j]), inc=d_tail, inc_n=16)
        s.wait("sync", d_tail)
        s.emit("sync", lambda e, j=j, pr=pr: e.dma_start(
            out=outT[j], in_=STG[pr][:]), inc=d_tail, inc_n=16)
    for name, c in list(s.sems.items()):
        if name.startswith("dma_"):
            s.wait("sync", c)


_NC_CACHE = {}


def kernel(**inputs):
    import os
    stop_after = os.environ.get("MK_STOP", "full")
    hp = {
        "win_t": tile_lhsT(np.asarray(inputs["L0_attn_a_W_in"])),
        "wout_t": tile_lhsT(np.asarray(inputs["L0_attn_a_W_out"]), groups=2),
        "wm1_t": tile_lhsT(np.asarray(inputs["L0_attn_a_W_mlp1"])),
        "wm2_t": tile_lhsT(np.asarray(inputs["L0_attn_a_W_mlp2"]), groups=4),
        "w1_t": tile_lhsT(np.asarray(inputs["L0_ffn_w1"])),
        "w2_t": tile_lhsT(np.asarray(inputs["L0_ffn_w2"])),
        "w3_t": tile_lhsT(np.asarray(inputs["L0_ffn_w3"]), groups=4),
        "wdkv_t": tile_lhsT(np.asarray(inputs["L0_attn_b_W_DKV_w"])),
        "wuk_t": tile_lhsT(np.asarray(inputs["L0_attn_b_W_UK_w"])),
        "wuv_t": tile_lhsT(np.asarray(inputs["L0_attn_b_W_UV_w"])),
        "wdq_t": tile_lhsT(np.asarray(inputs["L0_attn_b_W_DQ_w"])),
        "wuq_t": tile_lhsT(np.asarray(inputs["L0_attn_b_W_UQ_w"])),
        "wkr_t": tile_lhsT(np.asarray(inputs["L0_attn_b_W_KR_w"])),
        "wo_t": tile_lhsT(np.asarray(inputs["L0_attn_b_W_O_w"])),
        "convw": np.ascontiguousarray(
            np.asarray(inputs["L0_attn_a_conv_w"]).reshape(32, 128, 4).transpose(1, 0, 2)),
        "convb": chunk_vec(inputs["L0_attn_a_conv_b"]),
        "bdkv": chunk_vec(inputs["L0_attn_b_W_DKV_b"]),
        "buk": chunk_vec(inputs["L0_attn_b_W_UK_b"]),
        "buv_row": np.ascontiguousarray(
            np.asarray(inputs["L0_attn_b_W_UV_b"]).reshape(1, 2048)),
        "bdq": chunk_vec(inputs["L0_attn_b_W_DQ_b"]),
        "buq": chunk_vec(inputs["L0_attn_b_W_UQ_b"]),
        "bkr": chunk_vec(inputs["L0_attn_b_W_KR_b"]),
        "bo": chunk_vec(inputs["L0_attn_b_W_O_b"]),
    }
    x = np.asarray(inputs["L0_attn_a_x"])
    nc = build_program(stop_after)
    in_maps = []
    for b in range(8):
        m = dict(hp)
        m["xt"] = np.ascontiguousarray(x[b].T)
        in_maps.append(m)
    res = run_bass_kernel_spmd(nc, in_maps, core_ids=list(range(8)))
    out = np.empty((8, T, D), np.float32)
    for b in range(8):
        ot = np.asarray(res.results[b]["outT"])
        out[b] = ot.reshape(D, T).T
    return out


# version 11
# speedup vs baseline: 1.0894x; 1.0894x over previous
import numpy as np
import ml_dtypes
from contextlib import ExitStack

import concourse.bass as bass
import concourse.mybir as mybir
from concourse.bass_utils import run_bass_kernel_spmd

F32 = mybir.dt.float32
BF16 = mybir.dt.bfloat16
AF = mybir.ActivationFunctionType
ALU = mybir.AluOpType
AX = mybir.AxisListType
BF = ml_dtypes.bfloat16

D = 2048
T = 2048
KD = 16
NQ = 4
H = 16
KF = 44
EPS_LN = 1e-5
NEG_BIG = -3.0e38
FFN_CNT = 5632 * 2048

ENGINES = ("tensor", "vector", "scalar", "gpsimd", "sync")


class Ctr:
    __slots__ = ("h", "count", "name")

    def __init__(self, h, name):
        self.h = h
        self.count = 0
        self.name = name


class Sched:
    def __init__(self, nc, st):
        self.nc = nc
        self.st = st
        self.q = {e: [] for e in ENGINES}
        self.sems = {}

    def sem(self, name):
        if name not in self.sems:
            h = self.st.enter_context(self.nc.semaphore(name))
            self.sems[name] = Ctr(h, name)
        return self.sems[name]

    def wait(self, eng, ctr, val=None):
        v = ctr.count if val is None else val
        if v <= 0:
            return
        self.q[eng].append(lambda e, h=ctr.h, v=v: e.wait_ge(h, v))

    def emit(self, eng, fn, inc=None, inc_n=1):
        if inc is not None:
            inc.count += inc_n

        def run(e, fn=fn, inc=inc, n=inc_n):
            ins = fn(e)
            if inc is not None:
                ins.then_inc(inc.h, n)

        self.q[eng].append(run)

    def raw(self, eng, fn):
        self.q[eng].append(fn)

    def fence(self):
        for name, c in list(self.sems.items()):
            if name.startswith("dma_"):
                self.wait("sync", c)
        for eng in ENGINES:
            f = self.sem("fence_" + eng)
            if eng in ("tensor", "sync"):
                self.emit(eng, lambda e: e.nop(), inc=f)
            else:
                self.emit(eng, lambda e: e.drain(fusable=False), inc=f)
        for eng in ENGINES:
            for other in ENGINES:
                if other != eng:
                    self.wait(eng, self.sem("fence_" + other))

    def build(self):
        block = self.st.enter_context(self.nc.Block())

        def mk(eng):
            def body(e):
                for f in self.q[eng]:
                    f(e)
            return body

        block.tensor(mk("tensor"))
        block.vector(mk("vector"))
        block.scalar(mk("scalar"))
        block.gpsimd(mk("gpsimd"))
        block.sync(mk("sync"))


def tile_lhsT(Wt, groups=1):
    Fd, Cd = Wt.shape
    nf, kc = Fd // 128, Cd // 128
    L = Wt.reshape(nf, 128, kc, 128).transpose(0, 3, 2, 1)
    if groups == 1:
        return np.ascontiguousarray(L.reshape(nf, 128, kc * 128))
    kg = kc // groups
    Lg = L.reshape(nf, 128, groups, kg, 128).transpose(2, 0, 1, 3, 4)
    return np.ascontiguousarray(Lg.reshape(groups, nf, 128, kg * 128))


def chunk_vec(b):
    return np.ascontiguousarray(np.asarray(b).reshape(-1, 128).T)


def rope_tables():
    pos = np.arange(T, dtype=np.float64)
    idx = np.arange(64, dtype=np.float64)
    theta = (10000.0 ** (-2.0 * idx / 128.0))[:, None] * pos[None, :]
    C = np.repeat(np.cos(theta), 2, axis=0).astype(BF)
    S = np.empty((128, T), np.float64)
    S[0::2] = -np.sin(theta)
    S[1::2] = np.sin(theta)
    return C, S.astype(BF)


def build_program(stop_after="full"):
    nc = bass.Bass("TRN2", target_bir_lowering=False)

    def ein(name, shape):
        return nc.dram_tensor(name, shape, F32, kind="ExternalInput").ap()

    xt = ein("xt", [D, T])
    win_t = ein("win_t", [64, 128, 2048])
    wout_t = ein("wout_t", [2, 16, 128, 2048])
    wm1_t = ein("wm1_t", [64, 128, 2048])
    wm2_t = ein("wm2_t", [4, 16, 128, 2048])
    w1_t = ein("w1_t", [KF, 128, 2048])
    w2_t = ein("w2_t", [KF, 128, 2048])
    w3_t = ein("w3_t", [4, 16, 128, 1408])
    wdkv_t = ein("wdkv_t", [4, 128, 2048])
    wuk_t = ein("wuk_t", [16, 128, 512])
    wuv_t = ein("wuv_t", [16, 128, 512])
    wdq_t = ein("wdq_t", [8, 128, 2048])
    wuq_t = ein("wuq_t", [16, 128, 1024])
    wkr_t = ein("wkr_t", [16, 128, 2048])
    wo_t = ein("wo_t", [16, 128, 2048])
    convw_d = ein("convw", [128, 32, 4])
    convb_d = ein("convb", [128, 32])
    bdkv_d = ein("bdkv", [128, 4])
    buk_d = ein("buk", [128, 16])
    buv_row_d = ein("buv_row", [1, 2048])
    bdq_d = ein("bdq", [128, 8])
    buq_d = ein("buq", [128, 16])
    bkr_d = ein("bkr", [128, 16])
    bo_d = ein("bo", [128, 16])

    import os as _os
    _dbg = set(filter(None, _os.environ.get("MK_DBG", "").split(",")))

    def _scratch(name, shape, dt):
        kind = "ExternalOutput" if name in _dbg else "Internal"
        return nc.dram_tensor(name, shape, dt, kind=kind).ap()

    outT = nc.dram_tensor("outT", [KD, 128, T], F32, kind="ExternalOutput").ap()
    Rm = _scratch("Rm", [KD, 128, T], F32)
    ysp = _scratch("ysp", [32, 128, T], BF16)
    usp = _scratch("usp", [64, 128, T], BF16)
    sgsp = _scratch("sgsp", [KF, 128, T], BF16)
    hidsp = _scratch("hidsp", [KF, 128, T], BF16)
    osp = _scratch("osp", [16, 128, T], BF16)
    wq1sp = _scratch("wq1sp", [KF, 128, 2048], BF16)
    wq2sp = _scratch("wq2sp", [KF, 128, 2048], BF16)
    wq3sp = _scratch("wq3sp", [4, 16, 128, 1408], BF16)
    woutbf = _scratch("woutbf", [2, 16, 128, 2048], BF16)
    wm2bf = _scratch("wm2bf", [4, 16, 128, 2048], BF16)
    hsp = _scratch("hsp", [KD, 128, T], BF16) if "hsp" in _dbg else None
    scdbg = _scratch("scdbg", [4, 128, 512], F32) if "scdbg" in _dbg else None

    ident_c = nc.inline_tensor(np.eye(128, dtype=BF), "ident_c")
    onescol_c = nc.inline_tensor(np.ones((128, 1), dtype=BF), "onescol_c")
    onesrow_f_c = nc.inline_tensor(np.ones((1, 128), np.float32), "onesrow_f_c")
    onescolf_c = nc.inline_tensor(np.ones((128, 1), np.float32), "onescolf_c")
    epsc_np = np.zeros((128, 2), np.float32)
    epsc_np[:, 0] = EPS_LN
    epsc_np[:, 1] = 0.5e-8
    epsc_c = nc.inline_tensor(epsc_np, "epsc_c")
    swapP_np = np.zeros((128, 128), dtype=BF)
    for i in range(64):
        swapP_np[2 * i, 2 * i + 1] = 1
        swapP_np[2 * i + 1, 2 * i] = 1
    swapP_c = nc.inline_tensor(swapP_np, "swapP_c")
    tril_np = np.where(np.arange(128)[None, :] <= np.arange(128)[:, None],
                       np.float32(0.0), np.float32(NEG_BIG)).astype(np.float32)
    tril_c = nc.inline_tensor(tril_np, "tril_c")
    ropeC_np, ropeS_np = rope_tables()
    ropeC_c = nc.inline_tensor(ropeC_np, "ropeC_c")
    ropeS_c = nc.inline_tensor(ropeS_np, "ropeS_c")

    st = ExitStack()
    with st:
        RES = st.enter_context(nc.sbuf_tensor("RES", [128, KD, T], BF16))
        STG = [st.enter_context(nc.sbuf_tensor(f"STG{i}", [128, 2048], F32)) for i in range(2)]
        PAN = [st.enter_context(nc.sbuf_tensor(f"PAN{i}", [128, 2048], BF16)) for i in range(2)]
        EVB = [st.enter_context(nc.sbuf_tensor(f"EVB{i}", [128, 2048], BF16)) for i in range(2)]
        ident = st.enter_context(nc.sbuf_tensor("ident", [128, 128], BF16))
        onescol = st.enter_context(nc.sbuf_tensor("onescol", [128, 1], BF16))
        onesrow_f = st.enter_context(nc.sbuf_tensor("onesrow_f", [1, 128], F32))
        convw = st.enter_context(nc.sbuf_tensor("convw_sb", [128, 32, 4], F32))
        convb = st.enter_context(nc.sbuf_tensor("convb_sb", [128, 32], F32))
        bdkv = st.enter_context(nc.sbuf_tensor("bdkv_sb", [128, 4], F32))
        buk = st.enter_context(nc.sbuf_tensor("buk_sb", [128, 16], F32))
        buv_row = st.enter_context(nc.sbuf_tensor("buv_row_sb", [1, 2048], F32))
        bdq = st.enter_context(nc.sbuf_tensor("bdq_sb", [128, 8], F32))
        buq16 = st.enter_context(nc.sbuf_tensor("buq16", [128, 16], F32))
        bkr = st.enter_context(nc.sbuf_tensor("bkr_sb", [128, 16], F32))
        bo = st.enter_context(nc.sbuf_tensor("bo_sb", [128, 16], F32))
        gacc = st.enter_context(nc.sbuf_tensor("gacc", [128, 4], F32))
        epsc = st.enter_context(nc.sbuf_tensor("epsc", [128, 2], F32))
        gtmp = st.enter_context(nc.sbuf_tensor("gtmp", [128, 1], F32))
        thrv = st.enter_context(nc.sbuf_tensor("thrv", [128, 4], F32))
        nthrv = st.enter_context(nc.sbuf_tensor("nthrv", [128, 4], F32))

        s = Sched(nc, st)
        d_stg = [s.sem("dma_stg0"), s.sem("dma_stg1")]
        d_grp = [s.sem("dma_grp0"), s.sem("dma_grp1")]
        d_res = s.sem("dma_res")
        d_xb = s.sem("dma_xb")
        d_stv = s.sem("dma_stv")
        d_ev = s.sem("dma_ev")
        d_acc = s.sem("dma_acc")
        d_wq = s.sem("dma_wq")
        d_side = s.sem("dma_side")
        d_sta = s.sem("dma_sta")
        d_misc = s.sem("dma_misc")
        d_g = [s.sem("dma_g0"), s.sem("dma_g1")]
        c_pool = s.sem("c_pool")
        c_pan = [s.sem("c_pan0"), s.sem("c_pan1")]
        c_mm = [s.sem("c_mm0"), s.sem("c_mm1")]
        c_evv = s.sem("c_evv")
        c_eva = s.sem("c_eva")
        c_pe = s.sem("c_pe")
        c_g = s.sem("c_g")

        for dst, src in ((epsc, epsc_c.ap()), (ident, ident_c.ap()), (onescol, onescol_c.ap()),
                         (onesrow_f, onesrow_f_c.ap()), (convw, convw_d),
                         (convb, convb_d), (bdkv, bdkv_d), (buk, buk_d),
                         (buv_row, buv_row_d), (bdq, bdq_d), (bkr, bkr_d),
                         (bo, bo_d), (buq16, buq_d)):
            s.emit("sync", lambda e, dst=dst, src=src: e.dma_start(out=dst[:], in_=src),
                   inc=d_misc, inc_n=16)
        s.wait("vector", d_misc)
        s.emit("vector", lambda e: e.tensor_scalar(
            out=buq16[:], in0=buq16[:], scalar1=1.0 / 16.0, scalar2=None, op0=ALU.mult),
            inc=c_evv)
        s.emit("gpsimd", lambda e: e.memset(gacc[:], 0.0), inc=c_pool)
        s.wait("tensor", d_misc)

        uid = [0]

        def mktile(scope, shape, dt):
            uid[0] += 1
            return scope.enter_context(nc.sbuf_tensor(f"t{uid[0]}", shape, dt))

        def mkpsum(scope, shape, dt=F32):
            uid[0] += 1
            return scope.enter_context(nc.psum_tensor(f"p{uid[0]}", shape, dt))

        def load_res_stats(scope, src_chunks, mode):
            SQ = [mktile(scope, [128, T], BF16) for _ in range(2)]
            SR1 = mktile(scope, [1, T], F32)
            SR2 = mktile(scope, [1, T], F32)
            nbt = 3 if mode == "A" else 2
            BT = [mktile(scope, [128, T], BF16) for _ in range(nbt)]
            BTS = mktile(scope, [128, T], BF16)
            cast_c = []
            sq_c = []
            with ExitStack() as ps_sc:
                PS_S = mkpsum(ps_sc, [1, T])
                PS_Q = mkpsum(ps_sc, [1, T])
                for j in range(KD):
                    pr = j % 2
                    if j >= 2:
                        s.wait("sync", c_pan[pr], cast_c[j - 2])
                    s.emit("sync", lambda e, j=j, pr=pr: e.dma_start(
                        out=STG[pr][:], in_=src_chunks[j]), inc=d_stg[pr], inc_n=16)
                    ceng = "scalar" if pr == 0 else "vector"
                    s.wait(ceng, d_stg[pr])
                    if pr == 0:
                        s.emit("scalar", lambda e, j=j, pr=pr: e.activation(
                            out=RES[:, j, :], in_=STG[pr][:], func=AF.Identity), inc=c_pan[0])
                    else:
                        s.emit("vector", lambda e, j=j, pr=pr: e.tensor_copy(
                            out=RES[:, j, :], in_=STG[pr][:]), inc=c_pan[1])
                    cast_c.append(c_pan[pr].count)
                    s.wait("scalar", c_pan[pr], cast_c[j])
                    if j >= 2:
                        s.wait("scalar", c_pe, sq_c[j - 2])
                    s.emit("scalar", lambda e, j=j, pr=pr: e.activation(
                        out=SQ[pr][:], in_=RES[:, j, :], func=AF.Square), inc=c_eva)
                    sq_done = c_eva.count
                    s.wait("tensor", c_pan[pr], cast_c[j])
                    for q in range(NQ):
                        s.raw("tensor", lambda e, j=j, q=q: e.matmul(
                            PS_S[:, q * 512:(q + 1) * 512], onescol[:],
                            RES[:, j, q * 512:(q + 1) * 512],
                            start=(j == 0), stop=(j == KD - 1)))
                    s.wait("tensor", c_eva, sq_done)
                    for q in range(NQ):
                        fn = lambda e, pr=pr, q=q, j=j: e.matmul(
                            PS_Q[:, q * 512:(q + 1) * 512], onescol[:],
                            SQ[pr][:, q * 512:(q + 1) * 512],
                            start=(j == 0), stop=(j == KD - 1))
                        if q == NQ - 1:
                            s.emit("tensor", fn, inc=c_pe)
                        else:
                            s.raw("tensor", fn)
                    sq_c.append(c_pe.count)
                s.wait("vector", c_pe)
                s.emit("vector", lambda e: e.tensor_copy(out=SR1[:], in_=PS_S[:, :]), inc=c_evv)
                s.emit("vector", lambda e: e.tensor_copy(out=SR2[:], in_=PS_Q[:, :]), inc=c_evv)
                rows_done = c_evv.count
            with ExitStack() as ps_sc:
                PSB = mkpsum(ps_sc, [128, 2048])
                s.wait("tensor", c_evv, rows_done)
                for q in range(NQ):
                    fn = lambda e, q=q: e.matmul(
                        PSB[:, q * 512:(q + 1) * 512], onesrow_f[:],
                        SR1[:, q * 512:(q + 1) * 512], start=True, stop=True)
                    s.emit("tensor", fn, inc=c_pe) if q == NQ - 1 else s.raw("tensor", fn)
                s.wait("scalar", c_pe)
                s.emit("scalar", lambda e: e.activation(
                    out=BT[0][:], in_=PSB[:, :], func=AF.Identity, scale=1.0 / D), inc=c_eva)
                s.wait("tensor", c_eva)
                for q in range(NQ):
                    fn = lambda e, q=q: e.matmul(
                        PSB[:, q * 512:(q + 1) * 512], onesrow_f[:],
                        SR2[:, q * 512:(q + 1) * 512], start=True, stop=True)
                    s.emit("tensor", fn, inc=c_pe) if q == NQ - 1 else s.raw("tensor", fn)
                s.wait("scalar", c_pe)
                s.emit("scalar", lambda e: e.activation(
                    out=BT[1][:], in_=PSB[:, :], func=AF.Identity, scale=1.0 / D), inc=c_eva)
            s.wait("vector", c_eva)
            s.emit("vector", lambda e: e.tensor_mul(out=BTS[:], in0=BT[0][:], in1=BT[0][:]), inc=c_evv)
            s.emit("vector", lambda e: e.tensor_sub(out=BT[1][:], in0=BT[1][:], in1=BTS[:]), inc=c_evv)
            s.wait("scalar", c_evv)
            if mode == "A":
                s.emit("scalar", lambda e: e.activation(
                    out=BT[2][:], in_=BT[1][:], func=AF.Ln, bias=epsc[:, 0:1]), inc=c_eva)
                s.emit("scalar", lambda e: e.activation(
                    out=BT[1][:], in_=BT[1][:], func=AF.Ln), inc=c_eva)
                s.wait("vector", c_eva)
                s.emit("vector", lambda e: e.tensor_sub(out=BT[2][:], in0=BT[1][:], in1=BT[2][:]), inc=c_evv)
                s.wait("scalar", c_evv)
                s.emit("scalar", lambda e: e.activation(
                    out=BT[2][:], in_=BT[2][:], func=AF.Exp, scale=0.5), inc=c_eva)
                s.emit("scalar", lambda e: e.activation(
                    out=BT[1][:], in_=BT[1][:], func=AF.Exp, scale=-0.5), inc=c_eva)
                return BT[0], BT[1], BT[2], BTS
            else:
                s.emit("scalar", lambda e: e.activation(
                    out=BT[1][:], in_=BT[1][:], func=AF.Ln, bias=epsc[:, 0:1]), inc=c_eva)
                s.emit("scalar", lambda e: e.activation(
                    out=BT[1][:], in_=BT[1][:], func=AF.Exp, scale=-0.5), inc=c_eva)
                return BT[0], BT[1], BTS

        def center_res(Bmu, Bscale, scr):
            s.wait("vector", c_eva)
            for j in range(KD):
                s.emit("vector", lambda e, j=j: e.tensor_sub(
                    out=scr[:], in0=RES[:, j, :], in1=Bmu[:]), inc=c_evv)
                s.emit("vector", lambda e, j=j: e.tensor_mul(
                    out=RES[:, j, :], in0=scr[:], in1=Bscale[:]), inc=c_evv)

        def stream_panels(panels, kc, psA, psB, evac_fn, quant=None,
                          pool_extra=None, bf16_src=False, rhs=None, rhs_kc=None,
                          spills=None):
            if rhs is None:
                rhs, rhs_kc = RES, KD
            for qq in ("sync", "gpsimd"):
                s.wait(qq, c_mm[0])
                s.wait(qq, c_mm[1])
                s.wait(qq, c_pan[0])
                s.wait(qq, c_pan[1])
            wid = kc * 128
            n = len(panels)
            cast_c = [None] * n
            mm_c = [None] * n
            free_c = [None] * n
            for i in range(n):
                pr = i % 2
                if bf16_src:
                    if i >= 2:
                        s.wait("sync", c_mm[pr], mm_c[i - 2])
                    s.emit("sync", lambda e, i=i, pr=pr: e.dma_start(
                        out=PAN[pr][:, 0:wid], in_=panels[i]), inc=d_stg[pr], inc_n=16)
                    s.wait("tensor", d_stg[pr])
                else:
                    if i >= 2:
                        s.wait("sync", c_pan[pr], cast_c[i - 2])
                    s.emit("sync", lambda e, i=i, pr=pr: e.dma_start(
                        out=STG[pr][:, 0:wid], in_=panels[i]), inc=d_stg[pr], inc_n=16)
                    ceng = ("scalar" if pr == 0 else "vector") if quant is None else "vector"
                    s.wait(ceng, d_stg[pr])
                    if i >= 2:
                        s.wait(ceng, c_mm[pr], mm_c[i - 2])
                    if spills is not None and i >= 2 and spills[i - 2] is not None:
                        s.wait(ceng, d_wq)
                    if quant is None:
                        if ceng == "scalar":
                            s.emit("scalar", lambda e, pr=pr: e.activation(
                                out=PAN[pr][:, 0:wid], in_=STG[pr][:, 0:wid],
                                func=AF.Identity), inc=c_pan[pr])
                        else:
                            s.emit("vector", lambda e, pr=pr: e.tensor_copy(
                                out=PAN[pr][:, 0:wid], in_=STG[pr][:, 0:wid]), inc=c_pan[pr])
                    else:
                        m, QA, QB = quant
                        s.raw("vector", lambda e, pr=pr: e.tensor_scalar(
                            out=QA[:, 0:wid], in0=STG[pr][:, 0:wid],
                            scalar1=thrv[:, m:m + 1], scalar2=1.0,
                            op0=ALU.is_gt, op1=ALU.mult))
                        s.raw("vector", lambda e, pr=pr: e.tensor_scalar(
                            out=QB[:, 0:wid], in0=STG[pr][:, 0:wid],
                            scalar1=nthrv[:, m:m + 1], scalar2=1.0,
                            op0=ALU.is_ge, op1=ALU.mult))
                        s.emit("vector", lambda e, pr=pr: e.scalar_tensor_tensor(
                            out=PAN[pr][:, 0:wid], in0=QA[:, 0:wid], scalar=-1.0,
                            in1=QB[:, 0:wid], op0=ALU.add, op1=ALU.add), inc=c_pan[pr])
                    cast_c[i] = c_pan[pr].count
                    if spills is not None and spills[i] is not None:
                        s.wait("scalar", c_pan[pr], cast_c[i])
                        s.emit("scalar", lambda e, i=i, pr=pr: e.dma_start(
                            out=spills[i], in_=PAN[pr][:, 0:wid]), inc=d_wq, inc_n=16)
                    s.wait("tensor", c_pan[pr], cast_c[i])
                if pool_extra is not None:
                    pool_extra(i)
                if i >= 2:
                    s.wait("tensor", *free_c[i - 2])
                ps = psA if pr == 0 else psB
                for t in range(NQ):
                    for k in range(kc):
                        fn = lambda e, ps=ps, pr=pr, k=k, t=t: e.matmul(
                            ps[:, t * 512:(t + 1) * 512],
                            PAN[pr][:, k * 128:(k + 1) * 128],
                            rhs[:, k, t * 512:(t + 1) * 512],
                            start=(k == 0), stop=(k == kc - 1))
                        if t == NQ - 1 and k == kc - 1:
                            s.emit("tensor", fn, inc=c_mm[pr])
                        else:
                            s.raw("tensor", fn)
                mm_c[i] = c_mm[pr].count
                free_c[i] = evac_fn(i, ps, pr)

        def acc_stage(qg_panels, kc_g, ngrp, grp_src_fn, ACC, GRP, evac_q_fn,
                      bf16_pan=False, side_fn=None):
            wid = kc_g * 128
            PQ = None
            with ExitStack() as ps_sc:
                PQ = [mkpsum(ps_sc, [128, 512]) for _ in range(2)]
                pi = 0
                grp_mm = {}
                pan_cast = {}
                pan_mm = {}
                ps_free = {}
                for q in range(NQ):
                    for g in range(ngrp):
                        gpr = g % 2
                        prev = (q, g - 2) if g >= 2 else ((q - 1, ngrp - 2 + g) if q >= 1 else None)
                        if prev in grp_mm:
                            s.wait("sync", c_mm[0], grp_mm[prev])
                        s.emit("sync", lambda e, q=q, g=g, gpr=gpr: e.dma_start(
                            out=GRP[gpr][:], in_=grp_src_fn(q, g)), inc=d_grp[gpr], inc_n=16)
                        s.wait("tensor", d_grp[gpr])
                        for f in range(16):
                            pr = pi % 2
                            if bf16_pan:
                                if pi >= 2:
                                    s.wait("sync", c_mm[0], pan_mm.get(pi - 2, 0))
                                s.emit("sync", lambda e, q=q, g=g, f=f, pr=pr: e.dma_start(
                                    out=PAN[pr][:, 0:wid], in_=qg_panels[q][g][f]),
                                    inc=d_stg[pr], inc_n=16)
                                s.wait("tensor", d_stg[pr])
                            else:
                                if pi >= 2:
                                    s.wait("sync", c_pan[pr], pan_cast.get(pi - 2, 0))
                                s.emit("sync", lambda e, q=q, g=g, f=f, pr=pr: e.dma_start(
                                    out=STG[pr][:, 0:wid], in_=qg_panels[q][g][f]),
                                    inc=d_stg[pr], inc_n=16)
                                s.wait("gpsimd", d_stg[pr])
                                if pi >= 2:
                                    s.wait("gpsimd", c_mm[0], pan_mm.get(pi - 2, 0))
                                s.emit("gpsimd", lambda e, pr=pr: e.tensor_copy(
                                    out=PAN[pr][:, 0:wid], in_=STG[pr][:, 0:wid]),
                                    inc=c_pan[pr])
                                pan_cast[pi] = c_pan[pr].count
                                s.wait("tensor", c_pan[pr], pan_cast[pi])
                            if pi >= 2:
                                s.wait("tensor", c_evv, ps_free.get(pi - 2, 0))
                            ps = PQ[pi % 2]
                            for k in range(kc_g):
                                fn = lambda e, ps=ps, pr=pr, k=k, gpr=gpr: e.matmul(
                                    ps[:, :], PAN[pr][:, k * 128:(k + 1) * 128],
                                    GRP[gpr][:, k, :],
                                    start=(k == 0), stop=(k == kc_g - 1))
                                if k == kc_g - 1:
                                    s.emit("tensor", fn, inc=c_mm[0])
                                else:
                                    s.raw("tensor", fn)
                            pan_mm[pi] = c_mm[0].count
                            if side_fn is not None:
                                side_fn(pi)
                            s.wait("vector", c_mm[0], pan_mm[pi])
                            if g == 0:
                                if q > 0:
                                    s.wait("vector", d_acc)
                                s.emit("vector", lambda e, f=f, ps=ps: e.tensor_copy(
                                    out=ACC[:, f, :], in_=ps[:, :]), inc=c_evv)
                            else:
                                s.emit("vector", lambda e, f=f, ps=ps: e.tensor_add(
                                    out=ACC[:, f, :], in0=ACC[:, f, :], in1=ps[:, :]), inc=c_evv)
                            ps_free[pi] = c_evv.count
                            pi += 1
                        grp_mm[(q, g)] = c_mm[0].count
                    evac_q_fn(q)

        def bcast_const_row(scope, src_row_ap, out_tile, scale=1.0, extra=None):
            with ExitStack() as ps_sc:
                PSB = mkpsum(ps_sc, [128, 512])
                s.wait("tensor", c_evv)
                s.wait("tensor", c_eva)
                s.emit("tensor", lambda e: e.matmul(
                    PSB[:, 0:src_row_ap.shape[-1]], onesrow_f[:], src_row_ap,
                    start=True, stop=True), inc=c_pe)
                s.wait("scalar", c_pe)
                s.emit("scalar", lambda e: e.activation(
                    out=out_tile, in_=PSB[:, 0:src_row_ap.shape[-1]],
                    func=AF.Identity, scale=scale), inc=c_eva)

        side_jobs = ([("c", wout_t[g, f], woutbf[g, f], 2048)
                      for g in range(2) for f in range(16)]
                     + [("g", w1_t[i], 2048, 0) for i in range(KF)]
                     + [("g", w2_t[i], 2048, 1) for i in range(KF)]
                     + [("g", w3_t[g, f], 1408, 2) for g in range(4) for f in range(16)]
                     + [("c", wm2_t[g, f], wm2bf[g, f], 2048)
                        for g in range(4) for f in range(16)])
        N_WOUT_JOBS = 32
        N_GAMMA_END = 32 + 152
        side_i = [0]
        side_done = {}

        def side_pump(GSTb, SBTb, n=1):
            for _ in range(n):
                gi = side_i[0]
                if gi >= len(side_jobs):
                    return
                side_i[0] += 1
                job = side_jobs[gi]
                gpr = gi % 2
                if gi >= 2:
                    s.wait("sync", c_g, side_done[gi - 2])
                src = job[1]
                wid = job[2] if job[0] == "g" else job[3]
                s.emit("sync", lambda e, src=src, wid=wid, gpr=gpr: e.dma_start(
                    out=GSTb[gpr][:, 0:wid], in_=src), inc=d_g[gpr], inc_n=16)
                if job[0] == "g":
                    m = job[3]
                    s.wait("vector", d_g[gpr])
                    qw = wid // 4
                    for qq in range(4):
                        s.emit("vector", lambda e, qq=qq, qw=qw, gpr=gpr, GSTb=GSTb: e.tensor_reduce(
                            out=gtmp[:], in_=GSTb[gpr][:, qq * qw:(qq + 1) * qw],
                            axis=AX.X, op=ALU.add, apply_absolute_value=True), inc=c_evv)
                        s.wait("vector", c_evv)
                        fn = lambda e, m=m: e.tensor_add(
                            out=gacc[:, m:m + 1], in0=gacc[:, m:m + 1], in1=gtmp[:])
                        if qq == 3:
                            s.emit("vector", fn, inc=c_g)
                        else:
                            s.emit("vector", fn, inc=c_evv)
                        s.wait("vector", c_evv if qq < 3 else c_g)
                else:
                    dst = job[2]
                    ceng = "scalar" if gpr == 0 else "vector"
                    s.wait(ceng, d_g[gpr])
                    s.wait(ceng, d_side)
                    if ceng == "scalar":
                        s.emit("scalar", lambda e, gpr=gpr, GSTb=GSTb, SBTb=SBTb: e.activation(
                            out=SBTb[gpr][:], in_=GSTb[gpr][:], func=AF.Identity), inc=c_g)
                        s.wait("scalar", c_g)
                        s.emit("scalar", lambda e, dst=dst, gpr=gpr, SBTb=SBTb: e.dma_start(
                            out=dst, in_=SBTb[gpr][:]), inc=d_side, inc_n=16)
                    else:
                        s.emit("vector", lambda e, gpr=gpr, GSTb=GSTb, SBTb=SBTb: e.tensor_copy(
                            out=SBTb[gpr][:], in_=GSTb[gpr][:]), inc=c_g)
                        s.wait("scalar", c_g)
                        s.emit("scalar", lambda e, dst=dst, gpr=gpr, SBTb=SBTb: e.dma_start(
                            out=dst, in_=SBTb[gpr][:]), inc=d_side, inc_n=16)
                side_done[gi] = c_g.count

        xt_r = xt.rearrange("(j p) t -> p j t", p=128)
        with ExitStack() as scA:
            Bsig0 = mktile(scA, [128, T], BF16)
            Brho2 = mktile(scA, [128, T], BF16)

            with ExitStack() as sc:
                Bmu, Brho, Bsig, scr = load_res_stats(
                    sc, [xt_r[:, j, :] for j in range(KD)], "A")
                s.wait("vector", c_eva)
                s.emit("vector", lambda e: e.tensor_copy(out=Bsig0[:], in_=Bsig[:]), inc=c_evv)
                center_res(Bmu, Brho, scr)

            s.fence()
            with ExitStack() as sc:
                XPS = [mktile(sc, [128, 3 + T], F32) for _ in range(2)]
                XC = mktile(sc, [128, T], F32)
                XCS = mktile(sc, [128, T], BF16)
                ZSIL = mktile(sc, [128, T], BF16)
                GST = [mktile(sc, [128, 2048], F32) for _ in range(2)]
                SBT = [mktile(sc, [128, 2048], BF16) for _ in range(2)]
                PSA = mkpsum(sc, [128, 2048])
                PSB = mkpsum(sc, [128, 2048])
                s.wait("gpsimd", c_pool)
                s.emit("gpsimd", lambda e: e.memset(XPS[0][:, 0:3], 0.0), inc=c_pool)
                s.emit("gpsimd", lambda e: e.memset(XPS[1][:, 0:3], 0.0), inc=c_pool)

                def pool_extra(i):
                    side_pump(GST, SBT, 2 if i % 2 == 0 else 1)

                state = {"xc_sil": 0, "ytt": 0, "xpcopy": {}, "zsil": 0}

                def evac_a1(i, ps, pr):
                    pair = i // 2
                    j = pair
                    if pr == 0:
                        s.wait("scalar", c_mm[0])
                        if pair >= 2:
                            s.wait("scalar", c_evv, state["xpcopy"].get(pair - 2, (0, 0))[1])
                        s.emit("scalar", lambda e, pair=pair: e.activation(
                            out=XPS[pair % 2][:, 3:3 + T], in_=ps[:, :],
                            func=AF.Identity), inc=c_eva)
                        free = (c_eva, c_eva.count)
                        s.wait("vector", c_eva)
                        if pair >= 1:
                            s.wait("vector", c_eva, state["xc_sil"])
                        s.raw("vector", lambda e, pair=pair, j=j: e.tensor_scalar(
                            out=XC[:], in0=XPS[pair % 2][:, 0:T],
                            scalar1=convw[:, j, 0:1], scalar2=0.0,
                            op0=ALU.mult, op1=ALU.add))
                        for k in range(1, 3):
                            s.raw("vector", lambda e, pair=pair, j=j, k=k: e.scalar_tensor_tensor(
                                out=XC[:], in0=XPS[pair % 2][:, k:k + T],
                                scalar=convw[:, j, k:k + 1], in1=XC[:],
                                op0=ALU.mult, op1=ALU.add))
                        s.emit("vector", lambda e, pair=pair, j=j: e.scalar_tensor_tensor(
                            out=XC[:], in0=XPS[pair % 2][:, 3:3 + T],
                            scalar=convw[:, j, 3:4], in1=XC[:],
                            op0=ALU.mult, op1=ALU.add), inc=c_evv)
                        state["xpcopy"][pair] = (c_evv, c_evv.count)
                        s.wait("scalar", c_evv)
                        if pair >= 1:
                            s.wait("scalar", c_evv, state["ytt"])
                        s.emit("scalar", lambda e, j=j: e.activation(
                            out=XCS[:], in_=XC[:], func=AF.Silu,
                            bias=convb[:, j:j + 1]), inc=c_eva)
                        state["xc_sil"] = c_eva.count
                        return free
                    else:
                        s.wait("scalar", c_mm[1])
                        if pair >= 1:
                            s.wait("scalar", c_evv, state["ytt"])
                        s.emit("scalar", lambda e: e.activation(
                            out=ZSIL[:], in_=ps[:, :], func=AF.Silu), inc=c_eva)
                        free = (c_eva, c_eva.count)
                        s.wait("vector", c_eva)
                        s.wait("vector", d_ev)
                        s.emit("vector", lambda e, pair=pair: e.tensor_mul(
                            out=EVB[pair % 2][:], in0=XCS[:], in1=ZSIL[:]), inc=c_evv)
                        state["ytt"] = c_evv.count
                        s.wait("scalar", c_evv, state["ytt"])
                        s.emit("scalar", lambda e, j=j, pair=pair: e.dma_start(
                            out=ysp[j], in_=EVB[pair % 2][:]), inc=d_ev, inc_n=16)
                        return free

                order = []
                for j in range(32):
                    order.append(win_t[j])
                    order.append(win_t[j + 32])
                stream_panels(order, 16, PSA, PSB, evac_a1, pool_extra=pool_extra)

            s.fence()
            with ExitStack() as sc:
                GRP = [mktile(sc, [128, 16, 512], BF16) for _ in range(2)]
                ACC = mktile(sc, [128, 16, 512], F32)
                T1 = mktile(sc, [128, 512], BF16)
                SQ2 = T1
                SR = mktile(sc, [1, T], F32)
                rho2_row_done = [0]

                ps_q2 = mkpsum(sc, [1, T])

                def grp_src(q, g):
                    return ysp[g * 16:(g + 1) * 16, :, q * 512:(q + 1) * 512].rearrange(
                        "c p t -> p c t")

                def evac_a2(q):
                    s.wait("vector", c_evv)
                    for j in range(KD):
                        s.wait("vector", c_pe)
                        s.emit("vector", lambda e, j=j, q=q, T1=T1: e.tensor_mul(
                            out=T1[:], in0=RES[:, j, q * 512:(q + 1) * 512],
                            in1=Bsig0[:, q * 512:(q + 1) * 512]), inc=c_evv)
                        s.emit("vector", lambda e, j=j, q=q, T1=T1, ACC=ACC: e.tensor_add(
                            out=RES[:, j, q * 512:(q + 1) * 512], in0=T1[:],
                            in1=ACC[:, j, :]), inc=c_evv)
                        x1_done = c_evv.count
                        s.wait("scalar", c_evv, x1_done)
                        if j >= 1 or q >= 1:
                            s.wait("scalar", c_pe)
                        s.emit("scalar", lambda e, j=j, q=q, SQ2=SQ2: e.activation(
                            out=SQ2[:], in_=RES[:, j, q * 512:(q + 1) * 512],
                            func=AF.Square), inc=c_eva)
                        s.wait("tensor", c_eva)
                        fn = lambda e, j=j, q=q, SQ2=SQ2, ps_q2=ps_q2: e.matmul(
                            ps_q2[:, q * 512:(q + 1) * 512], onescol[:], SQ2[:],
                            start=(j == 0), stop=(j == KD - 1))
                        s.emit("tensor", fn, inc=c_pe)

                GST2 = [mktile(sc, [128, 2048], F32) for _ in range(2)]
                _sbt2 = mktile(sc, [128, 2048], BF16)
                SBT2 = [_sbt2, _sbt2]

                def side_a2(pi):
                    if pi % 8 < 3:
                        side_pump(GST2, SBT2, 1)

                panels = [[[woutbf[g, f] for f in range(16)] for g in range(2)]
                          for q in range(NQ)]
                acc_stage(panels, 16, 2, grp_src, ACC, GRP, evac_a2,
                          bf16_pan=True, side_fn=side_a2)
                s.wait("vector", c_pe)
                s.emit("vector", lambda e, SR=SR, ps_q2=ps_q2: e.tensor_copy(out=SR[:], in_=ps_q2[:, :]), inc=c_evv)
                with ExitStack() as ps_sc:
                    PSB2 = mkpsum(ps_sc, [128, 2048])
                    s.wait("tensor", c_evv)
                    for q in range(NQ):
                        fn = lambda e, q=q, PSB2=PSB2, SR=SR: e.matmul(
                            PSB2[:, q * 512:(q + 1) * 512], onesrow_f[:],
                            SR[:, q * 512:(q + 1) * 512], start=True, stop=True)
                        s.emit("tensor", fn, inc=c_pe) if q == NQ - 1 else s.raw("tensor", fn)
                    s.wait("scalar", c_pe)
                    s.emit("scalar", lambda e, PSB2=PSB2: e.activation(
                        out=Brho2[:], in_=PSB2[:, :], func=AF.Ln), inc=c_eva)
                    s.emit("scalar", lambda e: e.activation(
                        out=Brho2[:], in_=Brho2[:], func=AF.Exp, scale=-0.5), inc=c_eva)
                    s.emit("scalar", lambda e: e.activation(
                        out=Brho2[:], in_=Brho2[:], func=AF.Identity,
                        scale=float(np.sqrt(D))), inc=c_eva)

            s.fence()

            with ExitStack() as sc:
                PSA = mkpsum(sc, [128, 2048])
                PSB = mkpsum(sc, [128, 2048])

                GST3 = [mktile(sc, [128, 2048], F32) for _ in range(2)]
                SBT3 = [mktile(sc, [128, 2048], BF16) for _ in range(2)]

                def pool_extra_a3(i):
                    side_pump(GST3, SBT3, 2)

                def evac_a3(i, ps, pr):
                    s.wait("scalar", c_mm[pr])
                    s.wait("scalar", d_ev)
                    s.emit("scalar", lambda e, i=i, ps=ps: e.activation(
                        out=EVB[i % 2][:], in_=ps[:, :], func=AF.Relu), inc=c_eva)
                    free = (c_eva, c_eva.count)
                    s.wait("vector", c_eva)
                    s.emit("vector", lambda e, i=i: e.tensor_mul(
                        out=EVB[i % 2][:], in0=EVB[i % 2][:], in1=Brho2[:]), inc=c_evv)
                    s.wait("scalar", c_evv)
                    s.emit("scalar", lambda e, i=i: e.dma_start(
                        out=usp[i], in_=EVB[i % 2][:]), inc=d_ev, inc_n=16)
                    return free

                stream_panels([wm1_t[i] for i in range(64)], 16, PSA, PSB, evac_a3,
                              pool_extra=pool_extra_a3)

            s.fence()
            with ExitStack() as sc:
                GRP = [mktile(sc, [128, 16, 512], BF16) for _ in range(2)]
                ACC = mktile(sc, [128, 16, 512], F32)
                XB = mktile(sc, [128, 4, 512], F32)

                def grp_src(q, g):
                    return usp[g * 16:(g + 1) * 16, :, q * 512:(q + 1) * 512].rearrange(
                        "c p t -> p c t")

                def evac_a4(q):
                    for j in range(KD):
                        xslot = j % 4
                        s.wait("sync", c_evv)
                        s.emit("sync", lambda e, j=j, q=q, xslot=xslot, XB=XB: e.dma_start(
                            out=XB[:, xslot, :], in_=xt_r[:, j, q * 512:(q + 1) * 512]),
                            inc=d_xb, inc_n=16)
                        s.wait("vector", d_xb)
                        s.wait("vector", c_evv)
                        s.emit("vector", lambda e, j=j, xslot=xslot, ACC=ACC, XB=XB: e.tensor_add(
                            out=ACC[:, j, :], in0=ACC[:, j, :], in1=XB[:, xslot, :]),
                            inc=c_evv)
                        s.emit("vector", lambda e, j=j, q=q, ACC=ACC: e.tensor_add(
                            out=ACC[:, j, :], in0=ACC[:, j, :],
                            in1=RES[:, j, q * 512:(q + 1) * 512]), inc=c_evv)
                        add_done = c_evv.count
                        s.wait("scalar", c_evv, add_done)
                        s.emit("scalar", lambda e, j=j, q=q, ACC=ACC: e.dma_start(
                            out=Rm[j][:, q * 512:(q + 1) * 512], in_=ACC[:, j, :]),
                            inc=d_acc, inc_n=16)

                assert side_i[0] >= 0
                panels = [[[wm2bf[g, f] for f in range(16)] for g in range(4)]
                          for q in range(NQ)]
                acc_stage(panels, 16, 4, grp_src, ACC, GRP, evac_a4, bf16_pan=True)

        s.fence()
        if stop_after == "A":
            emit_tail(nc, s, Rm, outT, STG)
            s.build()
            return nc

        with ExitStack() as sc:
            SG = mktile(sc, [1, 512], F32)
            with ExitStack() as ps_sc:
                PSG = mkpsum(ps_sc, [1, 512])
                onescol_f = mktile(sc, [128, 1], F32)
                s.emit("sync", lambda e: e.dma_start(
                    out=onescol_f[:], in_=onescolf_c.ap()), inc=d_misc, inc_n=16)
                s.wait("tensor", d_misc)
                s.wait("tensor", c_g)
                s.emit("tensor", lambda e: e.matmul(
                    PSG[:, 0:4], onescol_f[:], gacc[:], start=True, stop=True), inc=c_pe)
                s.wait("scalar", c_pe)
                s.emit("scalar", lambda e: e.activation(
                    out=SG[:, 0:4], in_=PSG[:, 0:4], func=AF.Identity,
                    scale=0.5 / FFN_CNT, bias=epsc[0:1, 1:2]), inc=c_eva)
            with ExitStack() as ps_sc:
                PSG2 = mkpsum(ps_sc, [128, 512])
                s.wait("tensor", c_eva)
                s.emit("tensor", lambda e: e.matmul(
                    PSG2[:, 0:4], onesrow_f[:], SG[:, 0:4], start=True, stop=True), inc=c_pe)
                s.wait("scalar", c_pe)
                s.emit("scalar", lambda e: e.activation(
                    out=thrv[:], in_=PSG2[:, 0:4], func=AF.Identity), inc=c_eva)
                s.emit("scalar", lambda e: e.activation(
                    out=nthrv[:], in_=PSG2[:, 0:4], func=AF.Identity, scale=-1.0), inc=c_eva)
        s.fence()

        def emit_ffn(app, final_to_out):
            with ExitStack() as scF:
                with ExitStack() as sc:
                    Bmu, Bs, scr = load_res_stats(
                        sc, [Rm[j] for j in range(KD)], "LN")
                    center_res(Bmu, Bs, scr)
                    if hsp is not None and app == 0:
                        s.wait("sync", c_evv)
                        for j in range(KD):
                            s.emit("sync", lambda e, j=j: e.dma_start(
                                out=hsp[j], in_=RES[:, j, :]), inc=d_stv, inc_n=16)
                s.fence()

                with ExitStack() as sc:
                    PSA = mkpsum(sc, [128, 2048])
                    PSB = mkpsum(sc, [128, 2048])
                    if app == 0:
                        QA = mktile(sc, [128, 2048], F32)
                        QB = mktile(sc, [128, 2048], F32)
                        WQS = [mktile(sc, [128, 1408], F32) for _ in range(2)]
                        WQB = mktile(sc, [128, 1408], BF16)
                        w3cur = [0]
                        w3q_done = []

                        def pool_extra_w3(i):
                            gi = w3cur[0]
                            if gi >= 64:
                                return
                            w3cur[0] += 1
                            gpr = gi % 2
                            g, f = gi // 16, gi % 16
                            if gi >= 2:
                                s.wait("sync", c_g, w3q_done[gi - 2])
                            s.emit("sync", lambda e, g=g, f=f, gpr=gpr: e.dma_start(
                                out=WQS[gpr][:], in_=w3_t[g, f]), inc=d_g[gpr], inc_n=16)
                            s.wait("vector", d_g[gpr])
                            s.wait("vector", d_stv)
                            s.raw("vector", lambda e, gpr=gpr: e.tensor_scalar(
                                out=QA[:, 0:1408], in0=WQS[gpr][:],
                                scalar1=thrv[:, 2:3], scalar2=1.0,
                                op0=ALU.is_gt, op1=ALU.mult))
                            s.raw("vector", lambda e, gpr=gpr: e.tensor_scalar(
                                out=QB[:, 0:1408], in0=WQS[gpr][:],
                                scalar1=nthrv[:, 2:3], scalar2=1.0,
                                op0=ALU.is_ge, op1=ALU.mult))
                            s.emit("vector", lambda e: e.scalar_tensor_tensor(
                                out=WQB[:], in0=QA[:, 0:1408], scalar=-1.0,
                                in1=QB[:, 0:1408], op0=ALU.add, op1=ALU.add), inc=c_g)
                            w3q_done.append(c_g.count)
                            s.wait("sync", c_g, w3q_done[-1])
                            s.emit("sync", lambda e, g=g, f=f: e.dma_start(
                                out=wq3sp[g, f], in_=WQB[:]), inc=d_stv, inc_n=16)
                    else:
                        QA = QB = None
                        pool_extra_w3 = None

                    def evac_f1a(i, ps, pr):
                        s.wait("scalar", c_mm[pr])
                        s.wait("scalar", d_sta)
                        s.wait("scalar", d_ev)
                        s.emit("scalar", lambda e, i=i, ps=ps: e.activation(
                            out=EVB[i % 2][:], in_=ps[:, :], func=AF.Sigmoid), inc=c_eva)
                        free = (c_eva, c_eva.count)
                        s.wait("scalar", c_eva)
                        s.emit("scalar", lambda e, i=i: e.dma_start(
                            out=sgsp[i], in_=EVB[i % 2][:]), inc=d_sta, inc_n=16)
                        return free

                    if app == 0:
                        stream_panels([w1_t[i] for i in range(KF)], 16, PSA, PSB,
                                      evac_f1a, quant=(0, QA, QB),
                                      pool_extra=pool_extra_w3,
                                      spills=[wq1sp[i] for i in range(KF)])
                    else:
                        stream_panels([wq1sp[i] for i in range(KF)], 16, PSA, PSB,
                                      evac_f1a, bf16_src=True)

                    SGB = [mktile(sc, [128, 2048], BF16) for _ in range(2)]
                    sg_ld = {}

                    def evac_f1b(i, ps, pr):
                        s.wait("sync", c_evv)
                        s.emit("sync", lambda e, i=i: e.dma_start(
                            out=SGB[i % 2][:], in_=sgsp[i]), inc=d_grp[i % 2], inc_n=16)
                        s.wait("vector", d_grp[i % 2])
                        s.wait("vector", c_mm[pr])
                        s.wait("vector", d_ev)
                        s.wait("vector", d_sta)
                        s.emit("vector", lambda e, i=i, ps=ps: e.tensor_mul(
                            out=EVB[i % 2][:], in0=SGB[i % 2][:], in1=ps[:, :]), inc=c_evv)
                        free = (c_evv, c_evv.count)
                        s.wait("scalar", c_evv)
                        s.emit("scalar", lambda e, i=i: e.dma_start(
                            out=hidsp[i], in_=EVB[i % 2][:]), inc=d_ev, inc_n=16)
                        return free

                    if app == 0:
                        stream_panels([w2_t[i] for i in range(KF)], 16, PSA, PSB,
                                      evac_f1b, quant=(1, QA, QB),
                                      pool_extra=pool_extra_w3,
                                      spills=[wq2sp[i] for i in range(KF)])
                    else:
                        stream_panels([wq2sp[i] for i in range(KF)], 16, PSA, PSB,
                                      evac_f1b, bf16_src=True)
                s.fence()

                with ExitStack() as sc:
                    GRP = [mktile(sc, [128, 11, 512], BF16) for _ in range(2)]
                    ACC = mktile(sc, [128, 16, 512], F32)
                    RB = mktile(sc, [128, 4, 512], F32)

                    def grp_src(q, g):
                        return hidsp[g * 11:(g + 1) * 11, :, q * 512:(q + 1) * 512].rearrange(
                            "c p t -> p c t")

                    def evac_f2(q):
                        for j in range(KD):
                            xslot = j % 4
                            s.wait("sync", c_evv)
                            s.emit("sync", lambda e, j=j, q=q, xslot=xslot: e.dma_start(
                                out=RB[:, xslot, :], in_=Rm[j][:, q * 512:(q + 1) * 512]),
                                inc=d_xb, inc_n=16)
                            s.wait("vector", d_xb)
                            s.wait("vector", c_evv)
                            s.emit("vector", lambda e, j=j, xslot=xslot: e.tensor_add(
                                out=ACC[:, j, :], in0=ACC[:, j, :], in1=RB[:, xslot, :]),
                                inc=c_evv)
                            add_done = c_evv.count
                            s.wait("scalar", c_evv, add_done)
                            dst = outT if final_to_out else Rm
                            s.emit("scalar", lambda e, j=j, q=q, dst=dst: e.dma_start(
                                out=dst[j][:, q * 512:(q + 1) * 512], in_=ACC[:, j, :]),
                                inc=d_acc, inc_n=16)

                    panels = [[[wq3sp[g, f] for f in range(16)] for g in range(4)]
                              for q in range(NQ)]
                    acc_stage(panels, 11, 4, grp_src, ACC, GRP, evac_f2, bf16_pan=True)
            s.fence()

        emit_ffn(0, False)
        if stop_after == "F1":
            emit_tail(nc, s, Rm, outT, STG)
            s.build()
            return nc

        with ExitStack() as scB:
            ckv = mktile(scB, [128, 4, T], BF16)
            cq = mktile(scB, [128, 8, T], BF16)

            with ExitStack() as sc:
                Bmu, Bs, scr = load_res_stats(sc, [Rm[j] for j in range(KD)], "LN")
                center_res(Bmu, Bs, scr)
            s.fence()

            with ExitStack() as sc:
                PSA = mkpsum(sc, [128, 2048])
                PSB = mkpsum(sc, [128, 2048])

                def evac_ckv(i, ps, pr):
                    s.wait("scalar", c_mm[pr])
                    s.emit("scalar", lambda e, i=i, ps=ps: e.activation(
                        out=ckv[:, i, :], in_=ps[:, :], func=AF.Identity,
                        bias=bdkv[:, i:i + 1]), inc=c_eva)
                    return (c_eva, c_eva.count)

                def evac_cq(i, ps, pr):
                    s.wait("scalar", c_mm[pr])
                    s.emit("scalar", lambda e, i=i, ps=ps: e.activation(
                        out=cq[:, i, :], in_=ps[:, :], func=AF.Identity,
                        bias=bdq[:, i:i + 1]), inc=c_eva)
                    return (c_eva, c_eva.count)

                stream_panels([wdkv_t[i] for i in range(4)], 16, PSA, PSB, evac_ckv)
                s.wait("tensor", c_eva)
                stream_panels([wdq_t[i] for i in range(8)], 16, PSA, PSB, evac_cq)
            s.fence()

            with ExitStack() as sc:
                kCb = mktile(sc, [128, T], BF16)
                kRb = mktile(sc, [128, T], BF16)
                qCb = mktile(sc, [128, T], BF16)
                krp = qCb
                qRb = mktile(sc, [128, T], BF16)
                Vtmb = mktile(sc, [128, 16, 132], BF16)
                ptile_full = mktile(sc, [128, T], BF16)
                pTs_full = mktile(sc, [128, 1024], BF16)
                otm = mktile(sc, [128, 128], BF16)
                ropeC = mktile(sc, [128, T], BF16)
                ropeS = mktile(sc, [128, T], BF16)
                tril = mktile(sc, [128, 128], F32)
                rows = mktile(sc, [128, 8], F32)
                HP = [mktile(sc, [128, 4096], BF16) for _ in range(2)]

                dbgt = None
                PBLD = mkpsum(sc, [128, 1024])
                PSF = mkpsum(sc, [128, 2048])
                PO = mkpsum(sc, [128, 512])
                PT = mkpsum(sc, [128, 1024], BF16)

                s.emit("sync", lambda e: e.dma_start(out=ropeC[:], in_=ropeC_c.ap()),
                       inc=d_misc, inc_n=16)
                s.emit("sync", lambda e: e.dma_start(out=ropeS[:], in_=ropeS_c.ap()),
                       inc=d_misc, inc_n=16)
                s.emit("sync", lambda e: e.dma_start(out=tril[:], in_=tril_c.ap()),
                       inc=d_misc, inc_n=16)
                s.wait("vector", d_misc)
                s.wait("tensor", d_misc)

                hp_cast = [0, 0]
                build_done = {}

                def load_head(h, prev_attend_pe):
                    pr = h % 2
                    if h >= 2:
                        s.wait("sync", c_pe, build_done[h - 2])
                    if h >= 1:
                        s.wait("sync", c_pool, hp_cast[1 - pr])
                    for (dst0, src) in ((0, wuk_t[h]), (512, wuv_t[h]), (1024, wuq_t[h])):
                        wdt = src.shape[-1]
                        s.emit("sync", lambda e, dst0=dst0, src=src, wdt=wdt: e.dma_start(
                            out=STG[0][:, dst0:dst0 + wdt], in_=src), inc=d_stg[0], inc_n=16)
                    s.emit("sync", lambda e, h=h: e.dma_start(
                        out=STG[1][:], in_=wkr_t[h]), inc=d_stg[1], inc_n=16)
                    s.wait("gpsimd", d_stg[0])
                    s.wait("gpsimd", d_stg[1])
                    if h >= 2:
                        s.wait("gpsimd", c_pe, build_done[h - 2])
                    s.raw("gpsimd", lambda e, pr=pr: e.tensor_copy(
                        out=HP[pr][:, 0:2048], in_=STG[0][:]))
                    s.emit("gpsimd", lambda e, pr=pr: e.tensor_copy(
                        out=HP[pr][:, 2048:4096], in_=STG[1][:]), inc=c_pool)
                    hp_cast[pr] = c_pool.count

                def build_head(h, prev_attend_pe, prev_ropeq_evv):
                    pr = h % 2
                    W = HP[pr]
                    s.wait("tensor", c_pool, hp_cast[pr])
                    if prev_ropeq_evv:
                        s.wait("tensor", c_evv, prev_ropeq_evv)
                    if prev_attend_pe:
                        s.wait("scalar", c_pe, prev_attend_pe)
                        s.wait("vector", c_pe, prev_attend_pe)

                    def proj(dstbuf, w0, nc_chunks, rhs, rhs_kc0, biascol, scale=1.0,
                             rows16=False):
                        evs = []
                        for t in range(NQ):
                            half = (t % 2) * 512
                            if t >= 2:
                                s.wait("tensor", c_eva, evs[t - 2])
                            for c in range(nc_chunks):
                                fn = lambda e, t=t, c=c, half=half, w0=w0, rhs=rhs: e.matmul(
                                    PBLD[:, half:half + 512],
                                    W[:, w0 + c * 128:w0 + (c + 1) * 128],
                                    rhs[:, c, t * 512:(t + 1) * 512],
                                    start=(c == 0), stop=(c == nc_chunks - 1))
                                if c == nc_chunks - 1:
                                    s.emit("tensor", fn, inc=c_pe)
                                else:
                                    s.raw("tensor", fn)
                            s.wait("scalar", c_pe)
                            s.emit("scalar", lambda e, t=t, half=half, dstbuf=dstbuf,
                                   biascol=biascol, scale=scale: e.activation(
                                out=dstbuf[:, t * 512:(t + 1) * 512],
                                in_=PBLD[:, half:half + 512], func=AF.Identity,
                                scale=scale, bias=biascol), inc=c_eva)
                            evs.append(c_eva.count)
                        return evs

                    def rope(srcbuf, dstbuf, scratch):
                        s.wait("tensor", c_eva)
                        done = 0
                        for t in range(NQ):
                            half = (t % 2) * 512
                            sl = slice(t * 512, (t + 1) * 512)
                            if t >= 2:
                                s.wait("tensor", c_evv, done)
                            s.emit("tensor", lambda e, sl=sl, half=half, srcbuf=srcbuf: e.matmul(
                                PBLD[:, half:half + 512], swapP_sb[:], srcbuf[:, sl],
                                start=True, stop=True), inc=c_pe)
                            sw = c_pe.count
                            s.wait("vector", c_pe, sw)
                            s.raw("vector", lambda e, sl=sl, srcbuf=srcbuf, dstbuf=dstbuf: e.tensor_mul(
                                out=dstbuf[:, sl], in0=srcbuf[:, sl], in1=ropeC[:, sl]))
                            s.raw("vector", lambda e, sl=sl, half=half, scratch=scratch: e.tensor_mul(
                                out=scratch[:, 0:512], in0=PBLD[:, half:half + 512],
                                in1=ropeS[:, sl]))
                            s.emit("vector", lambda e, sl=sl, dstbuf=dstbuf, scratch=scratch: e.tensor_add(
                                out=dstbuf[:, sl], in0=dstbuf[:, sl], in1=scratch[:, 0:512]),
                                inc=c_evv)
                            done = c_evv.count
                        return done

                    proj(kCb, 0, 4, ckv, 0, buk[:, h:h + 1])
                    proj(krp, 2048, KD, RES, 0, bkr[:, h:h + 1])
                    rope_k_done = rope(krp, kRb, ptile_full)
                    s.wait("tensor", c_evv, rope_k_done)
                    for g in range(2):
                        if g == 1:
                            s.wait("tensor", c_eva)
                        for ktl in range(8):
                            kt = g * 8 + ktl
                            for c in range(4):
                                s.raw("tensor", lambda e, kt=kt, ktl=ktl, c=c, W=W: e.matmul(
                                    PBLD[:, ktl * 128:(ktl + 1) * 128],
                                    ckv[:, c, kt * 128:(kt + 1) * 128],
                                    W[:, 512 + c * 128:512 + (c + 1) * 128],
                                    start=(c == 0), stop=False))
                            fn = lambda e, kt=kt, ktl=ktl, h=h: e.matmul(
                                PBLD[:, ktl * 128:(ktl + 1) * 128], onesrow_f[:],
                                buv_row[0:1, h * 128:(h + 1) * 128], start=False, stop=True)
                            if ktl == 7:
                                s.emit("tensor", fn, inc=c_pe)
                            else:
                                s.raw("tensor", fn)
                        s.wait("scalar", c_pe)
                        s.emit("scalar", lambda e, g=g: e.activation(
                            out=Vtmb[:, g * 8:(g + 1) * 8, 0:128],
                            in_=PBLD[:, :].rearrange("p (a b) -> p a b", b=128),
                            func=AF.Identity), inc=c_eva)
                    v_done = c_eva.count
                    s.wait("tensor", c_eva, v_done)
                    s.wait("scalar", c_evv, rope_k_done)
                    proj(qCb, 1024, 8, cq, 0, buq16[:, h:h + 1], scale=1.0 / 16.0)
                    ready = rope(qCb, qRb, ptile_full)
                    ropeq_evv = ready
                    build_done[h] = c_pe.count
                    return ready, build_done[h], ropeq_evv

                def attend_head(h, ready_c):
                    pr = h % 2
                    s.wait("tensor", c_evv, ready_c)
                    s.wait("tensor", c_eva)
                    psf_free = 0
                    pt_copy = 0
                    pt2_read = 0
                    po_read = 0
                    for qi in range(16):
                        nk = (qi + 1) * 128
                        nsl = (nk + 511) // 512
                        if psf_free:
                            s.wait("tensor", c_eva, psf_free)
                        for sl in range(nsl):
                            w = min(512, nk - sl * 512)
                            s.raw("tensor", lambda e, sl=sl, w=w, qi=qi: e.matmul(
                                PSF[:, sl * 512:sl * 512 + w],
                                qCb[:, qi * 128:(qi + 1) * 128],
                                kCb[:, sl * 512:sl * 512 + w], start=True, stop=False))
                            fn = lambda e, sl=sl, w=w, qi=qi: e.matmul(
                                PSF[:, sl * 512:sl * 512 + w],
                                qRb[:, qi * 128:(qi + 1) * 128],
                                kRb[:, sl * 512:sl * 512 + w], start=False, stop=True)
                            if sl == nsl - 1:
                                s.emit("tensor", fn, inc=c_pe)
                            else:
                                s.raw("tensor", fn)
                        sc_done = c_pe.count
                        s.wait("vector", c_pe, sc_done)
                        s.emit("vector", lambda e, nk=nk: e.tensor_add(
                            out=PSF[:, nk - 128:nk], in0=PSF[:, nk - 128:nk],
                            in1=tril[:]), inc=c_evv)
                        s.emit("vector", lambda e, nk=nk: e.reduce_max(
                            out=rows[:, 4:5], in_=PSF[:, 0:nk], axis=AX.X,
                            negate=True), inc=c_evv)
                        s.wait("vector", c_evv)
                        nm_done = c_evv.count
                        s.wait("scalar", c_evv, nm_done)
                        if pt_copy:
                            pass
                        s.emit("scalar", lambda e, nk=nk: e.activation(
                            out=ptile_full[:, 0:nk], in_=PSF[:, 0:nk], func=AF.Exp,
                            bias=rows[:, 4:5]), inc=c_eva)
                        psf_free = c_eva.count
                        exp_done = c_eva.count
                        nkt = nk // 128
                        first_pv = True
                        for g0 in range(0, nkt, 8):
                            gn = min(8, nkt - g0)
                            s.wait("tensor", c_eva, exp_done)
                            if pt_copy:
                                s.wait("tensor", c_evv, pt_copy)
                            if pt2_read:
                                s.wait("tensor", c_eva, pt2_read)
                            for ktl in range(gn):
                                fn = lambda e, g0=g0, ktl=ktl: e.transpose(
                                    PT[:, ktl * 128:(ktl + 1) * 128],
                                    ptile_full[:, (g0 + ktl) * 128:(g0 + ktl + 1) * 128],
                                    ident[:])
                                if ktl == gn - 1:
                                    s.emit("tensor", fn, inc=c_pe)
                                else:
                                    s.raw("tensor", fn)
                            tr_done = c_pe.count
                            s.wait("vector", c_pe, tr_done)
                            if first_pv and po_read:
                                pass
                            s.emit("vector", lambda e, gn=gn: e.tensor_copy(
                                out=pTs_full[:, 0:gn * 128], in_=PT[:, 0:gn * 128]),
                                inc=c_evv)
                            pt_copy = c_evv.count
                            s.wait("tensor", c_evv, pt_copy)
                            if first_pv and po_read:
                                s.wait("tensor", c_evv, po_read)
                            for ktl in range(gn):
                                kt = g0 + ktl
                                fn = lambda e, kt=kt, ktl=ktl, nkt=nkt: e.matmul(
                                    PO[:, 0:129], pTs_full[:, ktl * 128:(ktl + 1) * 128],
                                    Vtmb[:, kt, 0:129],
                                    start=(kt == 0), stop=(kt == nkt - 1))
                                if ktl == gn - 1:
                                    s.emit("tensor", fn, inc=c_pe)
                                else:
                                    s.raw("tensor", fn)
                            first_pv = False
                        pv_done = c_pe.count
                        s.wait("vector", c_pe, pv_done)
                        s.emit("vector", lambda e: e.reciprocal(
                            out=rows[:, 3:4], in_=PO[:, 128:129]), inc=c_evv)
                        s.wait("vector", c_evv)
                        s.emit("vector", lambda e: e.tensor_scalar(
                            out=otm[:], in0=PO[:, 0:128], scalar1=rows[:, 3:4],
                            scalar2=1.0, op0=ALU.mult, op1=ALU.mult), inc=c_evv)
                        po_read = c_evv.count
                        fin = c_evv.count
                        s.wait("tensor", c_evv, fin)
                        if pt_copy:
                            s.wait("tensor", c_evv, pt_copy)
                        if pt2_read:
                            s.wait("tensor", c_eva, pt2_read)
                        s.emit("tensor", lambda e: e.transpose(
                            PT[:, 0:128], otm[:], ident[:]), inc=c_pe)
                        ot_done = c_pe.count
                        s.wait("scalar", c_pe, ot_done)
                        s.wait("scalar", d_ev)
                        s.emit("scalar", lambda e, qi=qi, pr=pr: e.activation(
                            out=EVB[pr][:, qi * 128:(qi + 1) * 128], in_=PT[:, 0:128],
                            func=AF.Identity), inc=c_eva)
                        pt2_read = c_eva.count
                    s.wait("scalar", c_eva)
                    s.emit("scalar", lambda e, h=h, pr=pr: e.dma_start(
                        out=osp[h], in_=EVB[pr][:]), inc=d_ev, inc_n=16)
                    return c_pe.count

                swapP_sb = mktile(sc, [128, 128], BF16)
                s.emit("sync", lambda e: e.dma_start(out=swapP_sb[:], in_=swapP_c.ap()),
                       inc=d_misc, inc_n=16)
                s.wait("tensor", d_misc)
                s.wait("vector", d_misc)
                s.emit("vector", lambda e: e.memset(Vtmb[:, :, 128:129], 1.0), inc=c_evv)
                s.wait("tensor", c_evv)

                att_pe = 0
                load_head(0, 0)
                ready, bpe, ropeq = build_head(0, 0, 0)
                for h in range(1, H):
                    load_head(h, att_pe)
                    att_pe = attend_head(h - 1, ready)
                    ready, bpe, ropeq = build_head(h, att_pe, ropeq)
                att_pe = attend_head(H - 1, ready)
            s.fence()

            with ExitStack() as sc:
                EVF = [mktile(sc, [128, 2048], F32) for _ in range(2)]
                RST = [mktile(sc, [128, 2048], F32) for _ in range(2)]
                PSA = mkpsum(sc, [128, 2048])
                PSB = mkpsum(sc, [128, 2048])
                for j in range(16):
                    s.emit("sync", lambda e, j=j: e.dma_start(
                        out=RES[:, j, :], in_=osp[j]), inc=d_res, inc_n=16)
                s.wait("tensor", d_res)
                s.wait("gpsimd", d_res)

                def evac_b3(i, ps, pr):
                    s.wait("sync", c_evv)
                    s.emit("sync", lambda e, i=i, pr=pr: e.dma_start(
                        out=RST[pr][:], in_=Rm[i]), inc=d_xb, inc_n=16)
                    s.wait("scalar", c_mm[pr])
                    s.wait("scalar", c_evv)
                    s.emit("scalar", lambda e, i=i, ps=ps, pr=pr: e.activation(
                        out=EVF[pr][:], in_=ps[:, :], func=AF.Identity,
                        bias=bo[:, i:i + 1]), inc=c_eva)
                    free = (c_eva, c_eva.count)
                    s.wait("vector", c_eva)
                    s.wait("vector", d_xb)
                    s.wait("vector", d_acc)
                    s.emit("vector", lambda e, pr=pr: e.tensor_add(
                        out=EVF[pr][:], in0=EVF[pr][:], in1=RST[pr][:]), inc=c_evv)
                    s.wait("scalar", c_evv)
                    s.emit("scalar", lambda e, i=i, pr=pr: e.dma_start(
                        out=Rm[i], in_=EVF[pr][:]), inc=d_acc, inc_n=16)
                    return free

                stream_panels([wo_t[i] for i in range(16)], 16, PSA, PSB, evac_b3)
        s.fence()
        if stop_after == "B":
            emit_tail(nc, s, Rm, outT, STG)
            s.build()
            return nc

        emit_ffn(1, True)
        for name, c in list(s.sems.items()):
            if name.startswith("dma_"):
                s.wait("sync", c)
        s.build()
    return nc


def emit_tail(nc, s, Rm, outT, STG):
    d_tail = s.sem("dma_tail")
    for j in range(KD):
        pr = j % 2
        s.emit("sync", lambda e, j=j, pr=pr: e.dma_start(
            out=STG[pr][:], in_=Rm[j]), inc=d_tail, inc_n=16)
        s.wait("sync", d_tail)
        s.emit("sync", lambda e, j=j, pr=pr: e.dma_start(
            out=outT[j], in_=STG[pr][:]), inc=d_tail, inc_n=16)
    for name, c in list(s.sems.items()):
        if name.startswith("dma_"):
            s.wait("sync", c)


_NC_CACHE = {}


def kernel(**inputs):
    import os
    stop_after = os.environ.get("MK_STOP", "full")
    hp = {
        "win_t": tile_lhsT(np.asarray(inputs["L0_attn_a_W_in"])),
        "wout_t": tile_lhsT(np.asarray(inputs["L0_attn_a_W_out"]), groups=2),
        "wm1_t": tile_lhsT(np.asarray(inputs["L0_attn_a_W_mlp1"])),
        "wm2_t": tile_lhsT(np.asarray(inputs["L0_attn_a_W_mlp2"]), groups=4),
        "w1_t": tile_lhsT(np.asarray(inputs["L0_ffn_w1"])),
        "w2_t": tile_lhsT(np.asarray(inputs["L0_ffn_w2"])),
        "w3_t": tile_lhsT(np.asarray(inputs["L0_ffn_w3"]), groups=4),
        "wdkv_t": tile_lhsT(np.asarray(inputs["L0_attn_b_W_DKV_w"])),
        "wuk_t": tile_lhsT(np.asarray(inputs["L0_attn_b_W_UK_w"])),
        "wuv_t": tile_lhsT(np.asarray(inputs["L0_attn_b_W_UV_w"])),
        "wdq_t": tile_lhsT(np.asarray(inputs["L0_attn_b_W_DQ_w"])),
        "wuq_t": tile_lhsT(np.asarray(inputs["L0_attn_b_W_UQ_w"])),
        "wkr_t": tile_lhsT(np.asarray(inputs["L0_attn_b_W_KR_w"])),
        "wo_t": tile_lhsT(np.asarray(inputs["L0_attn_b_W_O_w"])),
        "convw": np.ascontiguousarray(
            np.asarray(inputs["L0_attn_a_conv_w"]).reshape(32, 128, 4).transpose(1, 0, 2)),
        "convb": chunk_vec(inputs["L0_attn_a_conv_b"]),
        "bdkv": chunk_vec(inputs["L0_attn_b_W_DKV_b"]),
        "buk": chunk_vec(inputs["L0_attn_b_W_UK_b"]),
        "buv_row": np.ascontiguousarray(
            np.asarray(inputs["L0_attn_b_W_UV_b"]).reshape(1, 2048)),
        "bdq": chunk_vec(inputs["L0_attn_b_W_DQ_b"]),
        "buq": chunk_vec(inputs["L0_attn_b_W_UQ_b"]),
        "bkr": chunk_vec(inputs["L0_attn_b_W_KR_b"]),
        "bo": chunk_vec(inputs["L0_attn_b_W_O_b"]),
    }
    x = np.asarray(inputs["L0_attn_a_x"])
    nc = build_program(stop_after)
    in_maps = []
    for b in range(8):
        m = dict(hp)
        m["xt"] = np.ascontiguousarray(x[b].T)
        in_maps.append(m)
    res = run_bass_kernel_spmd(nc, in_maps, core_ids=list(range(8)))
    out = np.empty((8, T, D), np.float32)
    for b in range(8):
        ot = np.asarray(res.results[b]["outT"])
        out[b] = ot.reshape(D, T).T
    return out


# version 12
# speedup vs baseline: 1.1513x; 1.0568x over previous
import numpy as np
import ml_dtypes
from contextlib import ExitStack

import concourse.bass as bass
import concourse.mybir as mybir
from concourse.bass_utils import run_bass_kernel_spmd

F32 = mybir.dt.float32
BF16 = mybir.dt.bfloat16
AF = mybir.ActivationFunctionType
ALU = mybir.AluOpType
AX = mybir.AxisListType
BF = ml_dtypes.bfloat16

D = 2048
T = 2048
KD = 16
NQ = 4
H = 16
KF = 44
EPS_LN = 1e-5
NEG_BIG = -3.0e38
FFN_CNT = 5632 * 2048

ENGINES = ("tensor", "vector", "scalar", "gpsimd", "sync")


class Ctr:
    __slots__ = ("h", "count", "name")

    def __init__(self, h, name):
        self.h = h
        self.count = 0
        self.name = name


class Sched:
    def __init__(self, nc, st):
        self.nc = nc
        self.st = st
        self.q = {e: [] for e in ENGINES}
        self.sems = {}

    def sem(self, name):
        if name not in self.sems:
            h = self.st.enter_context(self.nc.semaphore(name))
            self.sems[name] = Ctr(h, name)
        return self.sems[name]

    def wait(self, eng, ctr, val=None):
        v = ctr.count if val is None else val
        if v <= 0:
            return
        self.q[eng].append(lambda e, h=ctr.h, v=v: e.wait_ge(h, v))

    def emit(self, eng, fn, inc=None, inc_n=1):
        if inc is not None:
            inc.count += inc_n

        def run(e, fn=fn, inc=inc, n=inc_n):
            ins = fn(e)
            if inc is not None:
                ins.then_inc(inc.h, n)

        self.q[eng].append(run)

    def raw(self, eng, fn):
        self.q[eng].append(fn)

    def fence(self):
        for name, c in list(self.sems.items()):
            if name.startswith("dma_"):
                self.wait("sync", c)
        for eng in ENGINES:
            f = self.sem("fence_" + eng)
            if eng in ("tensor", "sync"):
                self.emit(eng, lambda e: e.nop(), inc=f)
            else:
                self.emit(eng, lambda e: e.drain(fusable=False), inc=f)
        for eng in ENGINES:
            for other in ENGINES:
                if other != eng:
                    self.wait(eng, self.sem("fence_" + other))

    def build(self):
        block = self.st.enter_context(self.nc.Block())

        def mk(eng):
            def body(e):
                for f in self.q[eng]:
                    f(e)
            return body

        block.tensor(mk("tensor"))
        block.vector(mk("vector"))
        block.scalar(mk("scalar"))
        block.gpsimd(mk("gpsimd"))
        block.sync(mk("sync"))


def tile_lhsT(Wt, groups=1):
    Fd, Cd = Wt.shape
    nf, kc = Fd // 128, Cd // 128
    L = Wt.reshape(nf, 128, kc, 128).transpose(0, 3, 2, 1)
    if groups == 1:
        return np.ascontiguousarray(L.reshape(nf, 128, kc * 128))
    kg = kc // groups
    Lg = L.reshape(nf, 128, groups, kg, 128).transpose(2, 0, 1, 3, 4)
    return np.ascontiguousarray(Lg.reshape(groups, nf, 128, kg * 128))


def chunk_vec(b):
    return np.ascontiguousarray(np.asarray(b).reshape(-1, 128).T)


def rope_tables():
    pos = np.arange(T, dtype=np.float64)
    idx = np.arange(64, dtype=np.float64)
    theta = (10000.0 ** (-2.0 * idx / 128.0))[:, None] * pos[None, :]
    C = np.repeat(np.cos(theta), 2, axis=0).astype(BF)
    S = np.empty((128, T), np.float64)
    S[0::2] = -np.sin(theta)
    S[1::2] = np.sin(theta)
    return C, S.astype(BF)


def build_program(stop_after="full"):
    nc = bass.Bass("TRN2", target_bir_lowering=False)

    def ein(name, shape):
        return nc.dram_tensor(name, shape, F32, kind="ExternalInput").ap()

    xt = ein("xt", [D, T])
    win_t = ein("win_t", [64, 128, 2048])
    wout_t = ein("wout_t", [2, 16, 128, 2048])
    wm1_t = ein("wm1_t", [64, 128, 2048])
    wm2_t = ein("wm2_t", [4, 16, 128, 2048])
    w1_t = ein("w1_t", [KF, 128, 2048])
    w2_t = ein("w2_t", [KF, 128, 2048])
    w3_t = ein("w3_t", [4, 16, 128, 1408])
    wdkv_t = ein("wdkv_t", [4, 128, 2048])
    wuk_t = ein("wuk_t", [16, 128, 512])
    wuv_t = ein("wuv_t", [16, 128, 512])
    wdq_t = ein("wdq_t", [8, 128, 2048])
    wuq_t = ein("wuq_t", [16, 128, 1024])
    wkr_t = ein("wkr_t", [16, 128, 2048])
    wo_t = ein("wo_t", [16, 128, 2048])
    convw_d = ein("convw", [128, 32, 4])
    convb_d = ein("convb", [128, 32])
    bdkv_d = ein("bdkv", [128, 4])
    buk_d = ein("buk", [128, 16])
    buv_row_d = ein("buv_row", [1, 2048])
    bdq_d = ein("bdq", [128, 8])
    buq_d = ein("buq", [128, 16])
    bkr_d = ein("bkr", [128, 16])
    bo_d = ein("bo", [128, 16])

    import os as _os
    _dbg = set(filter(None, _os.environ.get("MK_DBG", "").split(",")))

    def _scratch(name, shape, dt):
        kind = "ExternalOutput" if name in _dbg else "Internal"
        return nc.dram_tensor(name, shape, dt, kind=kind).ap()

    outT = nc.dram_tensor("outT", [KD, 128, T], F32, kind="ExternalOutput").ap()
    Rm = _scratch("Rm", [KD, 128, T], F32)
    ysp = _scratch("ysp", [32, 128, T], BF16)
    usp = _scratch("usp", [64, 128, T], BF16)
    sgsp = _scratch("sgsp", [KF, 128, T], BF16)
    hidsp = _scratch("hidsp", [KF, 128, T], BF16)
    osp = _scratch("osp", [16, 128, T], BF16)
    wq1sp = _scratch("wq1sp", [KF, 128, 2048], BF16)
    wq2sp = _scratch("wq2sp", [KF, 128, 2048], BF16)
    wq3sp = _scratch("wq3sp", [4, 16, 128, 1408], BF16)
    woutbf = _scratch("woutbf", [2, 16, 128, 2048], BF16)
    wm2bf = _scratch("wm2bf", [4, 16, 128, 2048], BF16)
    hsp = _scratch("hsp", [KD, 128, T], BF16) if "hsp" in _dbg else None
    scdbg = _scratch("scdbg", [4, 128, 512], F32) if "scdbg" in _dbg else None

    ident_c = nc.inline_tensor(np.eye(128, dtype=BF), "ident_c")
    onescol_c = nc.inline_tensor(np.ones((128, 1), dtype=BF), "onescol_c")
    onesrow_f_c = nc.inline_tensor(np.ones((1, 128), np.float32), "onesrow_f_c")
    onescolf_c = nc.inline_tensor(np.ones((128, 1), np.float32), "onescolf_c")
    epsc_np = np.zeros((128, 2), np.float32)
    epsc_np[:, 0] = EPS_LN
    epsc_np[:, 1] = 0.5e-8
    epsc_c = nc.inline_tensor(epsc_np, "epsc_c")
    swapP_np = np.zeros((128, 128), dtype=BF)
    for i in range(64):
        swapP_np[2 * i, 2 * i + 1] = 1
        swapP_np[2 * i + 1, 2 * i] = 1
    swapP_c = nc.inline_tensor(swapP_np, "swapP_c")
    tril_np = np.where(np.arange(128)[None, :] <= np.arange(128)[:, None],
                       np.float32(0.0), np.float32(NEG_BIG)).astype(np.float32)
    tril_c = nc.inline_tensor(tril_np, "tril_c")
    ropeC_np, ropeS_np = rope_tables()
    ropeC_c = nc.inline_tensor(ropeC_np, "ropeC_c")
    ropeS_c = nc.inline_tensor(ropeS_np, "ropeS_c")

    st = ExitStack()
    with st:
        RES = st.enter_context(nc.sbuf_tensor("RES", [128, KD, T], BF16))
        STG = [st.enter_context(nc.sbuf_tensor(f"STG{i}", [128, 2048], F32)) for i in range(2)]
        PAN = [st.enter_context(nc.sbuf_tensor(f"PAN{i}", [128, 2048], BF16)) for i in range(2)]
        EVB = [st.enter_context(nc.sbuf_tensor(f"EVB{i}", [128, 2048], BF16)) for i in range(2)]
        ident = st.enter_context(nc.sbuf_tensor("ident", [128, 128], BF16))
        onescol = st.enter_context(nc.sbuf_tensor("onescol", [128, 1], BF16))
        onesrow_f = st.enter_context(nc.sbuf_tensor("onesrow_f", [1, 128], F32))
        convw = st.enter_context(nc.sbuf_tensor("convw_sb", [128, 32, 4], F32))
        convb = st.enter_context(nc.sbuf_tensor("convb_sb", [128, 32], F32))
        bdkv = st.enter_context(nc.sbuf_tensor("bdkv_sb", [128, 4], F32))
        buk = st.enter_context(nc.sbuf_tensor("buk_sb", [128, 16], F32))
        buv_row = st.enter_context(nc.sbuf_tensor("buv_row_sb", [1, 2048], F32))
        bdq = st.enter_context(nc.sbuf_tensor("bdq_sb", [128, 8], F32))
        buq16 = st.enter_context(nc.sbuf_tensor("buq16", [128, 16], F32))
        bkr = st.enter_context(nc.sbuf_tensor("bkr_sb", [128, 16], F32))
        bo = st.enter_context(nc.sbuf_tensor("bo_sb", [128, 16], F32))
        gacc = st.enter_context(nc.sbuf_tensor("gacc", [128, 4], F32))
        epsc = st.enter_context(nc.sbuf_tensor("epsc", [128, 2], F32))
        gtmp = st.enter_context(nc.sbuf_tensor("gtmp", [128, 1], F32))
        thrv = st.enter_context(nc.sbuf_tensor("thrv", [128, 4], F32))
        nthrv = st.enter_context(nc.sbuf_tensor("nthrv", [128, 4], F32))

        s = Sched(nc, st)
        d_stg = [s.sem("dma_stg0"), s.sem("dma_stg1")]
        d_grp = [s.sem("dma_grp0"), s.sem("dma_grp1")]
        d_res = s.sem("dma_res")
        d_xb = s.sem("dma_xb")
        d_xb4 = [s.sem(f"dma_xb4_{i}") for i in range(4)]
        d_stg4 = [s.sem("dma_stg2"), s.sem("dma_stg3")]
        d_stv = s.sem("dma_stv")
        d_ev = s.sem("dma_ev")
        d_acc = s.sem("dma_acc")
        d_wq = s.sem("dma_wq")
        d_side = s.sem("dma_side")
        d_sta = s.sem("dma_sta")
        d_misc = s.sem("dma_misc")
        d_g = [s.sem("dma_g0"), s.sem("dma_g1")]
        c_pool = s.sem("c_pool")
        c_pan = [s.sem("c_pan0"), s.sem("c_pan1")]
        c_mm = [s.sem("c_mm0"), s.sem("c_mm1")]
        c_evv = s.sem("c_evv")
        c_eva = s.sem("c_eva")
        c_pe = s.sem("c_pe")
        c_g = s.sem("c_g")

        for dst, src in ((epsc, epsc_c.ap()), (ident, ident_c.ap()), (onescol, onescol_c.ap()),
                         (onesrow_f, onesrow_f_c.ap()), (convw, convw_d),
                         (convb, convb_d), (bdkv, bdkv_d), (buk, buk_d),
                         (buv_row, buv_row_d), (bdq, bdq_d), (bkr, bkr_d),
                         (bo, bo_d), (buq16, buq_d)):
            s.emit("sync", lambda e, dst=dst, src=src: e.dma_start(out=dst[:], in_=src),
                   inc=d_misc, inc_n=16)
        s.wait("vector", d_misc)
        s.emit("vector", lambda e: e.tensor_scalar(
            out=buq16[:], in0=buq16[:], scalar1=1.0 / 16.0, scalar2=None, op0=ALU.mult),
            inc=c_evv)
        s.emit("gpsimd", lambda e: e.memset(gacc[:], 0.0), inc=c_pool)
        s.wait("tensor", d_misc)

        uid = [0]

        def mktile(scope, shape, dt):
            uid[0] += 1
            return scope.enter_context(nc.sbuf_tensor(f"t{uid[0]}", shape, dt))

        def mkpsum(scope, shape, dt=F32):
            uid[0] += 1
            return scope.enter_context(nc.psum_tensor(f"p{uid[0]}", shape, dt))

        def load_res_stats(scope, src_chunks, mode):
            SQ = [mktile(scope, [128, T], BF16) for _ in range(2)]
            SR1 = mktile(scope, [1, T], F32)
            SR2 = mktile(scope, [1, T], F32)
            nbt = 3 if mode == "A" else 2
            BT = [mktile(scope, [128, T], BF16) for _ in range(nbt)]
            BTS = mktile(scope, [128, T], BF16)
            cast_c = []
            sq_c = []
            with ExitStack() as ps_sc:
                PS_S = mkpsum(ps_sc, [1, T])
                PS_Q = mkpsum(ps_sc, [1, T])
                for j in range(KD):
                    pr = j % 2
                    if j >= 2:
                        s.wait("sync", c_pan[pr], cast_c[j - 2])
                    s.emit("sync", lambda e, j=j, pr=pr: e.dma_start(
                        out=STG[pr][:], in_=src_chunks[j]), inc=d_stg[pr], inc_n=16)
                    ceng = "scalar" if pr == 0 else "vector"
                    s.wait(ceng, d_stg[pr])
                    if pr == 0:
                        s.emit("scalar", lambda e, j=j, pr=pr: e.activation(
                            out=RES[:, j, :], in_=STG[pr][:], func=AF.Identity), inc=c_pan[0])
                    else:
                        s.emit("vector", lambda e, j=j, pr=pr: e.tensor_copy(
                            out=RES[:, j, :], in_=STG[pr][:]), inc=c_pan[1])
                    cast_c.append(c_pan[pr].count)
                    s.wait("scalar", c_pan[pr], cast_c[j])
                    if j >= 2:
                        s.wait("scalar", c_pe, sq_c[j - 2])
                    s.emit("scalar", lambda e, j=j, pr=pr: e.activation(
                        out=SQ[pr][:], in_=RES[:, j, :], func=AF.Square), inc=c_eva)
                    sq_done = c_eva.count
                    s.wait("tensor", c_pan[pr], cast_c[j])
                    for q in range(NQ):
                        s.raw("tensor", lambda e, j=j, q=q: e.matmul(
                            PS_S[:, q * 512:(q + 1) * 512], onescol[:],
                            RES[:, j, q * 512:(q + 1) * 512],
                            start=(j == 0), stop=(j == KD - 1)))
                    s.wait("tensor", c_eva, sq_done)
                    for q in range(NQ):
                        fn = lambda e, pr=pr, q=q, j=j: e.matmul(
                            PS_Q[:, q * 512:(q + 1) * 512], onescol[:],
                            SQ[pr][:, q * 512:(q + 1) * 512],
                            start=(j == 0), stop=(j == KD - 1))
                        if q == NQ - 1:
                            s.emit("tensor", fn, inc=c_pe)
                        else:
                            s.raw("tensor", fn)
                    sq_c.append(c_pe.count)
                s.wait("vector", c_pe)
                s.emit("vector", lambda e: e.tensor_copy(out=SR1[:], in_=PS_S[:, :]), inc=c_evv)
                s.emit("vector", lambda e: e.tensor_copy(out=SR2[:], in_=PS_Q[:, :]), inc=c_evv)
                rows_done = c_evv.count
            with ExitStack() as ps_sc:
                PSB = mkpsum(ps_sc, [128, 2048])
                s.wait("tensor", c_evv, rows_done)
                for q in range(NQ):
                    fn = lambda e, q=q: e.matmul(
                        PSB[:, q * 512:(q + 1) * 512], onesrow_f[:],
                        SR1[:, q * 512:(q + 1) * 512], start=True, stop=True)
                    s.emit("tensor", fn, inc=c_pe) if q == NQ - 1 else s.raw("tensor", fn)
                s.wait("scalar", c_pe)
                s.emit("scalar", lambda e: e.activation(
                    out=BT[0][:], in_=PSB[:, :], func=AF.Identity, scale=1.0 / D), inc=c_eva)
                s.wait("tensor", c_eva)
                for q in range(NQ):
                    fn = lambda e, q=q: e.matmul(
                        PSB[:, q * 512:(q + 1) * 512], onesrow_f[:],
                        SR2[:, q * 512:(q + 1) * 512], start=True, stop=True)
                    s.emit("tensor", fn, inc=c_pe) if q == NQ - 1 else s.raw("tensor", fn)
                s.wait("scalar", c_pe)
                s.emit("scalar", lambda e: e.activation(
                    out=BT[1][:], in_=PSB[:, :], func=AF.Identity, scale=1.0 / D), inc=c_eva)
            s.wait("vector", c_eva)
            s.emit("vector", lambda e: e.tensor_mul(out=BTS[:], in0=BT[0][:], in1=BT[0][:]), inc=c_evv)
            s.emit("vector", lambda e: e.tensor_sub(out=BT[1][:], in0=BT[1][:], in1=BTS[:]), inc=c_evv)
            s.wait("scalar", c_evv)
            if mode == "A":
                s.emit("scalar", lambda e: e.activation(
                    out=BT[2][:], in_=BT[1][:], func=AF.Ln, bias=epsc[:, 0:1]), inc=c_eva)
                s.emit("scalar", lambda e: e.activation(
                    out=BT[1][:], in_=BT[1][:], func=AF.Ln), inc=c_eva)
                s.wait("vector", c_eva)
                s.emit("vector", lambda e: e.tensor_sub(out=BT[2][:], in0=BT[1][:], in1=BT[2][:]), inc=c_evv)
                s.wait("scalar", c_evv)
                s.emit("scalar", lambda e: e.activation(
                    out=BT[2][:], in_=BT[2][:], func=AF.Exp, scale=0.5), inc=c_eva)
                s.emit("scalar", lambda e: e.activation(
                    out=BT[1][:], in_=BT[1][:], func=AF.Exp, scale=-0.5), inc=c_eva)
                return BT[0], BT[1], BT[2], BTS
            else:
                s.emit("scalar", lambda e: e.activation(
                    out=BT[1][:], in_=BT[1][:], func=AF.Ln, bias=epsc[:, 0:1]), inc=c_eva)
                s.emit("scalar", lambda e: e.activation(
                    out=BT[1][:], in_=BT[1][:], func=AF.Exp, scale=-0.5), inc=c_eva)
                return BT[0], BT[1], BTS

        def center_res(Bmu, Bscale, scr):
            s.wait("vector", c_eva)
            for j in range(KD):
                s.emit("vector", lambda e, j=j: e.tensor_sub(
                    out=scr[:], in0=RES[:, j, :], in1=Bmu[:]), inc=c_evv)
                s.emit("vector", lambda e, j=j: e.tensor_mul(
                    out=RES[:, j, :], in0=scr[:], in1=Bscale[:]), inc=c_evv)

        def stream_panels(panels, kc, psA, psB, evac_fn, quant=None,
                          pool_extra=None, bf16_src=False, rhs=None, rhs_kc=None,
                          spills=None):
            if rhs is None:
                rhs, rhs_kc = RES, KD
            for qq in ("sync", "gpsimd"):
                s.wait(qq, c_mm[0])
                s.wait(qq, c_mm[1])
                s.wait(qq, c_pan[0])
                s.wait(qq, c_pan[1])
            wid = kc * 128
            n = len(panels)
            cast_c = [None] * n
            mm_c = [None] * n
            free_c = [None] * n
            for i in range(n):
                pr = i % 2
                if bf16_src:
                    if i >= 2:
                        s.wait("sync", c_mm[pr], mm_c[i - 2])
                    s.emit("sync", lambda e, i=i, pr=pr: e.dma_start(
                        out=PAN[pr][:, 0:wid], in_=panels[i]), inc=d_stg[pr], inc_n=16)
                    s.wait("tensor", d_stg[pr])
                else:
                    if i >= 2:
                        s.wait("sync", c_pan[pr], cast_c[i - 2])
                    s.emit("sync", lambda e, i=i, pr=pr: e.dma_start(
                        out=STG[pr][:, 0:wid], in_=panels[i]), inc=d_stg[pr], inc_n=16)
                    ceng = ("scalar" if pr == 0 else "vector") if quant is None else "vector"
                    s.wait(ceng, d_stg[pr])
                    if i >= 2:
                        s.wait(ceng, c_mm[pr], mm_c[i - 2])
                    if spills is not None and i >= 2 and spills[i - 2] is not None:
                        s.wait(ceng, d_wq)
                    if quant is None:
                        if ceng == "scalar":
                            s.emit("scalar", lambda e, pr=pr: e.activation(
                                out=PAN[pr][:, 0:wid], in_=STG[pr][:, 0:wid],
                                func=AF.Identity), inc=c_pan[pr])
                        else:
                            s.emit("vector", lambda e, pr=pr: e.tensor_copy(
                                out=PAN[pr][:, 0:wid], in_=STG[pr][:, 0:wid]), inc=c_pan[pr])
                    else:
                        m, QA, QB = quant
                        s.raw("vector", lambda e, pr=pr: e.tensor_scalar(
                            out=QA[:, 0:wid], in0=STG[pr][:, 0:wid],
                            scalar1=thrv[:, m:m + 1], scalar2=1.0,
                            op0=ALU.is_gt, op1=ALU.mult))
                        s.raw("vector", lambda e, pr=pr: e.tensor_scalar(
                            out=QB[:, 0:wid], in0=STG[pr][:, 0:wid],
                            scalar1=nthrv[:, m:m + 1], scalar2=1.0,
                            op0=ALU.is_ge, op1=ALU.mult))
                        s.emit("vector", lambda e, pr=pr: e.scalar_tensor_tensor(
                            out=PAN[pr][:, 0:wid], in0=QA[:, 0:wid], scalar=-1.0,
                            in1=QB[:, 0:wid], op0=ALU.add, op1=ALU.add), inc=c_pan[pr])
                    cast_c[i] = c_pan[pr].count
                    if spills is not None and spills[i] is not None:
                        s.wait("scalar", c_pan[pr], cast_c[i])
                        s.emit("scalar", lambda e, i=i, pr=pr: e.dma_start(
                            out=spills[i], in_=PAN[pr][:, 0:wid]), inc=d_wq, inc_n=16)
                    s.wait("tensor", c_pan[pr], cast_c[i])
                if pool_extra is not None:
                    pool_extra(i)
                if i >= 2:
                    s.wait("tensor", *free_c[i - 2])
                ps = psA if pr == 0 else psB
                for t in range(NQ):
                    for k in range(kc):
                        fn = lambda e, ps=ps, pr=pr, k=k, t=t: e.matmul(
                            ps[:, t * 512:(t + 1) * 512],
                            PAN[pr][:, k * 128:(k + 1) * 128],
                            rhs[:, k, t * 512:(t + 1) * 512],
                            start=(k == 0), stop=(k == kc - 1))
                        if t == NQ - 1 and k == kc - 1:
                            s.emit("tensor", fn, inc=c_mm[pr])
                        else:
                            s.raw("tensor", fn)
                mm_c[i] = c_mm[pr].count
                free_c[i] = evac_fn(i, ps, pr)

        def acc_stage(qg_panels, kc_g, ngrp, grp_src_fn, ACC, GRP, evac_q_fn,
                      bf16_pan=False, side_fn=None, deep=True):
            wid = kc_g * 128
            PQ = None
            with ExitStack() as ps_sc:
                PQ = [mkpsum(ps_sc, [128, 512]) for _ in range(2)]
                if bf16_pan:
                    if deep:
                        PANX = [mktile(ps_sc, [128, wid], BF16) for _ in range(2)]
                        pbufs = [PAN[0], PAN[1], PANX[0], PANX[1]]
                        psems = [d_stg[0], d_stg[1], d_stg4[0], d_stg4[1]]
                    else:
                        pbufs = [PAN[0], PAN[1]]
                        psems = [d_stg[0], d_stg[1]]
                    npb = len(pbufs)
                pi = 0
                grp_mm = {}
                pan_cast = {}
                pan_mm = {}
                ps_free = {}
                for q in range(NQ):
                    for g in range(ngrp):
                        gpr = g % 2
                        prev = (q, g - 2) if g >= 2 else ((q - 1, ngrp - 2 + g) if q >= 1 else None)
                        if prev in grp_mm:
                            s.wait("sync", c_mm[0], grp_mm[prev])
                        s.emit("sync", lambda e, q=q, g=g, gpr=gpr: e.dma_start(
                            out=GRP[gpr][:], in_=grp_src_fn(q, g)), inc=d_grp[gpr], inc_n=16)
                        s.wait("tensor", d_grp[gpr])
                        for f in range(16):
                            pr = pi % 2
                            if bf16_pan:
                                pb = pi % npb
                                if pi >= npb:
                                    s.wait("sync", c_mm[0], pan_mm.get(pi - npb, 0))
                                s.emit("sync", lambda e, q=q, g=g, f=f, pb=pb: e.dma_start(
                                    out=pbufs[pb][:, 0:wid], in_=qg_panels[q][g][f]),
                                    inc=psems[pb], inc_n=16)
                                s.wait("tensor", psems[pb])
                            else:
                                if pi >= 2:
                                    s.wait("sync", c_pan[pr], pan_cast.get(pi - 2, 0))
                                s.emit("sync", lambda e, q=q, g=g, f=f, pr=pr: e.dma_start(
                                    out=STG[pr][:, 0:wid], in_=qg_panels[q][g][f]),
                                    inc=d_stg[pr], inc_n=16)
                                s.wait("gpsimd", d_stg[pr])
                                if pi >= 2:
                                    s.wait("gpsimd", c_mm[0], pan_mm.get(pi - 2, 0))
                                s.emit("gpsimd", lambda e, pr=pr: e.tensor_copy(
                                    out=PAN[pr][:, 0:wid], in_=STG[pr][:, 0:wid]),
                                    inc=c_pan[pr])
                                pan_cast[pi] = c_pan[pr].count
                                s.wait("tensor", c_pan[pr], pan_cast[pi])
                            if pi >= 2:
                                s.wait("tensor", c_evv, ps_free.get(pi - 2, 0))
                            ps = PQ[pi % 2]
                            pbuf = pbufs[pi % npb] if bf16_pan else PAN[pr]
                            for k in range(kc_g):
                                fn = lambda e, ps=ps, pbuf=pbuf, k=k, gpr=gpr: e.matmul(
                                    ps[:, :], pbuf[:, k * 128:(k + 1) * 128],
                                    GRP[gpr][:, k, :],
                                    start=(k == 0), stop=(k == kc_g - 1))
                                if k == kc_g - 1:
                                    s.emit("tensor", fn, inc=c_mm[0])
                                else:
                                    s.raw("tensor", fn)
                            pan_mm[pi] = c_mm[0].count
                            if side_fn is not None:
                                side_fn(pi)
                            s.wait("vector", c_mm[0], pan_mm[pi])
                            if g == 0:
                                if q > 0:
                                    s.wait("vector", d_acc)
                                s.emit("vector", lambda e, f=f, ps=ps: e.tensor_copy(
                                    out=ACC[:, f, :], in_=ps[:, :]), inc=c_evv)
                            else:
                                s.emit("vector", lambda e, f=f, ps=ps: e.tensor_add(
                                    out=ACC[:, f, :], in0=ACC[:, f, :], in1=ps[:, :]), inc=c_evv)
                            ps_free[pi] = c_evv.count
                            pi += 1
                        grp_mm[(q, g)] = c_mm[0].count
                    evac_q_fn(q)

        def bcast_const_row(scope, src_row_ap, out_tile, scale=1.0, extra=None):
            with ExitStack() as ps_sc:
                PSB = mkpsum(ps_sc, [128, 512])
                s.wait("tensor", c_evv)
                s.wait("tensor", c_eva)
                s.emit("tensor", lambda e: e.matmul(
                    PSB[:, 0:src_row_ap.shape[-1]], onesrow_f[:], src_row_ap,
                    start=True, stop=True), inc=c_pe)
                s.wait("scalar", c_pe)
                s.emit("scalar", lambda e: e.activation(
                    out=out_tile, in_=PSB[:, 0:src_row_ap.shape[-1]],
                    func=AF.Identity, scale=scale), inc=c_eva)

        side_jobs = ([("c", wout_t[g, f], woutbf[g, f], 2048)
                      for g in range(2) for f in range(16)]
                     + [("g", w1_t[i], 2048, 0) for i in range(KF)]
                     + [("g", w2_t[i], 2048, 1) for i in range(KF)]
                     + [("g", w3_t[g, f], 1408, 2) for g in range(4) for f in range(16)]
                     + [("c", wm2_t[g, f], wm2bf[g, f], 2048)
                        for g in range(4) for f in range(16)])
        N_WOUT_JOBS = 32
        N_GAMMA_END = 32 + 152
        side_i = [0]
        side_done = {}

        def side_pump(GSTb, SBTb, n=1):
            for _ in range(n):
                gi = side_i[0]
                if gi >= len(side_jobs):
                    return
                side_i[0] += 1
                job = side_jobs[gi]
                gpr = gi % 2
                if gi >= 2:
                    s.wait("sync", c_g, side_done[gi - 2])
                src = job[1]
                wid = job[2] if job[0] == "g" else job[3]
                s.emit("sync", lambda e, src=src, wid=wid, gpr=gpr: e.dma_start(
                    out=GSTb[gpr][:, 0:wid], in_=src), inc=d_g[gpr], inc_n=16)
                if job[0] == "g":
                    m = job[3]
                    s.wait("vector", d_g[gpr])
                    qw = wid // 4
                    for qq in range(4):
                        s.emit("vector", lambda e, qq=qq, qw=qw, gpr=gpr, GSTb=GSTb: e.tensor_reduce(
                            out=gtmp[:], in_=GSTb[gpr][:, qq * qw:(qq + 1) * qw],
                            axis=AX.X, op=ALU.add, apply_absolute_value=True), inc=c_evv)
                        s.wait("vector", c_evv)
                        fn = lambda e, m=m: e.tensor_add(
                            out=gacc[:, m:m + 1], in0=gacc[:, m:m + 1], in1=gtmp[:])
                        if qq == 3:
                            s.emit("vector", fn, inc=c_g)
                        else:
                            s.emit("vector", fn, inc=c_evv)
                        s.wait("vector", c_evv if qq < 3 else c_g)
                else:
                    dst = job[2]
                    ceng = "scalar" if gpr == 0 else "vector"
                    s.wait(ceng, d_g[gpr])
                    s.wait(ceng, d_side)
                    if ceng == "scalar":
                        s.emit("scalar", lambda e, gpr=gpr, GSTb=GSTb, SBTb=SBTb: e.activation(
                            out=SBTb[gpr][:], in_=GSTb[gpr][:], func=AF.Identity), inc=c_g)
                        s.wait("scalar", c_g)
                        s.emit("scalar", lambda e, dst=dst, gpr=gpr, SBTb=SBTb: e.dma_start(
                            out=dst, in_=SBTb[gpr][:]), inc=d_side, inc_n=16)
                    else:
                        s.emit("vector", lambda e, gpr=gpr, GSTb=GSTb, SBTb=SBTb: e.tensor_copy(
                            out=SBTb[gpr][:], in_=GSTb[gpr][:]), inc=c_g)
                        s.wait("scalar", c_g)
                        s.emit("scalar", lambda e, dst=dst, gpr=gpr, SBTb=SBTb: e.dma_start(
                            out=dst, in_=SBTb[gpr][:]), inc=d_side, inc_n=16)
                side_done[gi] = c_g.count

        xt_r = xt.rearrange("(j p) t -> p j t", p=128)
        with ExitStack() as scA:
            Bsig0 = mktile(scA, [128, T], BF16)
            Brho2 = mktile(scA, [128, T], BF16)

            with ExitStack() as sc:
                Bmu, Brho, Bsig, scr = load_res_stats(
                    sc, [xt_r[:, j, :] for j in range(KD)], "A")
                s.wait("vector", c_eva)
                s.emit("vector", lambda e: e.tensor_copy(out=Bsig0[:], in_=Bsig[:]), inc=c_evv)
                center_res(Bmu, Brho, scr)

            s.fence()
            with ExitStack() as sc:
                XPS = [mktile(sc, [128, 3 + T], F32) for _ in range(2)]
                XC = mktile(sc, [128, T], F32)
                XCS = mktile(sc, [128, T], BF16)
                ZSIL = mktile(sc, [128, T], BF16)
                GST = [mktile(sc, [128, 2048], F32) for _ in range(2)]
                SBT = [mktile(sc, [128, 2048], BF16) for _ in range(2)]
                PSA = mkpsum(sc, [128, 2048])
                PSB = mkpsum(sc, [128, 2048])
                s.wait("gpsimd", c_pool)
                s.emit("gpsimd", lambda e: e.memset(XPS[0][:, 0:3], 0.0), inc=c_pool)
                s.emit("gpsimd", lambda e: e.memset(XPS[1][:, 0:3], 0.0), inc=c_pool)

                def pool_extra(i):
                    side_pump(GST, SBT, 2 if i % 2 == 0 else 1)

                state = {"xc_sil": 0, "ytt": 0, "xpcopy": {}, "zsil": 0}

                def evac_a1(i, ps, pr):
                    pair = i // 2
                    j = pair
                    if pr == 0:
                        s.wait("scalar", c_mm[0])
                        if pair >= 2:
                            s.wait("scalar", c_evv, state["xpcopy"].get(pair - 2, (0, 0))[1])
                        s.emit("scalar", lambda e, pair=pair: e.activation(
                            out=XPS[pair % 2][:, 3:3 + T], in_=ps[:, :],
                            func=AF.Identity), inc=c_eva)
                        free = (c_eva, c_eva.count)
                        s.wait("vector", c_eva)
                        if pair >= 1:
                            s.wait("vector", c_eva, state["xc_sil"])
                        s.raw("vector", lambda e, pair=pair, j=j: e.tensor_scalar(
                            out=XC[:], in0=XPS[pair % 2][:, 0:T],
                            scalar1=convw[:, j, 0:1], scalar2=0.0,
                            op0=ALU.mult, op1=ALU.add))
                        for k in range(1, 3):
                            s.raw("vector", lambda e, pair=pair, j=j, k=k: e.scalar_tensor_tensor(
                                out=XC[:], in0=XPS[pair % 2][:, k:k + T],
                                scalar=convw[:, j, k:k + 1], in1=XC[:],
                                op0=ALU.mult, op1=ALU.add))
                        s.emit("vector", lambda e, pair=pair, j=j: e.scalar_tensor_tensor(
                            out=XC[:], in0=XPS[pair % 2][:, 3:3 + T],
                            scalar=convw[:, j, 3:4], in1=XC[:],
                            op0=ALU.mult, op1=ALU.add), inc=c_evv)
                        state["xpcopy"][pair] = (c_evv, c_evv.count)
                        s.wait("scalar", c_evv)
                        if pair >= 1:
                            s.wait("scalar", c_evv, state["ytt"])
                        s.emit("scalar", lambda e, j=j: e.activation(
                            out=XCS[:], in_=XC[:], func=AF.Silu,
                            bias=convb[:, j:j + 1]), inc=c_eva)
                        state["xc_sil"] = c_eva.count
                        return free
                    else:
                        s.wait("scalar", c_mm[1])
                        if pair >= 1:
                            s.wait("scalar", c_evv, state["ytt"])
                        s.emit("scalar", lambda e: e.activation(
                            out=ZSIL[:], in_=ps[:, :], func=AF.Silu), inc=c_eva)
                        free = (c_eva, c_eva.count)
                        s.wait("vector", c_eva)
                        s.wait("vector", d_ev)
                        s.emit("vector", lambda e, pair=pair: e.tensor_mul(
                            out=EVB[pair % 2][:], in0=XCS[:], in1=ZSIL[:]), inc=c_evv)
                        state["ytt"] = c_evv.count
                        s.wait("scalar", c_evv, state["ytt"])
                        s.emit("scalar", lambda e, j=j, pair=pair: e.dma_start(
                            out=ysp[j], in_=EVB[pair % 2][:]), inc=d_ev, inc_n=16)
                        return free

                order = []
                for j in range(32):
                    order.append(win_t[j])
                    order.append(win_t[j + 32])
                stream_panels(order, 16, PSA, PSB, evac_a1, pool_extra=pool_extra)

            s.fence()
            with ExitStack() as sc:
                GRP = [mktile(sc, [128, 16, 512], BF16) for _ in range(2)]
                ACC = mktile(sc, [128, 16, 512], F32)
                T1 = mktile(sc, [128, 512], BF16)
                SQ2 = T1
                SR = mktile(sc, [1, T], F32)
                rho2_row_done = [0]

                ps_q2 = mkpsum(sc, [1, T])

                def grp_src(q, g):
                    return ysp[g * 16:(g + 1) * 16, :, q * 512:(q + 1) * 512].rearrange(
                        "c p t -> p c t")

                def evac_a2(q):
                    s.wait("vector", c_evv)
                    for j in range(KD):
                        s.wait("vector", c_pe)
                        s.emit("vector", lambda e, j=j, q=q, T1=T1: e.tensor_mul(
                            out=T1[:], in0=RES[:, j, q * 512:(q + 1) * 512],
                            in1=Bsig0[:, q * 512:(q + 1) * 512]), inc=c_evv)
                        s.emit("vector", lambda e, j=j, q=q, T1=T1, ACC=ACC: e.tensor_add(
                            out=RES[:, j, q * 512:(q + 1) * 512], in0=T1[:],
                            in1=ACC[:, j, :]), inc=c_evv)
                        x1_done = c_evv.count
                        s.wait("scalar", c_evv, x1_done)
                        if j >= 1 or q >= 1:
                            s.wait("scalar", c_pe)
                        s.emit("scalar", lambda e, j=j, q=q, SQ2=SQ2: e.activation(
                            out=SQ2[:], in_=RES[:, j, q * 512:(q + 1) * 512],
                            func=AF.Square), inc=c_eva)
                        s.wait("tensor", c_eva)
                        fn = lambda e, j=j, q=q, SQ2=SQ2, ps_q2=ps_q2: e.matmul(
                            ps_q2[:, q * 512:(q + 1) * 512], onescol[:], SQ2[:],
                            start=(j == 0), stop=(j == KD - 1))
                        s.emit("tensor", fn, inc=c_pe)

                GST2 = [mktile(sc, [128, 2048], F32) for _ in range(2)]
                _sbt2 = mktile(sc, [128, 2048], BF16)
                SBT2 = [_sbt2, _sbt2]

                def side_a2(pi):
                    if pi % 8 < 3:
                        side_pump(GST2, SBT2, 1)

                panels = [[[woutbf[g, f] for f in range(16)] for g in range(2)]
                          for q in range(NQ)]
                acc_stage(panels, 16, 2, grp_src, ACC, GRP, evac_a2,
                          bf16_pan=True, side_fn=side_a2, deep=False)
                s.wait("vector", c_pe)
                s.emit("vector", lambda e, SR=SR, ps_q2=ps_q2: e.tensor_copy(out=SR[:], in_=ps_q2[:, :]), inc=c_evv)
                with ExitStack() as ps_sc:
                    PSB2 = mkpsum(ps_sc, [128, 2048])
                    s.wait("tensor", c_evv)
                    for q in range(NQ):
                        fn = lambda e, q=q, PSB2=PSB2, SR=SR: e.matmul(
                            PSB2[:, q * 512:(q + 1) * 512], onesrow_f[:],
                            SR[:, q * 512:(q + 1) * 512], start=True, stop=True)
                        s.emit("tensor", fn, inc=c_pe) if q == NQ - 1 else s.raw("tensor", fn)
                    s.wait("scalar", c_pe)
                    s.emit("scalar", lambda e, PSB2=PSB2: e.activation(
                        out=Brho2[:], in_=PSB2[:, :], func=AF.Ln), inc=c_eva)
                    s.emit("scalar", lambda e: e.activation(
                        out=Brho2[:], in_=Brho2[:], func=AF.Exp, scale=-0.5), inc=c_eva)
                    s.emit("scalar", lambda e: e.activation(
                        out=Brho2[:], in_=Brho2[:], func=AF.Identity,
                        scale=float(np.sqrt(D))), inc=c_eva)

            s.fence()

            with ExitStack() as sc:
                PSA = mkpsum(sc, [128, 2048])
                PSB = mkpsum(sc, [128, 2048])

                GST3 = [mktile(sc, [128, 2048], F32) for _ in range(2)]
                SBT3 = [mktile(sc, [128, 2048], BF16) for _ in range(2)]

                def pool_extra_a3(i):
                    side_pump(GST3, SBT3, 2)

                def evac_a3(i, ps, pr):
                    s.wait("scalar", c_mm[pr])
                    s.wait("scalar", d_ev)
                    s.emit("scalar", lambda e, i=i, ps=ps: e.activation(
                        out=EVB[i % 2][:], in_=ps[:, :], func=AF.Relu), inc=c_eva)
                    free = (c_eva, c_eva.count)
                    s.wait("vector", c_eva)
                    s.emit("vector", lambda e, i=i: e.tensor_mul(
                        out=EVB[i % 2][:], in0=EVB[i % 2][:], in1=Brho2[:]), inc=c_evv)
                    s.wait("scalar", c_evv)
                    s.emit("scalar", lambda e, i=i: e.dma_start(
                        out=usp[i], in_=EVB[i % 2][:]), inc=d_ev, inc_n=16)
                    return free

                stream_panels([wm1_t[i] for i in range(64)], 16, PSA, PSB, evac_a3,
                              pool_extra=pool_extra_a3)

            s.fence()
            with ExitStack() as sc:
                GRP = [mktile(sc, [128, 16, 512], BF16) for _ in range(2)]
                ACC = mktile(sc, [128, 16, 512], F32)
                XB = mktile(sc, [128, 4, 512], F32)

                def grp_src(q, g):
                    return usp[g * 16:(g + 1) * 16, :, q * 512:(q + 1) * 512].rearrange(
                        "c p t -> p c t")

                def evac_a4(q):
                    add_c = {}
                    for j in range(KD):
                        xslot = j % 4
                        if j >= 4:
                            s.wait("sync", c_evv, add_c[j - 4])
                        s.emit("sync", lambda e, j=j, q=q, xslot=xslot, XB=XB: e.dma_start(
                            out=XB[:, xslot, :], in_=xt_r[:, j, q * 512:(q + 1) * 512]),
                            inc=d_xb4[xslot], inc_n=16)
                        s.wait("vector", d_xb4[xslot])
                        s.emit("vector", lambda e, j=j, xslot=xslot, ACC=ACC, XB=XB: e.tensor_add(
                            out=ACC[:, j, :], in0=ACC[:, j, :], in1=XB[:, xslot, :]),
                            inc=c_evv)
                        s.emit("vector", lambda e, j=j, q=q, ACC=ACC: e.tensor_add(
                            out=ACC[:, j, :], in0=ACC[:, j, :],
                            in1=RES[:, j, q * 512:(q + 1) * 512]), inc=c_evv)
                        add_c[j] = c_evv.count
                        s.wait("scalar", c_evv, add_c[j])
                        s.emit("scalar", lambda e, j=j, q=q, ACC=ACC: e.dma_start(
                            out=Rm[j][:, q * 512:(q + 1) * 512], in_=ACC[:, j, :]),
                            inc=d_acc, inc_n=16)

                assert side_i[0] >= 0
                panels = [[[wm2bf[g, f] for f in range(16)] for g in range(4)]
                          for q in range(NQ)]
                acc_stage(panels, 16, 4, grp_src, ACC, GRP, evac_a4, bf16_pan=True)

        s.fence()
        if stop_after == "A":
            emit_tail(nc, s, Rm, outT, STG)
            s.build()
            return nc

        with ExitStack() as sc:
            SG = mktile(sc, [1, 512], F32)
            with ExitStack() as ps_sc:
                PSG = mkpsum(ps_sc, [1, 512])
                onescol_f = mktile(sc, [128, 1], F32)
                s.emit("sync", lambda e: e.dma_start(
                    out=onescol_f[:], in_=onescolf_c.ap()), inc=d_misc, inc_n=16)
                s.wait("tensor", d_misc)
                s.wait("tensor", c_g)
                s.emit("tensor", lambda e: e.matmul(
                    PSG[:, 0:4], onescol_f[:], gacc[:], start=True, stop=True), inc=c_pe)
                s.wait("scalar", c_pe)
                s.emit("scalar", lambda e: e.activation(
                    out=SG[:, 0:4], in_=PSG[:, 0:4], func=AF.Identity,
                    scale=0.5 / FFN_CNT, bias=epsc[0:1, 1:2]), inc=c_eva)
            with ExitStack() as ps_sc:
                PSG2 = mkpsum(ps_sc, [128, 512])
                s.wait("tensor", c_eva)
                s.emit("tensor", lambda e: e.matmul(
                    PSG2[:, 0:4], onesrow_f[:], SG[:, 0:4], start=True, stop=True), inc=c_pe)
                s.wait("scalar", c_pe)
                s.emit("scalar", lambda e: e.activation(
                    out=thrv[:], in_=PSG2[:, 0:4], func=AF.Identity), inc=c_eva)
                s.emit("scalar", lambda e: e.activation(
                    out=nthrv[:], in_=PSG2[:, 0:4], func=AF.Identity, scale=-1.0), inc=c_eva)
        s.fence()

        def emit_ffn(app, final_to_out):
            with ExitStack() as scF:
                with ExitStack() as sc:
                    Bmu, Bs, scr = load_res_stats(
                        sc, [Rm[j] for j in range(KD)], "LN")
                    center_res(Bmu, Bs, scr)
                    if hsp is not None and app == 0:
                        s.wait("sync", c_evv)
                        for j in range(KD):
                            s.emit("sync", lambda e, j=j: e.dma_start(
                                out=hsp[j], in_=RES[:, j, :]), inc=d_stv, inc_n=16)
                s.fence()

                with ExitStack() as sc:
                    PSA = mkpsum(sc, [128, 2048])
                    PSB = mkpsum(sc, [128, 2048])
                    if app == 0:
                        QA = mktile(sc, [128, 2048], F32)
                        QB = mktile(sc, [128, 2048], F32)
                        WQS = [mktile(sc, [128, 1408], F32) for _ in range(2)]
                        WQB = mktile(sc, [128, 1408], BF16)
                        w3cur = [0]
                        w3q_done = []

                        def pool_extra_w3(i):
                            gi = w3cur[0]
                            if gi >= 64:
                                return
                            w3cur[0] += 1
                            gpr = gi % 2
                            g, f = gi // 16, gi % 16
                            if gi >= 2:
                                s.wait("sync", c_g, w3q_done[gi - 2])
                            s.emit("sync", lambda e, g=g, f=f, gpr=gpr: e.dma_start(
                                out=WQS[gpr][:], in_=w3_t[g, f]), inc=d_g[gpr], inc_n=16)
                            s.wait("vector", d_g[gpr])
                            s.wait("vector", d_stv)
                            s.raw("vector", lambda e, gpr=gpr: e.tensor_scalar(
                                out=QA[:, 0:1408], in0=WQS[gpr][:],
                                scalar1=thrv[:, 2:3], scalar2=1.0,
                                op0=ALU.is_gt, op1=ALU.mult))
                            s.raw("vector", lambda e, gpr=gpr: e.tensor_scalar(
                                out=QB[:, 0:1408], in0=WQS[gpr][:],
                                scalar1=nthrv[:, 2:3], scalar2=1.0,
                                op0=ALU.is_ge, op1=ALU.mult))
                            s.emit("vector", lambda e: e.scalar_tensor_tensor(
                                out=WQB[:], in0=QA[:, 0:1408], scalar=-1.0,
                                in1=QB[:, 0:1408], op0=ALU.add, op1=ALU.add), inc=c_g)
                            w3q_done.append(c_g.count)
                            s.wait("sync", c_g, w3q_done[-1])
                            s.emit("sync", lambda e, g=g, f=f: e.dma_start(
                                out=wq3sp[g, f], in_=WQB[:]), inc=d_stv, inc_n=16)
                    else:
                        QA = QB = None
                        pool_extra_w3 = None

                    def evac_f1a(i, ps, pr):
                        s.wait("scalar", c_mm[pr])
                        s.wait("scalar", d_sta)
                        s.wait("scalar", d_ev)
                        s.emit("scalar", lambda e, i=i, ps=ps: e.activation(
                            out=EVB[i % 2][:], in_=ps[:, :], func=AF.Sigmoid), inc=c_eva)
                        free = (c_eva, c_eva.count)
                        s.wait("scalar", c_eva)
                        s.emit("scalar", lambda e, i=i: e.dma_start(
                            out=sgsp[i], in_=EVB[i % 2][:]), inc=d_sta, inc_n=16)
                        return free

                    if app == 0:
                        stream_panels([w1_t[i] for i in range(KF)], 16, PSA, PSB,
                                      evac_f1a, quant=(0, QA, QB),
                                      pool_extra=pool_extra_w3,
                                      spills=[wq1sp[i] for i in range(KF)])
                    else:
                        stream_panels([wq1sp[i] for i in range(KF)], 16, PSA, PSB,
                                      evac_f1a, bf16_src=True)

                    SGB = [mktile(sc, [128, 2048], BF16) for _ in range(2)]
                    sg_ld = {}

                    def evac_f1b(i, ps, pr):
                        s.wait("sync", c_evv)
                        s.emit("sync", lambda e, i=i: e.dma_start(
                            out=SGB[i % 2][:], in_=sgsp[i]), inc=d_grp[i % 2], inc_n=16)
                        s.wait("vector", d_grp[i % 2])
                        s.wait("vector", c_mm[pr])
                        s.wait("vector", d_ev)
                        s.wait("vector", d_sta)
                        s.emit("vector", lambda e, i=i, ps=ps: e.tensor_mul(
                            out=EVB[i % 2][:], in0=SGB[i % 2][:], in1=ps[:, :]), inc=c_evv)
                        free = (c_evv, c_evv.count)
                        s.wait("scalar", c_evv)
                        s.emit("scalar", lambda e, i=i: e.dma_start(
                            out=hidsp[i], in_=EVB[i % 2][:]), inc=d_ev, inc_n=16)
                        return free

                    if app == 0:
                        stream_panels([w2_t[i] for i in range(KF)], 16, PSA, PSB,
                                      evac_f1b, quant=(1, QA, QB),
                                      pool_extra=pool_extra_w3,
                                      spills=[wq2sp[i] for i in range(KF)])
                    else:
                        stream_panels([wq2sp[i] for i in range(KF)], 16, PSA, PSB,
                                      evac_f1b, bf16_src=True)
                s.fence()

                with ExitStack() as sc:
                    GRP = [mktile(sc, [128, 11, 512], BF16) for _ in range(2)]
                    ACC = mktile(sc, [128, 16, 512], F32)
                    RB = mktile(sc, [128, 4, 512], F32)

                    def grp_src(q, g):
                        return hidsp[g * 11:(g + 1) * 11, :, q * 512:(q + 1) * 512].rearrange(
                            "c p t -> p c t")

                    def evac_f2(q):
                        add_c = {}
                        for j in range(KD):
                            xslot = j % 4
                            if j >= 4:
                                s.wait("sync", c_evv, add_c[j - 4])
                            s.emit("sync", lambda e, j=j, q=q, xslot=xslot: e.dma_start(
                                out=RB[:, xslot, :], in_=Rm[j][:, q * 512:(q + 1) * 512]),
                                inc=d_xb4[xslot], inc_n=16)
                            s.wait("vector", d_xb4[xslot])
                            s.emit("vector", lambda e, j=j, xslot=xslot: e.tensor_add(
                                out=ACC[:, j, :], in0=ACC[:, j, :], in1=RB[:, xslot, :]),
                                inc=c_evv)
                            add_c[j] = c_evv.count
                            s.wait("scalar", c_evv, add_c[j])
                            dst = outT if final_to_out else Rm
                            s.emit("scalar", lambda e, j=j, q=q, dst=dst: e.dma_start(
                                out=dst[j][:, q * 512:(q + 1) * 512], in_=ACC[:, j, :]),
                                inc=d_acc, inc_n=16)

                    panels = [[[wq3sp[g, f] for f in range(16)] for g in range(4)]
                              for q in range(NQ)]
                    acc_stage(panels, 11, 4, grp_src, ACC, GRP, evac_f2, bf16_pan=True)
            s.fence()

        emit_ffn(0, False)
        if stop_after == "F1":
            emit_tail(nc, s, Rm, outT, STG)
            s.build()
            return nc

        with ExitStack() as scB:
            ckv = mktile(scB, [128, 4, T], BF16)
            cq = mktile(scB, [128, 8, T], BF16)

            with ExitStack() as sc:
                Bmu, Bs, scr = load_res_stats(sc, [Rm[j] for j in range(KD)], "LN")
                center_res(Bmu, Bs, scr)
            s.fence()

            with ExitStack() as sc:
                PSA = mkpsum(sc, [128, 2048])
                PSB = mkpsum(sc, [128, 2048])

                def evac_ckv(i, ps, pr):
                    s.wait("scalar", c_mm[pr])
                    s.emit("scalar", lambda e, i=i, ps=ps: e.activation(
                        out=ckv[:, i, :], in_=ps[:, :], func=AF.Identity,
                        bias=bdkv[:, i:i + 1]), inc=c_eva)
                    return (c_eva, c_eva.count)

                def evac_cq(i, ps, pr):
                    s.wait("scalar", c_mm[pr])
                    s.emit("scalar", lambda e, i=i, ps=ps: e.activation(
                        out=cq[:, i, :], in_=ps[:, :], func=AF.Identity,
                        bias=bdq[:, i:i + 1]), inc=c_eva)
                    return (c_eva, c_eva.count)

                stream_panels([wdkv_t[i] for i in range(4)], 16, PSA, PSB, evac_ckv)
                s.wait("tensor", c_eva)
                stream_panels([wdq_t[i] for i in range(8)], 16, PSA, PSB, evac_cq)
            s.fence()

            with ExitStack() as sc:
                kCb = mktile(sc, [128, T], BF16)
                kRb = mktile(sc, [128, T], BF16)
                qCb = mktile(sc, [128, T], BF16)
                krp = qCb
                qRb = mktile(sc, [128, T], BF16)
                Vtmb = mktile(sc, [128, 16, 132], BF16)
                ptile_full = mktile(sc, [128, T], BF16)
                pTs_full = mktile(sc, [128, 1024], BF16)
                otm = mktile(sc, [128, 128], BF16)
                ropeC = mktile(sc, [128, T], BF16)
                ropeS = mktile(sc, [128, T], BF16)
                tril = mktile(sc, [128, 128], F32)
                rows = mktile(sc, [128, 8], F32)
                HP = [mktile(sc, [128, 4096], BF16) for _ in range(2)]

                dbgt = None
                PBLD = mkpsum(sc, [128, 1024])
                PSF = mkpsum(sc, [128, 2048])
                PO = mkpsum(sc, [128, 512])
                PT = mkpsum(sc, [128, 1024], BF16)

                s.emit("sync", lambda e: e.dma_start(out=ropeC[:], in_=ropeC_c.ap()),
                       inc=d_misc, inc_n=16)
                s.emit("sync", lambda e: e.dma_start(out=ropeS[:], in_=ropeS_c.ap()),
                       inc=d_misc, inc_n=16)
                s.emit("sync", lambda e: e.dma_start(out=tril[:], in_=tril_c.ap()),
                       inc=d_misc, inc_n=16)
                s.wait("vector", d_misc)
                s.wait("tensor", d_misc)

                hp_cast = [0, 0]
                build_done = {}

                def load_head(h, prev_attend_pe):
                    pr = h % 2
                    if h >= 2:
                        s.wait("sync", c_pe, build_done[h - 2])
                    if h >= 1:
                        s.wait("sync", c_pool, hp_cast[1 - pr])
                    for (dst0, src) in ((0, wuk_t[h]), (512, wuv_t[h]), (1024, wuq_t[h])):
                        wdt = src.shape[-1]
                        s.emit("sync", lambda e, dst0=dst0, src=src, wdt=wdt: e.dma_start(
                            out=STG[0][:, dst0:dst0 + wdt], in_=src), inc=d_stg[0], inc_n=16)
                    s.emit("sync", lambda e, h=h: e.dma_start(
                        out=STG[1][:], in_=wkr_t[h]), inc=d_stg[1], inc_n=16)
                    s.wait("gpsimd", d_stg[0])
                    s.wait("gpsimd", d_stg[1])
                    if h >= 2:
                        s.wait("gpsimd", c_pe, build_done[h - 2])
                    s.raw("gpsimd", lambda e, pr=pr: e.tensor_copy(
                        out=HP[pr][:, 0:2048], in_=STG[0][:]))
                    s.emit("gpsimd", lambda e, pr=pr: e.tensor_copy(
                        out=HP[pr][:, 2048:4096], in_=STG[1][:]), inc=c_pool)
                    hp_cast[pr] = c_pool.count

                def build_head(h, prev_attend_pe, prev_ropeq_evv):
                    pr = h % 2
                    W = HP[pr]
                    s.wait("tensor", c_pool, hp_cast[pr])
                    if prev_ropeq_evv:
                        s.wait("tensor", c_evv, prev_ropeq_evv)
                    if prev_attend_pe:
                        s.wait("scalar", c_pe, prev_attend_pe)
                        s.wait("vector", c_pe, prev_attend_pe)

                    def proj(dstbuf, w0, nc_chunks, rhs, rhs_kc0, biascol, scale=1.0,
                             rows16=False):
                        evs = []
                        for t in range(NQ):
                            half = (t % 2) * 512
                            if t >= 2:
                                s.wait("tensor", c_eva, evs[t - 2])
                            for c in range(nc_chunks):
                                fn = lambda e, t=t, c=c, half=half, w0=w0, rhs=rhs: e.matmul(
                                    PBLD[:, half:half + 512],
                                    W[:, w0 + c * 128:w0 + (c + 1) * 128],
                                    rhs[:, c, t * 512:(t + 1) * 512],
                                    start=(c == 0), stop=(c == nc_chunks - 1))
                                if c == nc_chunks - 1:
                                    s.emit("tensor", fn, inc=c_pe)
                                else:
                                    s.raw("tensor", fn)
                            s.wait("scalar", c_pe)
                            s.emit("scalar", lambda e, t=t, half=half, dstbuf=dstbuf,
                                   biascol=biascol, scale=scale: e.activation(
                                out=dstbuf[:, t * 512:(t + 1) * 512],
                                in_=PBLD[:, half:half + 512], func=AF.Identity,
                                scale=scale, bias=biascol), inc=c_eva)
                            evs.append(c_eva.count)
                        return evs

                    def rope(srcbuf, dstbuf, scratch):
                        s.wait("tensor", c_eva)
                        done = 0
                        for t in range(NQ):
                            half = (t % 2) * 512
                            sl = slice(t * 512, (t + 1) * 512)
                            if t >= 2:
                                s.wait("tensor", c_evv, done)
                            s.emit("tensor", lambda e, sl=sl, half=half, srcbuf=srcbuf: e.matmul(
                                PBLD[:, half:half + 512], swapP_sb[:], srcbuf[:, sl],
                                start=True, stop=True), inc=c_pe)
                            sw = c_pe.count
                            s.wait("vector", c_pe, sw)
                            s.raw("vector", lambda e, sl=sl, srcbuf=srcbuf, dstbuf=dstbuf: e.tensor_mul(
                                out=dstbuf[:, sl], in0=srcbuf[:, sl], in1=ropeC[:, sl]))
                            s.raw("vector", lambda e, sl=sl, half=half, scratch=scratch: e.tensor_mul(
                                out=scratch[:, 0:512], in0=PBLD[:, half:half + 512],
                                in1=ropeS[:, sl]))
                            s.emit("vector", lambda e, sl=sl, dstbuf=dstbuf, scratch=scratch: e.tensor_add(
                                out=dstbuf[:, sl], in0=dstbuf[:, sl], in1=scratch[:, 0:512]),
                                inc=c_evv)
                            done = c_evv.count
                        return done

                    proj(kCb, 0, 4, ckv, 0, buk[:, h:h + 1])
                    proj(krp, 2048, KD, RES, 0, bkr[:, h:h + 1])
                    rope_k_done = rope(krp, kRb, ptile_full)
                    s.wait("tensor", c_evv, rope_k_done)
                    for g in range(2):
                        if g == 1:
                            s.wait("tensor", c_eva)
                        for ktl in range(8):
                            kt = g * 8 + ktl
                            for c in range(4):
                                s.raw("tensor", lambda e, kt=kt, ktl=ktl, c=c, W=W: e.matmul(
                                    PBLD[:, ktl * 128:(ktl + 1) * 128],
                                    ckv[:, c, kt * 128:(kt + 1) * 128],
                                    W[:, 512 + c * 128:512 + (c + 1) * 128],
                                    start=(c == 0), stop=False))
                            fn = lambda e, kt=kt, ktl=ktl, h=h: e.matmul(
                                PBLD[:, ktl * 128:(ktl + 1) * 128], onesrow_f[:],
                                buv_row[0:1, h * 128:(h + 1) * 128], start=False, stop=True)
                            if ktl == 7:
                                s.emit("tensor", fn, inc=c_pe)
                            else:
                                s.raw("tensor", fn)
                        s.wait("scalar", c_pe)
                        s.emit("scalar", lambda e, g=g: e.activation(
                            out=Vtmb[:, g * 8:(g + 1) * 8, 0:128],
                            in_=PBLD[:, :].rearrange("p (a b) -> p a b", b=128),
                            func=AF.Identity), inc=c_eva)
                    v_done = c_eva.count
                    s.wait("tensor", c_eva, v_done)
                    s.wait("scalar", c_evv, rope_k_done)
                    proj(qCb, 1024, 8, cq, 0, buq16[:, h:h + 1], scale=1.0 / 16.0)
                    ready = rope(qCb, qRb, ptile_full)
                    ropeq_evv = ready
                    build_done[h] = c_pe.count
                    return ready, build_done[h], ropeq_evv

                def attend_head(h, ready_c):
                    pr = h % 2
                    s.wait("tensor", c_evv, ready_c)
                    s.wait("tensor", c_eva)
                    psf_free = 0
                    pt_copy = 0
                    pt2_read = 0
                    po_read = 0
                    for qi in range(16):
                        nk = (qi + 1) * 128
                        nsl = (nk + 511) // 512
                        if psf_free:
                            s.wait("tensor", c_eva, psf_free)
                        for sl in range(nsl):
                            w = min(512, nk - sl * 512)
                            s.raw("tensor", lambda e, sl=sl, w=w, qi=qi: e.matmul(
                                PSF[:, sl * 512:sl * 512 + w],
                                qCb[:, qi * 128:(qi + 1) * 128],
                                kCb[:, sl * 512:sl * 512 + w], start=True, stop=False))
                            fn = lambda e, sl=sl, w=w, qi=qi: e.matmul(
                                PSF[:, sl * 512:sl * 512 + w],
                                qRb[:, qi * 128:(qi + 1) * 128],
                                kRb[:, sl * 512:sl * 512 + w], start=False, stop=True)
                            if sl == nsl - 1:
                                s.emit("tensor", fn, inc=c_pe)
                            else:
                                s.raw("tensor", fn)
                        sc_done = c_pe.count
                        s.wait("vector", c_pe, sc_done)
                        s.emit("vector", lambda e, nk=nk: e.tensor_add(
                            out=PSF[:, nk - 128:nk], in0=PSF[:, nk - 128:nk],
                            in1=tril[:]), inc=c_evv)
                        s.emit("vector", lambda e, nk=nk: e.reduce_max(
                            out=rows[:, 4:5], in_=PSF[:, 0:nk], axis=AX.X,
                            negate=True), inc=c_evv)
                        s.wait("vector", c_evv)
                        nm_done = c_evv.count
                        s.wait("scalar", c_evv, nm_done)
                        if pt_copy:
                            pass
                        s.emit("scalar", lambda e, nk=nk: e.activation(
                            out=ptile_full[:, 0:nk], in_=PSF[:, 0:nk], func=AF.Exp,
                            bias=rows[:, 4:5]), inc=c_eva)
                        psf_free = c_eva.count
                        exp_done = c_eva.count
                        nkt = nk // 128
                        first_pv = True
                        for g0 in range(0, nkt, 8):
                            gn = min(8, nkt - g0)
                            s.wait("tensor", c_eva, exp_done)
                            if pt_copy:
                                s.wait("tensor", c_evv, pt_copy)
                            if pt2_read:
                                s.wait("tensor", c_eva, pt2_read)
                            for ktl in range(gn):
                                fn = lambda e, g0=g0, ktl=ktl: e.transpose(
                                    PT[:, ktl * 128:(ktl + 1) * 128],
                                    ptile_full[:, (g0 + ktl) * 128:(g0 + ktl + 1) * 128],
                                    ident[:])
                                if ktl == gn - 1:
                                    s.emit("tensor", fn, inc=c_pe)
                                else:
                                    s.raw("tensor", fn)
                            tr_done = c_pe.count
                            s.wait("vector", c_pe, tr_done)
                            if first_pv and po_read:
                                pass
                            s.emit("vector", lambda e, gn=gn: e.tensor_copy(
                                out=pTs_full[:, 0:gn * 128], in_=PT[:, 0:gn * 128]),
                                inc=c_evv)
                            pt_copy = c_evv.count
                            s.wait("tensor", c_evv, pt_copy)
                            if first_pv and po_read:
                                s.wait("tensor", c_evv, po_read)
                            for ktl in range(gn):
                                kt = g0 + ktl
                                fn = lambda e, kt=kt, ktl=ktl, nkt=nkt: e.matmul(
                                    PO[:, 0:129], pTs_full[:, ktl * 128:(ktl + 1) * 128],
                                    Vtmb[:, kt, 0:129],
                                    start=(kt == 0), stop=(kt == nkt - 1))
                                if ktl == gn - 1:
                                    s.emit("tensor", fn, inc=c_pe)
                                else:
                                    s.raw("tensor", fn)
                            first_pv = False
                        pv_done = c_pe.count
                        s.wait("vector", c_pe, pv_done)
                        s.emit("vector", lambda e: e.reciprocal(
                            out=rows[:, 3:4], in_=PO[:, 128:129]), inc=c_evv)
                        s.wait("vector", c_evv)
                        s.emit("vector", lambda e: e.tensor_scalar(
                            out=otm[:], in0=PO[:, 0:128], scalar1=rows[:, 3:4],
                            scalar2=1.0, op0=ALU.mult, op1=ALU.mult), inc=c_evv)
                        po_read = c_evv.count
                        fin = c_evv.count
                        s.wait("tensor", c_evv, fin)
                        if pt_copy:
                            s.wait("tensor", c_evv, pt_copy)
                        if pt2_read:
                            s.wait("tensor", c_eva, pt2_read)
                        s.emit("tensor", lambda e: e.transpose(
                            PT[:, 0:128], otm[:], ident[:]), inc=c_pe)
                        ot_done = c_pe.count
                        s.wait("scalar", c_pe, ot_done)
                        s.wait("scalar", d_ev)
                        s.emit("scalar", lambda e, qi=qi, pr=pr: e.activation(
                            out=EVB[pr][:, qi * 128:(qi + 1) * 128], in_=PT[:, 0:128],
                            func=AF.Identity), inc=c_eva)
                        pt2_read = c_eva.count
                    s.wait("scalar", c_eva)
                    s.emit("scalar", lambda e, h=h, pr=pr: e.dma_start(
                        out=osp[h], in_=EVB[pr][:]), inc=d_ev, inc_n=16)
                    return c_pe.count

                swapP_sb = mktile(sc, [128, 128], BF16)
                s.emit("sync", lambda e: e.dma_start(out=swapP_sb[:], in_=swapP_c.ap()),
                       inc=d_misc, inc_n=16)
                s.wait("tensor", d_misc)
                s.wait("vector", d_misc)
                s.emit("vector", lambda e: e.memset(Vtmb[:, :, 128:129], 1.0), inc=c_evv)
                s.wait("tensor", c_evv)

                att_pe = 0
                load_head(0, 0)
                ready, bpe, ropeq = build_head(0, 0, 0)
                for h in range(1, H):
                    load_head(h, att_pe)
                    att_pe = attend_head(h - 1, ready)
                    ready, bpe, ropeq = build_head(h, att_pe, ropeq)
                att_pe = attend_head(H - 1, ready)
            s.fence()

            with ExitStack() as sc:
                EVF = [mktile(sc, [128, 2048], F32) for _ in range(2)]
                RST = [mktile(sc, [128, 2048], F32) for _ in range(2)]
                PSA = mkpsum(sc, [128, 2048])
                PSB = mkpsum(sc, [128, 2048])
                for j in range(16):
                    s.emit("sync", lambda e, j=j: e.dma_start(
                        out=RES[:, j, :], in_=osp[j]), inc=d_res, inc_n=16)
                s.wait("tensor", d_res)
                s.wait("gpsimd", d_res)

                def evac_b3(i, ps, pr):
                    s.wait("sync", c_evv)
                    s.emit("sync", lambda e, i=i, pr=pr: e.dma_start(
                        out=RST[pr][:], in_=Rm[i]), inc=d_xb, inc_n=16)
                    s.wait("scalar", c_mm[pr])
                    s.wait("scalar", c_evv)
                    s.emit("scalar", lambda e, i=i, ps=ps, pr=pr: e.activation(
                        out=EVF[pr][:], in_=ps[:, :], func=AF.Identity,
                        bias=bo[:, i:i + 1]), inc=c_eva)
                    free = (c_eva, c_eva.count)
                    s.wait("vector", c_eva)
                    s.wait("vector", d_xb)
                    s.wait("vector", d_acc)
                    s.emit("vector", lambda e, pr=pr: e.tensor_add(
                        out=EVF[pr][:], in0=EVF[pr][:], in1=RST[pr][:]), inc=c_evv)
                    s.wait("scalar", c_evv)
                    s.emit("scalar", lambda e, i=i, pr=pr: e.dma_start(
                        out=Rm[i], in_=EVF[pr][:]), inc=d_acc, inc_n=16)
                    return free

                stream_panels([wo_t[i] for i in range(16)], 16, PSA, PSB, evac_b3)
        s.fence()
        if stop_after == "B":
            emit_tail(nc, s, Rm, outT, STG)
            s.build()
            return nc

        emit_ffn(1, True)
        for name, c in list(s.sems.items()):
            if name.startswith("dma_"):
                s.wait("sync", c)
        s.build()
    return nc


def emit_tail(nc, s, Rm, outT, STG):
    d_tail = s.sem("dma_tail")
    for j in range(KD):
        pr = j % 2
        s.emit("sync", lambda e, j=j, pr=pr: e.dma_start(
            out=STG[pr][:], in_=Rm[j]), inc=d_tail, inc_n=16)
        s.wait("sync", d_tail)
        s.emit("sync", lambda e, j=j, pr=pr: e.dma_start(
            out=outT[j], in_=STG[pr][:]), inc=d_tail, inc_n=16)
    for name, c in list(s.sems.items()):
        if name.startswith("dma_"):
            s.wait("sync", c)


_NC_CACHE = {}


def kernel(**inputs):
    import os
    stop_after = os.environ.get("MK_STOP", "full")
    hp = {
        "win_t": tile_lhsT(np.asarray(inputs["L0_attn_a_W_in"])),
        "wout_t": tile_lhsT(np.asarray(inputs["L0_attn_a_W_out"]), groups=2),
        "wm1_t": tile_lhsT(np.asarray(inputs["L0_attn_a_W_mlp1"])),
        "wm2_t": tile_lhsT(np.asarray(inputs["L0_attn_a_W_mlp2"]), groups=4),
        "w1_t": tile_lhsT(np.asarray(inputs["L0_ffn_w1"])),
        "w2_t": tile_lhsT(np.asarray(inputs["L0_ffn_w2"])),
        "w3_t": tile_lhsT(np.asarray(inputs["L0_ffn_w3"]), groups=4),
        "wdkv_t": tile_lhsT(np.asarray(inputs["L0_attn_b_W_DKV_w"])),
        "wuk_t": tile_lhsT(np.asarray(inputs["L0_attn_b_W_UK_w"])),
        "wuv_t": tile_lhsT(np.asarray(inputs["L0_attn_b_W_UV_w"])),
        "wdq_t": tile_lhsT(np.asarray(inputs["L0_attn_b_W_DQ_w"])),
        "wuq_t": tile_lhsT(np.asarray(inputs["L0_attn_b_W_UQ_w"])),
        "wkr_t": tile_lhsT(np.asarray(inputs["L0_attn_b_W_KR_w"])),
        "wo_t": tile_lhsT(np.asarray(inputs["L0_attn_b_W_O_w"])),
        "convw": np.ascontiguousarray(
            np.asarray(inputs["L0_attn_a_conv_w"]).reshape(32, 128, 4).transpose(1, 0, 2)),
        "convb": chunk_vec(inputs["L0_attn_a_conv_b"]),
        "bdkv": chunk_vec(inputs["L0_attn_b_W_DKV_b"]),
        "buk": chunk_vec(inputs["L0_attn_b_W_UK_b"]),
        "buv_row": np.ascontiguousarray(
            np.asarray(inputs["L0_attn_b_W_UV_b"]).reshape(1, 2048)),
        "bdq": chunk_vec(inputs["L0_attn_b_W_DQ_b"]),
        "buq": chunk_vec(inputs["L0_attn_b_W_UQ_b"]),
        "bkr": chunk_vec(inputs["L0_attn_b_W_KR_b"]),
        "bo": chunk_vec(inputs["L0_attn_b_W_O_b"]),
    }
    x = np.asarray(inputs["L0_attn_a_x"])
    nc = build_program(stop_after)
    in_maps = []
    for b in range(8):
        m = dict(hp)
        m["xt"] = np.ascontiguousarray(x[b].T)
        in_maps.append(m)
    res = run_bass_kernel_spmd(nc, in_maps, core_ids=list(range(8)))
    out = np.empty((8, T, D), np.float32)
    for b in range(8):
        ot = np.asarray(res.results[b]["outT"])
        out[b] = ot.reshape(D, T).T
    return out


# version 13
# speedup vs baseline: 1.1881x; 1.0319x over previous
import numpy as np
import ml_dtypes
from contextlib import ExitStack

import concourse.bass as bass
import concourse.mybir as mybir
from concourse.bass_utils import run_bass_kernel_spmd

F32 = mybir.dt.float32
BF16 = mybir.dt.bfloat16
AF = mybir.ActivationFunctionType
ALU = mybir.AluOpType
AX = mybir.AxisListType
BF = ml_dtypes.bfloat16

D = 2048
T = 2048
KD = 16
NQ = 4
H = 16
KF = 44
EPS_LN = 1e-5
NEG_BIG = -3.0e38
FFN_CNT = 5632 * 2048

ENGINES = ("tensor", "vector", "scalar", "gpsimd", "sync")


class Ctr:
    __slots__ = ("h", "count", "name")

    def __init__(self, h, name):
        self.h = h
        self.count = 0
        self.name = name


class Sched:
    def __init__(self, nc, st):
        self.nc = nc
        self.st = st
        self.q = {e: [] for e in ENGINES}
        self.sems = {}

    def sem(self, name):
        if name not in self.sems:
            h = self.st.enter_context(self.nc.semaphore(name))
            self.sems[name] = Ctr(h, name)
        return self.sems[name]

    def wait(self, eng, ctr, val=None):
        v = ctr.count if val is None else val
        if v <= 0:
            return
        self.q[eng].append(lambda e, h=ctr.h, v=v: e.wait_ge(h, v))

    def emit(self, eng, fn, inc=None, inc_n=1):
        if inc is not None:
            inc.count += inc_n

        def run(e, fn=fn, inc=inc, n=inc_n):
            ins = fn(e)
            if inc is not None:
                ins.then_inc(inc.h, n)

        self.q[eng].append(run)

    def raw(self, eng, fn):
        self.q[eng].append(fn)

    def fence(self):
        for name, c in list(self.sems.items()):
            if name.startswith("dma_"):
                self.wait("sync", c)
        for eng in ENGINES:
            f = self.sem("fence_" + eng)
            if eng in ("tensor", "sync"):
                self.emit(eng, lambda e: e.nop(), inc=f)
            else:
                self.emit(eng, lambda e: e.drain(fusable=False), inc=f)
        for eng in ENGINES:
            for other in ENGINES:
                if other != eng:
                    self.wait(eng, self.sem("fence_" + other))

    def build(self):
        block = self.st.enter_context(self.nc.Block())

        def mk(eng):
            def body(e):
                for f in self.q[eng]:
                    f(e)
            return body

        block.tensor(mk("tensor"))
        block.vector(mk("vector"))
        block.scalar(mk("scalar"))
        block.gpsimd(mk("gpsimd"))
        block.sync(mk("sync"))


def tile_lhsT(Wt, groups=1):
    Fd, Cd = Wt.shape
    nf, kc = Fd // 128, Cd // 128
    L = Wt.reshape(nf, 128, kc, 128).transpose(0, 3, 2, 1)
    if groups == 1:
        return np.ascontiguousarray(L.reshape(nf, 128, kc * 128))
    kg = kc // groups
    Lg = L.reshape(nf, 128, groups, kg, 128).transpose(2, 0, 1, 3, 4)
    return np.ascontiguousarray(Lg.reshape(groups, nf, 128, kg * 128))


def chunk_vec(b):
    return np.ascontiguousarray(np.asarray(b).reshape(-1, 128).T)


def rope_tables():
    pos = np.arange(T, dtype=np.float64)
    idx = np.arange(64, dtype=np.float64)
    theta = (10000.0 ** (-2.0 * idx / 128.0))[:, None] * pos[None, :]
    C = np.repeat(np.cos(theta), 2, axis=0).astype(BF)
    S = np.empty((128, T), np.float64)
    S[0::2] = -np.sin(theta)
    S[1::2] = np.sin(theta)
    return C, S.astype(BF)


def build_program(stop_after="full"):
    nc = bass.Bass("TRN2", target_bir_lowering=False)

    def ein(name, shape):
        return nc.dram_tensor(name, shape, F32, kind="ExternalInput").ap()

    xt = ein("xt", [D, T])
    win_t = ein("win_t", [64, 128, 2048])
    wout_t = ein("wout_t", [2, 16, 128, 2048])
    wm1_t = ein("wm1_t", [64, 128, 2048])
    wm2_t = ein("wm2_t", [4, 16, 128, 2048])
    w1_t = ein("w1_t", [KF, 128, 2048])
    w2_t = ein("w2_t", [KF, 128, 2048])
    w3_t = ein("w3_t", [4, 16, 128, 1408])
    wdkv_t = ein("wdkv_t", [4, 128, 2048])
    wuk_t = ein("wuk_t", [16, 128, 512])
    wuv_t = ein("wuv_t", [16, 128, 512])
    wdq_t = ein("wdq_t", [8, 128, 2048])
    wuq_t = ein("wuq_t", [16, 128, 1024])
    hwcat_t = ein("hwcat_t", [16, 128, 2048])
    wkr_t = ein("wkr_t", [16, 128, 2048])
    wo_t = ein("wo_t", [16, 128, 2048])
    convw_d = ein("convw", [128, 32, 4])
    convb_d = ein("convb", [128, 32])
    bdkv_d = ein("bdkv", [128, 4])
    buk_d = ein("buk", [128, 16])
    buv_row_d = ein("buv_row", [1, 2048])
    bdq_d = ein("bdq", [128, 8])
    buq_d = ein("buq", [128, 16])
    bkr_d = ein("bkr", [128, 16])
    bo_d = ein("bo", [128, 16])

    import os as _os
    _dbg = set(filter(None, _os.environ.get("MK_DBG", "").split(",")))

    def _scratch(name, shape, dt):
        kind = "ExternalOutput" if name in _dbg else "Internal"
        return nc.dram_tensor(name, shape, dt, kind=kind).ap()

    outT = nc.dram_tensor("outT", [KD, 128, T], F32, kind="ExternalOutput").ap()
    Rm = _scratch("Rm", [KD, 128, T], F32)
    ysp = _scratch("ysp", [32, 128, T], BF16)
    usp = _scratch("usp", [64, 128, T], BF16)
    sgsp = _scratch("sgsp", [KF, 128, T], BF16)
    hidsp = _scratch("hidsp", [KF, 128, T], BF16)
    osp = _scratch("osp", [16, 128, T], BF16)
    wq1sp = _scratch("wq1sp", [KF, 128, 2048], BF16)
    wq2sp = _scratch("wq2sp", [KF, 128, 2048], BF16)
    wq3sp = _scratch("wq3sp", [4, 16, 128, 1408], BF16)
    woutbf = _scratch("woutbf", [2, 16, 128, 2048], BF16)
    wm2bf = _scratch("wm2bf", [4, 16, 128, 2048], BF16)
    hwbf = _scratch("hwbf", [16, 2, 128, 2048], BF16)
    hsp = _scratch("hsp", [KD, 128, T], BF16) if "hsp" in _dbg else None
    scdbg = _scratch("scdbg", [4, 128, 512], F32) if "scdbg" in _dbg else None

    ident_c = nc.inline_tensor(np.eye(128, dtype=BF), "ident_c")
    onescol_c = nc.inline_tensor(np.ones((128, 1), dtype=BF), "onescol_c")
    onesrow_f_c = nc.inline_tensor(np.ones((1, 128), np.float32), "onesrow_f_c")
    onescolf_c = nc.inline_tensor(np.ones((128, 1), np.float32), "onescolf_c")
    epsc_np = np.zeros((128, 2), np.float32)
    epsc_np[:, 0] = EPS_LN
    epsc_np[:, 1] = 0.5e-8
    epsc_c = nc.inline_tensor(epsc_np, "epsc_c")
    swapP_np = np.zeros((128, 128), dtype=BF)
    for i in range(64):
        swapP_np[2 * i, 2 * i + 1] = 1
        swapP_np[2 * i + 1, 2 * i] = 1
    swapP_c = nc.inline_tensor(swapP_np, "swapP_c")
    tril_np = np.where(np.arange(128)[None, :] <= np.arange(128)[:, None],
                       np.float32(0.0), np.float32(NEG_BIG)).astype(np.float32)
    tril_c = nc.inline_tensor(tril_np, "tril_c")
    ropeC_np, ropeS_np = rope_tables()
    ropeC_c = nc.inline_tensor(ropeC_np, "ropeC_c")
    ropeS_c = nc.inline_tensor(ropeS_np, "ropeS_c")

    st = ExitStack()
    with st:
        RES = st.enter_context(nc.sbuf_tensor("RES", [128, KD, T], BF16))
        EVB = [st.enter_context(nc.sbuf_tensor(f"EVB{i}", [128, 2048], BF16)) for i in range(2)]
        ident = st.enter_context(nc.sbuf_tensor("ident", [128, 128], BF16))
        onescol = st.enter_context(nc.sbuf_tensor("onescol", [128, 1], BF16))
        onesrow_f = st.enter_context(nc.sbuf_tensor("onesrow_f", [1, 128], F32))
        convw = st.enter_context(nc.sbuf_tensor("convw_sb", [128, 32, 4], F32))
        convb = st.enter_context(nc.sbuf_tensor("convb_sb", [128, 32], F32))
        bdkv = st.enter_context(nc.sbuf_tensor("bdkv_sb", [128, 4], F32))
        buk = st.enter_context(nc.sbuf_tensor("buk_sb", [128, 16], F32))
        buv_row = st.enter_context(nc.sbuf_tensor("buv_row_sb", [1, 2048], F32))
        bdq = st.enter_context(nc.sbuf_tensor("bdq_sb", [128, 8], F32))
        buq16 = st.enter_context(nc.sbuf_tensor("buq16", [128, 16], F32))
        bkr = st.enter_context(nc.sbuf_tensor("bkr_sb", [128, 16], F32))
        bo = st.enter_context(nc.sbuf_tensor("bo_sb", [128, 16], F32))
        gacc = st.enter_context(nc.sbuf_tensor("gacc", [128, 4], F32))
        epsc = st.enter_context(nc.sbuf_tensor("epsc", [128, 2], F32))
        gtmp = st.enter_context(nc.sbuf_tensor("gtmp", [128, 1], F32))
        thrv = st.enter_context(nc.sbuf_tensor("thrv", [128, 4], F32))
        nthrv = st.enter_context(nc.sbuf_tensor("nthrv", [128, 4], F32))

        s = Sched(nc, st)
        d_stg = [s.sem("dma_stg0"), s.sem("dma_stg1")]
        d_grp = [s.sem("dma_grp0"), s.sem("dma_grp1")]
        d_res = s.sem("dma_res")
        d_xb = s.sem("dma_xb")
        d_xb4 = [s.sem(f"dma_xb4_{i}") for i in range(4)]
        d_stg4 = [s.sem("dma_stg2"), s.sem("dma_stg3")]
        d_hw = [s.sem("dma_hw0"), s.sem("dma_hw1")]
        d_stv = s.sem("dma_stv")
        d_ev = s.sem("dma_ev")
        d_acc = s.sem("dma_acc")
        d_wq = s.sem("dma_wq")
        d_side = s.sem("dma_side")
        d_sta = s.sem("dma_sta")
        d_misc = s.sem("dma_misc")
        d_g = [s.sem("dma_g0"), s.sem("dma_g1")]
        c_pool = s.sem("c_pool")
        c_pan = [s.sem("c_pan0"), s.sem("c_pan1")]
        c_mm = [s.sem("c_mm0"), s.sem("c_mm1")]
        c_evv = s.sem("c_evv")
        c_eva = s.sem("c_eva")
        c_pe = s.sem("c_pe")
        c_g = s.sem("c_g")

        for dst, src in ((epsc, epsc_c.ap()), (ident, ident_c.ap()), (onescol, onescol_c.ap()),
                         (onesrow_f, onesrow_f_c.ap()), (convw, convw_d),
                         (convb, convb_d), (bdkv, bdkv_d), (buk, buk_d),
                         (buv_row, buv_row_d), (bdq, bdq_d), (bkr, bkr_d),
                         (bo, bo_d), (buq16, buq_d)):
            s.emit("sync", lambda e, dst=dst, src=src: e.dma_start(out=dst[:], in_=src),
                   inc=d_misc, inc_n=16)
        s.wait("vector", d_misc)
        s.emit("vector", lambda e: e.tensor_scalar(
            out=buq16[:], in0=buq16[:], scalar1=1.0 / 16.0, scalar2=None, op0=ALU.mult),
            inc=c_evv)
        s.emit("gpsimd", lambda e: e.memset(gacc[:], 0.0), inc=c_pool)
        s.wait("tensor", d_misc)

        uid = [0]

        def mktile(scope, shape, dt):
            uid[0] += 1
            return scope.enter_context(nc.sbuf_tensor(f"t{uid[0]}", shape, dt))

        def mkpsum(scope, shape, dt=F32):
            uid[0] += 1
            return scope.enter_context(nc.psum_tensor(f"p{uid[0]}", shape, dt))

        def load_res_stats(scope, src_chunks, mode, STG):
            SQ = [mktile(scope, [128, T], BF16) for _ in range(2)]
            SR1 = mktile(scope, [1, T], F32)
            SR2 = mktile(scope, [1, T], F32)
            nbt = 3 if mode == "A" else 2
            BT = [mktile(scope, [128, T], BF16) for _ in range(nbt)]
            BTS = mktile(scope, [128, T], BF16)
            cast_c = []
            sq_c = []
            with ExitStack() as ps_sc:
                PS_S = mkpsum(ps_sc, [1, T])
                PS_Q = mkpsum(ps_sc, [1, T])
                for j in range(KD):
                    pr = j % 2
                    if j >= 2:
                        s.wait("sync", c_pan[pr], cast_c[j - 2])
                    s.emit("sync", lambda e, j=j, pr=pr: e.dma_start(
                        out=STG[pr][:], in_=src_chunks[j]), inc=d_stg[pr], inc_n=16)
                    ceng = "scalar" if pr == 0 else "vector"
                    s.wait(ceng, d_stg[pr])
                    if pr == 0:
                        s.emit("scalar", lambda e, j=j, pr=pr: e.activation(
                            out=RES[:, j, :], in_=STG[pr][:], func=AF.Identity), inc=c_pan[0])
                    else:
                        s.emit("vector", lambda e, j=j, pr=pr: e.tensor_copy(
                            out=RES[:, j, :], in_=STG[pr][:]), inc=c_pan[1])
                    cast_c.append(c_pan[pr].count)
                    s.wait("scalar", c_pan[pr], cast_c[j])
                    if j >= 2:
                        s.wait("scalar", c_pe, sq_c[j - 2])
                    s.emit("scalar", lambda e, j=j, pr=pr: e.activation(
                        out=SQ[pr][:], in_=RES[:, j, :], func=AF.Square), inc=c_eva)
                    sq_done = c_eva.count
                    s.wait("tensor", c_pan[pr], cast_c[j])
                    for q in range(NQ):
                        s.raw("tensor", lambda e, j=j, q=q: e.matmul(
                            PS_S[:, q * 512:(q + 1) * 512], onescol[:],
                            RES[:, j, q * 512:(q + 1) * 512],
                            start=(j == 0), stop=(j == KD - 1)))
                    s.wait("tensor", c_eva, sq_done)
                    for q in range(NQ):
                        fn = lambda e, pr=pr, q=q, j=j: e.matmul(
                            PS_Q[:, q * 512:(q + 1) * 512], onescol[:],
                            SQ[pr][:, q * 512:(q + 1) * 512],
                            start=(j == 0), stop=(j == KD - 1))
                        if q == NQ - 1:
                            s.emit("tensor", fn, inc=c_pe)
                        else:
                            s.raw("tensor", fn)
                    sq_c.append(c_pe.count)
                s.wait("vector", c_pe)
                s.emit("vector", lambda e: e.tensor_copy(out=SR1[:], in_=PS_S[:, :]), inc=c_evv)
                s.emit("vector", lambda e: e.tensor_copy(out=SR2[:], in_=PS_Q[:, :]), inc=c_evv)
                rows_done = c_evv.count
            with ExitStack() as ps_sc:
                PSB = mkpsum(ps_sc, [128, 2048])
                s.wait("tensor", c_evv, rows_done)
                for q in range(NQ):
                    fn = lambda e, q=q: e.matmul(
                        PSB[:, q * 512:(q + 1) * 512], onesrow_f[:],
                        SR1[:, q * 512:(q + 1) * 512], start=True, stop=True)
                    s.emit("tensor", fn, inc=c_pe) if q == NQ - 1 else s.raw("tensor", fn)
                s.wait("scalar", c_pe)
                s.emit("scalar", lambda e: e.activation(
                    out=BT[0][:], in_=PSB[:, :], func=AF.Identity, scale=1.0 / D), inc=c_eva)
                s.wait("tensor", c_eva)
                for q in range(NQ):
                    fn = lambda e, q=q: e.matmul(
                        PSB[:, q * 512:(q + 1) * 512], onesrow_f[:],
                        SR2[:, q * 512:(q + 1) * 512], start=True, stop=True)
                    s.emit("tensor", fn, inc=c_pe) if q == NQ - 1 else s.raw("tensor", fn)
                s.wait("scalar", c_pe)
                s.emit("scalar", lambda e: e.activation(
                    out=BT[1][:], in_=PSB[:, :], func=AF.Identity, scale=1.0 / D), inc=c_eva)
            s.wait("vector", c_eva)
            s.emit("vector", lambda e: e.tensor_mul(out=BTS[:], in0=BT[0][:], in1=BT[0][:]), inc=c_evv)
            s.emit("vector", lambda e: e.tensor_sub(out=BT[1][:], in0=BT[1][:], in1=BTS[:]), inc=c_evv)
            s.wait("scalar", c_evv)
            if mode == "A":
                s.emit("scalar", lambda e: e.activation(
                    out=BT[2][:], in_=BT[1][:], func=AF.Ln, bias=epsc[:, 0:1]), inc=c_eva)
                s.emit("scalar", lambda e: e.activation(
                    out=BT[1][:], in_=BT[1][:], func=AF.Ln), inc=c_eva)
                s.wait("vector", c_eva)
                s.emit("vector", lambda e: e.tensor_sub(out=BT[2][:], in0=BT[1][:], in1=BT[2][:]), inc=c_evv)
                s.wait("scalar", c_evv)
                s.emit("scalar", lambda e: e.activation(
                    out=BT[2][:], in_=BT[2][:], func=AF.Exp, scale=0.5), inc=c_eva)
                s.emit("scalar", lambda e: e.activation(
                    out=BT[1][:], in_=BT[1][:], func=AF.Exp, scale=-0.5), inc=c_eva)
                return BT[0], BT[1], BT[2], BTS
            else:
                s.emit("scalar", lambda e: e.activation(
                    out=BT[1][:], in_=BT[1][:], func=AF.Ln, bias=epsc[:, 0:1]), inc=c_eva)
                s.emit("scalar", lambda e: e.activation(
                    out=BT[1][:], in_=BT[1][:], func=AF.Exp, scale=-0.5), inc=c_eva)
                return BT[0], BT[1], BTS

        def center_res(Bmu, Bscale, scr):
            s.wait("vector", c_eva)
            for j in range(KD):
                s.emit("vector", lambda e, j=j: e.tensor_sub(
                    out=scr[:], in0=RES[:, j, :], in1=Bmu[:]), inc=c_evv)
                s.emit("vector", lambda e, j=j: e.tensor_mul(
                    out=RES[:, j, :], in0=scr[:], in1=Bscale[:]), inc=c_evv)

        def stream_panels(panels, kc, psA, psB, evac_fn, STG=None, PAN=None, quant=None,
                          pool_extra=None, bf16_src=False, rhs=None, rhs_kc=None,
                          spills=None):
            if rhs is None:
                rhs, rhs_kc = RES, KD
            for qq in ("sync", "gpsimd"):
                s.wait(qq, c_mm[0])
                s.wait(qq, c_mm[1])
                s.wait(qq, c_pan[0])
                s.wait(qq, c_pan[1])
            wid = kc * 128
            n = len(panels)
            cast_c = [None] * n
            mm_c = [None] * n
            free_c = [None] * n
            for i in range(n):
                pr = i % 2
                if bf16_src:
                    if i >= 2:
                        s.wait("sync", c_mm[pr], mm_c[i - 2])
                    s.emit("sync", lambda e, i=i, pr=pr: e.dma_start(
                        out=PAN[pr][:, 0:wid], in_=panels[i]), inc=d_stg[pr], inc_n=16)
                    s.wait("tensor", d_stg[pr])
                else:
                    if i >= 2:
                        s.wait("sync", c_pan[pr], cast_c[i - 2])
                    s.emit("sync", lambda e, i=i, pr=pr: e.dma_start(
                        out=STG[pr][:, 0:wid], in_=panels[i]), inc=d_stg[pr], inc_n=16)
                    ceng = ("scalar" if pr == 0 else "vector") if quant is None else "vector"
                    s.wait(ceng, d_stg[pr])
                    if i >= 2:
                        s.wait(ceng, c_mm[pr], mm_c[i - 2])
                    if spills is not None and i >= 2 and spills[i - 2] is not None:
                        s.wait(ceng, d_wq)
                    if quant is None:
                        if ceng == "scalar":
                            s.emit("scalar", lambda e, pr=pr: e.activation(
                                out=PAN[pr][:, 0:wid], in_=STG[pr][:, 0:wid],
                                func=AF.Identity), inc=c_pan[pr])
                        else:
                            s.emit("vector", lambda e, pr=pr: e.tensor_copy(
                                out=PAN[pr][:, 0:wid], in_=STG[pr][:, 0:wid]), inc=c_pan[pr])
                    else:
                        m, QA, QB = quant
                        s.raw("vector", lambda e, pr=pr: e.tensor_scalar(
                            out=QA[:, 0:wid], in0=STG[pr][:, 0:wid],
                            scalar1=thrv[:, m:m + 1], scalar2=1.0,
                            op0=ALU.is_gt, op1=ALU.mult))
                        s.raw("vector", lambda e, pr=pr: e.tensor_scalar(
                            out=QB[:, 0:wid], in0=STG[pr][:, 0:wid],
                            scalar1=nthrv[:, m:m + 1], scalar2=1.0,
                            op0=ALU.is_ge, op1=ALU.mult))
                        s.emit("vector", lambda e, pr=pr: e.scalar_tensor_tensor(
                            out=PAN[pr][:, 0:wid], in0=QA[:, 0:wid], scalar=-1.0,
                            in1=QB[:, 0:wid], op0=ALU.add, op1=ALU.add), inc=c_pan[pr])
                    cast_c[i] = c_pan[pr].count
                    if spills is not None and spills[i] is not None:
                        s.wait("scalar", c_pan[pr], cast_c[i])
                        s.emit("scalar", lambda e, i=i, pr=pr: e.dma_start(
                            out=spills[i], in_=PAN[pr][:, 0:wid]), inc=d_wq, inc_n=16)
                    s.wait("tensor", c_pan[pr], cast_c[i])
                if pool_extra is not None:
                    pool_extra(i)
                if i >= 2:
                    s.wait("tensor", *free_c[i - 2])
                ps = psA if pr == 0 else psB
                for t in range(NQ):
                    for k in range(kc):
                        fn = lambda e, ps=ps, pr=pr, k=k, t=t: e.matmul(
                            ps[:, t * 512:(t + 1) * 512],
                            PAN[pr][:, k * 128:(k + 1) * 128],
                            rhs[:, k, t * 512:(t + 1) * 512],
                            start=(k == 0), stop=(k == kc - 1))
                        if t == NQ - 1 and k == kc - 1:
                            s.emit("tensor", fn, inc=c_mm[pr])
                        else:
                            s.raw("tensor", fn)
                mm_c[i] = c_mm[pr].count
                free_c[i] = evac_fn(i, ps, pr)

        def acc_stage(qg_panels, kc_g, ngrp, grp_src_fn, ACC, GRP, evac_q_fn,
                      STG=None, PAN=None, bf16_pan=False, side_fn=None, deep=True):
            wid = kc_g * 128
            PQ = None
            with ExitStack() as ps_sc:
                PQ = [mkpsum(ps_sc, [128, 512]) for _ in range(2)]
                if bf16_pan:
                    if deep:
                        PANX = [mktile(ps_sc, [128, wid], BF16) for _ in range(2)]
                        pbufs = [PAN[0], PAN[1], PANX[0], PANX[1]]
                        psems = [d_stg[0], d_stg[1], d_stg4[0], d_stg4[1]]
                    else:
                        pbufs = [PAN[0], PAN[1]]
                        psems = [d_stg[0], d_stg[1]]
                    npb = len(pbufs)
                pi = 0
                grp_mm = {}
                pan_cast = {}
                pan_mm = {}
                ps_free = {}
                for q in range(NQ):
                    for g in range(ngrp):
                        gpr = g % 2
                        prev = (q, g - 2) if g >= 2 else ((q - 1, ngrp - 2 + g) if q >= 1 else None)
                        if prev in grp_mm:
                            s.wait("sync", c_mm[0], grp_mm[prev])
                        s.emit("sync", lambda e, q=q, g=g, gpr=gpr: e.dma_start(
                            out=GRP[gpr][:], in_=grp_src_fn(q, g)), inc=d_grp[gpr], inc_n=16)
                        s.wait("tensor", d_grp[gpr])
                        for f in range(16):
                            pr = pi % 2
                            if bf16_pan:
                                pb = pi % npb
                                if pi >= npb:
                                    s.wait("sync", c_mm[0], pan_mm.get(pi - npb, 0))
                                s.emit("sync", lambda e, q=q, g=g, f=f, pb=pb: e.dma_start(
                                    out=pbufs[pb][:, 0:wid], in_=qg_panels[q][g][f]),
                                    inc=psems[pb], inc_n=16)
                                s.wait("tensor", psems[pb])
                            else:
                                if pi >= 2:
                                    s.wait("sync", c_pan[pr], pan_cast.get(pi - 2, 0))
                                s.emit("sync", lambda e, q=q, g=g, f=f, pr=pr: e.dma_start(
                                    out=STG[pr][:, 0:wid], in_=qg_panels[q][g][f]),
                                    inc=d_stg[pr], inc_n=16)
                                s.wait("gpsimd", d_stg[pr])
                                if pi >= 2:
                                    s.wait("gpsimd", c_mm[0], pan_mm.get(pi - 2, 0))
                                s.emit("gpsimd", lambda e, pr=pr: e.tensor_copy(
                                    out=PAN[pr][:, 0:wid], in_=STG[pr][:, 0:wid]),
                                    inc=c_pan[pr])
                                pan_cast[pi] = c_pan[pr].count
                                s.wait("tensor", c_pan[pr], pan_cast[pi])
                            if pi >= 2:
                                s.wait("tensor", c_evv, ps_free.get(pi - 2, 0))
                            ps = PQ[pi % 2]
                            pbuf = pbufs[pi % npb] if bf16_pan else PAN[pr]
                            for k in range(kc_g):
                                fn = lambda e, ps=ps, pbuf=pbuf, k=k, gpr=gpr: e.matmul(
                                    ps[:, :], pbuf[:, k * 128:(k + 1) * 128],
                                    GRP[gpr][:, k, :],
                                    start=(k == 0), stop=(k == kc_g - 1))
                                if k == kc_g - 1:
                                    s.emit("tensor", fn, inc=c_mm[0])
                                else:
                                    s.raw("tensor", fn)
                            pan_mm[pi] = c_mm[0].count
                            if side_fn is not None:
                                side_fn(pi)
                            s.wait("vector", c_mm[0], pan_mm[pi])
                            if g == 0:
                                if q > 0:
                                    s.wait("vector", d_acc)
                                s.emit("vector", lambda e, f=f, ps=ps: e.tensor_copy(
                                    out=ACC[:, f, :], in_=ps[:, :]), inc=c_evv)
                            else:
                                s.emit("vector", lambda e, f=f, ps=ps: e.tensor_add(
                                    out=ACC[:, f, :], in0=ACC[:, f, :], in1=ps[:, :]), inc=c_evv)
                            ps_free[pi] = c_evv.count
                            pi += 1
                        grp_mm[(q, g)] = c_mm[0].count
                    evac_q_fn(q)

        def bcast_const_row(scope, src_row_ap, out_tile, scale=1.0, extra=None):
            with ExitStack() as ps_sc:
                PSB = mkpsum(ps_sc, [128, 512])
                s.wait("tensor", c_evv)
                s.wait("tensor", c_eva)
                s.emit("tensor", lambda e: e.matmul(
                    PSB[:, 0:src_row_ap.shape[-1]], onesrow_f[:], src_row_ap,
                    start=True, stop=True), inc=c_pe)
                s.wait("scalar", c_pe)
                s.emit("scalar", lambda e: e.activation(
                    out=out_tile, in_=PSB[:, 0:src_row_ap.shape[-1]],
                    func=AF.Identity, scale=scale), inc=c_eva)

        side_jobs = ([("c", wout_t[g, f], woutbf[g, f], 2048)
                      for g in range(2) for f in range(16)]
                     + [("g", w1_t[i], 2048, 0) for i in range(KF)]
                     + [("g", w2_t[i], 2048, 1) for i in range(KF)]
                     + [("g", w3_t[g, f], 1408, 2) for g in range(4) for f in range(16)]
                     + [("c", wm2_t[g, f], wm2bf[g, f], 2048)
                        for g in range(4) for f in range(16)]
                     + [j for h in range(16) for j in
                        (("c", hwcat_t[h], hwbf[h, 0], 2048),
                         ("c", wkr_t[h], hwbf[h, 1], 2048))])
        N_WOUT_JOBS = 32
        N_GAMMA_END = 32 + 152
        side_i = [0]
        side_done = {}

        def side_pump(GSTb, SBTb, n=1):
            for _ in range(n):
                gi = side_i[0]
                if gi >= len(side_jobs):
                    return
                side_i[0] += 1
                job = side_jobs[gi]
                gpr = gi % 2
                if gi >= 2:
                    s.wait("sync", c_g, side_done[gi - 2])
                src = job[1]
                wid = job[2] if job[0] == "g" else job[3]
                s.emit("sync", lambda e, src=src, wid=wid, gpr=gpr: e.dma_start(
                    out=GSTb[gpr][:, 0:wid], in_=src), inc=d_g[gpr], inc_n=16)
                if job[0] == "g":
                    m = job[3]
                    s.wait("vector", d_g[gpr])
                    qw = wid // 4
                    for qq in range(4):
                        s.emit("vector", lambda e, qq=qq, qw=qw, gpr=gpr, GSTb=GSTb: e.tensor_reduce(
                            out=gtmp[:], in_=GSTb[gpr][:, qq * qw:(qq + 1) * qw],
                            axis=AX.X, op=ALU.add, apply_absolute_value=True), inc=c_evv)
                        s.wait("vector", c_evv)
                        fn = lambda e, m=m: e.tensor_add(
                            out=gacc[:, m:m + 1], in0=gacc[:, m:m + 1], in1=gtmp[:])
                        if qq == 3:
                            s.emit("vector", fn, inc=c_g)
                        else:
                            s.emit("vector", fn, inc=c_evv)
                        s.wait("vector", c_evv if qq < 3 else c_g)
                else:
                    dst = job[2]
                    ceng = "scalar" if gpr == 0 else "vector"
                    s.wait(ceng, d_g[gpr])
                    s.wait(ceng, d_side)
                    if ceng == "scalar":
                        s.emit("scalar", lambda e, gpr=gpr, GSTb=GSTb, SBTb=SBTb: e.activation(
                            out=SBTb[gpr][:], in_=GSTb[gpr][:], func=AF.Identity), inc=c_g)
                        s.wait("scalar", c_g)
                        s.emit("scalar", lambda e, dst=dst, gpr=gpr, SBTb=SBTb: e.dma_start(
                            out=dst, in_=SBTb[gpr][:]), inc=d_side, inc_n=16)
                    else:
                        s.emit("vector", lambda e, gpr=gpr, GSTb=GSTb, SBTb=SBTb: e.tensor_copy(
                            out=SBTb[gpr][:], in_=GSTb[gpr][:]), inc=c_g)
                        s.wait("scalar", c_g)
                        s.emit("scalar", lambda e, dst=dst, gpr=gpr, SBTb=SBTb: e.dma_start(
                            out=dst, in_=SBTb[gpr][:]), inc=d_side, inc_n=16)
                side_done[gi] = c_g.count

        xt_r = xt.rearrange("(j p) t -> p j t", p=128)
        with ExitStack() as scA:
            Bsig0 = mktile(scA, [128, T], BF16)
            Brho2 = mktile(scA, [128, T], BF16)

            with ExitStack() as sc:
                STGl = [mktile(sc, [128, 2048], F32) for _ in range(2)]
                Bmu, Brho, Bsig, scr = load_res_stats(
                    sc, [xt_r[:, j, :] for j in range(KD)], "A", STGl)
                s.wait("vector", c_eva)
                s.emit("vector", lambda e: e.tensor_copy(out=Bsig0[:], in_=Bsig[:]), inc=c_evv)
                center_res(Bmu, Brho, scr)

            s.fence()
            with ExitStack() as sc:
                XPS = [mktile(sc, [128, 3 + T], F32) for _ in range(2)]
                XC = mktile(sc, [128, T], F32)
                XCS = mktile(sc, [128, T], BF16)
                ZSIL = mktile(sc, [128, T], BF16)
                GST = [mktile(sc, [128, 2048], F32) for _ in range(2)]
                SBT = [mktile(sc, [128, 2048], BF16) for _ in range(2)]
                PSA = mkpsum(sc, [128, 2048])
                PSB = mkpsum(sc, [128, 2048])
                s.wait("gpsimd", c_pool)
                s.emit("gpsimd", lambda e: e.memset(XPS[0][:, 0:3], 0.0), inc=c_pool)
                s.emit("gpsimd", lambda e: e.memset(XPS[1][:, 0:3], 0.0), inc=c_pool)

                def pool_extra(i):
                    side_pump(GST, SBT, 2 if i % 2 == 0 else 1)

                state = {"xc_sil": 0, "ytt": 0, "xpcopy": {}, "zsil": 0}

                def evac_a1(i, ps, pr):
                    pair = i // 2
                    j = pair
                    if pr == 0:
                        s.wait("scalar", c_mm[0])
                        if pair >= 2:
                            s.wait("scalar", c_evv, state["xpcopy"].get(pair - 2, (0, 0))[1])
                        s.emit("scalar", lambda e, pair=pair: e.activation(
                            out=XPS[pair % 2][:, 3:3 + T], in_=ps[:, :],
                            func=AF.Identity), inc=c_eva)
                        free = (c_eva, c_eva.count)
                        s.wait("vector", c_eva)
                        if pair >= 1:
                            s.wait("vector", c_eva, state["xc_sil"])
                        s.raw("vector", lambda e, pair=pair, j=j: e.tensor_scalar(
                            out=XC[:], in0=XPS[pair % 2][:, 0:T],
                            scalar1=convw[:, j, 0:1], scalar2=0.0,
                            op0=ALU.mult, op1=ALU.add))
                        for k in range(1, 3):
                            s.raw("vector", lambda e, pair=pair, j=j, k=k: e.scalar_tensor_tensor(
                                out=XC[:], in0=XPS[pair % 2][:, k:k + T],
                                scalar=convw[:, j, k:k + 1], in1=XC[:],
                                op0=ALU.mult, op1=ALU.add))
                        s.emit("vector", lambda e, pair=pair, j=j: e.scalar_tensor_tensor(
                            out=XC[:], in0=XPS[pair % 2][:, 3:3 + T],
                            scalar=convw[:, j, 3:4], in1=XC[:],
                            op0=ALU.mult, op1=ALU.add), inc=c_evv)
                        state["xpcopy"][pair] = (c_evv, c_evv.count)
                        s.wait("scalar", c_evv)
                        if pair >= 1:
                            s.wait("scalar", c_evv, state["ytt"])
                        s.emit("scalar", lambda e, j=j: e.activation(
                            out=XCS[:], in_=XC[:], func=AF.Silu,
                            bias=convb[:, j:j + 1]), inc=c_eva)
                        state["xc_sil"] = c_eva.count
                        return free
                    else:
                        s.wait("scalar", c_mm[1])
                        if pair >= 1:
                            s.wait("scalar", c_evv, state["ytt"])
                        s.emit("scalar", lambda e: e.activation(
                            out=ZSIL[:], in_=ps[:, :], func=AF.Silu), inc=c_eva)
                        free = (c_eva, c_eva.count)
                        s.wait("vector", c_eva)
                        s.wait("vector", d_ev)
                        s.emit("vector", lambda e, pair=pair: e.tensor_mul(
                            out=EVB[pair % 2][:], in0=XCS[:], in1=ZSIL[:]), inc=c_evv)
                        state["ytt"] = c_evv.count
                        s.wait("scalar", c_evv, state["ytt"])
                        s.emit("scalar", lambda e, j=j, pair=pair: e.dma_start(
                            out=ysp[j], in_=EVB[pair % 2][:]), inc=d_ev, inc_n=16)
                        return free

                order = []
                for j in range(32):
                    order.append(win_t[j])
                    order.append(win_t[j + 32])
                STGl = [mktile(sc, [128, 2048], F32) for _ in range(2)]
                PANl = [mktile(sc, [128, 2048], BF16) for _ in range(2)]
                stream_panels(order, 16, PSA, PSB, evac_a1, STG=STGl, PAN=PANl,
                              pool_extra=pool_extra)

            s.fence()
            with ExitStack() as sc:
                GRP = [mktile(sc, [128, 16, 512], BF16) for _ in range(2)]
                ACC = mktile(sc, [128, 16, 512], F32)
                T1 = mktile(sc, [128, 512], BF16)
                SQ2 = T1
                SR = mktile(sc, [1, T], F32)
                rho2_row_done = [0]

                ps_q2 = mkpsum(sc, [1, T])

                def grp_src(q, g):
                    return ysp[g * 16:(g + 1) * 16, :, q * 512:(q + 1) * 512].rearrange(
                        "c p t -> p c t")

                def evac_a2(q):
                    s.wait("vector", c_evv)
                    for j in range(KD):
                        s.wait("vector", c_pe)
                        s.emit("vector", lambda e, j=j, q=q, T1=T1: e.tensor_mul(
                            out=T1[:], in0=RES[:, j, q * 512:(q + 1) * 512],
                            in1=Bsig0[:, q * 512:(q + 1) * 512]), inc=c_evv)
                        s.emit("vector", lambda e, j=j, q=q, T1=T1, ACC=ACC: e.tensor_add(
                            out=RES[:, j, q * 512:(q + 1) * 512], in0=T1[:],
                            in1=ACC[:, j, :]), inc=c_evv)
                        x1_done = c_evv.count
                        s.wait("scalar", c_evv, x1_done)
                        if j >= 1 or q >= 1:
                            s.wait("scalar", c_pe)
                        s.emit("scalar", lambda e, j=j, q=q, SQ2=SQ2: e.activation(
                            out=SQ2[:], in_=RES[:, j, q * 512:(q + 1) * 512],
                            func=AF.Square), inc=c_eva)
                        s.wait("tensor", c_eva)
                        fn = lambda e, j=j, q=q, SQ2=SQ2, ps_q2=ps_q2: e.matmul(
                            ps_q2[:, q * 512:(q + 1) * 512], onescol[:], SQ2[:],
                            start=(j == 0), stop=(j == KD - 1))
                        s.emit("tensor", fn, inc=c_pe)

                GST2 = [mktile(sc, [128, 2048], F32) for _ in range(2)]
                _sbt2 = mktile(sc, [128, 2048], BF16)
                SBT2 = [_sbt2, _sbt2]

                def side_a2(pi):
                    if pi % 8 < 4:
                        side_pump(GST2, SBT2, 1)

                panels = [[[woutbf[g, f] for f in range(16)] for g in range(2)]
                          for q in range(NQ)]
                PANl = [mktile(sc, [128, 2048], BF16) for _ in range(2)]
                acc_stage(panels, 16, 2, grp_src, ACC, GRP, evac_a2, PAN=PANl,
                          bf16_pan=True, side_fn=side_a2, deep=False)
                s.wait("vector", c_pe)
                s.emit("vector", lambda e, SR=SR, ps_q2=ps_q2: e.tensor_copy(out=SR[:], in_=ps_q2[:, :]), inc=c_evv)
                with ExitStack() as ps_sc:
                    PSB2 = mkpsum(ps_sc, [128, 2048])
                    s.wait("tensor", c_evv)
                    for q in range(NQ):
                        fn = lambda e, q=q, PSB2=PSB2, SR=SR: e.matmul(
                            PSB2[:, q * 512:(q + 1) * 512], onesrow_f[:],
                            SR[:, q * 512:(q + 1) * 512], start=True, stop=True)
                        s.emit("tensor", fn, inc=c_pe) if q == NQ - 1 else s.raw("tensor", fn)
                    s.wait("scalar", c_pe)
                    s.emit("scalar", lambda e, PSB2=PSB2: e.activation(
                        out=Brho2[:], in_=PSB2[:, :], func=AF.Ln), inc=c_eva)
                    s.emit("scalar", lambda e: e.activation(
                        out=Brho2[:], in_=Brho2[:], func=AF.Exp, scale=-0.5), inc=c_eva)
                    s.emit("scalar", lambda e: e.activation(
                        out=Brho2[:], in_=Brho2[:], func=AF.Identity,
                        scale=float(np.sqrt(D))), inc=c_eva)

            s.fence()

            with ExitStack() as sc:
                PSA = mkpsum(sc, [128, 2048])
                PSB = mkpsum(sc, [128, 2048])

                GST3 = [mktile(sc, [128, 2048], F32) for _ in range(2)]
                SBT3 = [mktile(sc, [128, 2048], BF16) for _ in range(2)]

                def pool_extra_a3(i):
                    side_pump(GST3, SBT3, 2)

                def evac_a3(i, ps, pr):
                    s.wait("scalar", c_mm[pr])
                    s.wait("scalar", d_ev)
                    s.emit("scalar", lambda e, i=i, ps=ps: e.activation(
                        out=EVB[i % 2][:], in_=ps[:, :], func=AF.Relu), inc=c_eva)
                    free = (c_eva, c_eva.count)
                    s.wait("vector", c_eva)
                    s.emit("vector", lambda e, i=i: e.tensor_mul(
                        out=EVB[i % 2][:], in0=EVB[i % 2][:], in1=Brho2[:]), inc=c_evv)
                    s.wait("scalar", c_evv)
                    s.emit("scalar", lambda e, i=i: e.dma_start(
                        out=usp[i], in_=EVB[i % 2][:]), inc=d_ev, inc_n=16)
                    return free

                STGl = [mktile(sc, [128, 2048], F32) for _ in range(2)]
                PANl = [mktile(sc, [128, 2048], BF16) for _ in range(2)]
                stream_panels([wm1_t[i] for i in range(64)], 16, PSA, PSB, evac_a3,
                              STG=STGl, PAN=PANl, pool_extra=pool_extra_a3)
                while side_i[0] < len(side_jobs):
                    side_pump(GST3, SBT3, 1)

            s.fence()
            with ExitStack() as sc:
                GRP = [mktile(sc, [128, 16, 512], BF16) for _ in range(2)]
                ACC = mktile(sc, [128, 16, 512], F32)
                XB = mktile(sc, [128, 4, 512], F32)

                def grp_src(q, g):
                    return usp[g * 16:(g + 1) * 16, :, q * 512:(q + 1) * 512].rearrange(
                        "c p t -> p c t")

                def evac_a4(q):
                    add_c = {}
                    for j in range(KD):
                        xslot = j % 4
                        if j >= 4:
                            s.wait("sync", c_evv, add_c[j - 4])
                        s.emit("sync", lambda e, j=j, q=q, xslot=xslot, XB=XB: e.dma_start(
                            out=XB[:, xslot, :], in_=xt_r[:, j, q * 512:(q + 1) * 512]),
                            inc=d_xb4[xslot], inc_n=16)
                        s.wait("vector", d_xb4[xslot])
                        s.emit("vector", lambda e, j=j, xslot=xslot, ACC=ACC, XB=XB: e.tensor_add(
                            out=ACC[:, j, :], in0=ACC[:, j, :], in1=XB[:, xslot, :]),
                            inc=c_evv)
                        s.emit("vector", lambda e, j=j, q=q, ACC=ACC: e.tensor_add(
                            out=ACC[:, j, :], in0=ACC[:, j, :],
                            in1=RES[:, j, q * 512:(q + 1) * 512]), inc=c_evv)
                        add_c[j] = c_evv.count
                        s.wait("scalar", c_evv, add_c[j])
                        s.emit("scalar", lambda e, j=j, q=q, ACC=ACC: e.dma_start(
                            out=Rm[j][:, q * 512:(q + 1) * 512], in_=ACC[:, j, :]),
                            inc=d_acc, inc_n=16)

                assert side_i[0] >= 0
                panels = [[[wm2bf[g, f] for f in range(16)] for g in range(4)]
                          for q in range(NQ)]
                PANl = [mktile(sc, [128, 2048], BF16) for _ in range(2)]
                acc_stage(panels, 16, 4, grp_src, ACC, GRP, evac_a4, PAN=PANl,
                          bf16_pan=True)

        s.fence()
        if stop_after == "A":
            emit_tail(nc, s, Rm, outT)
            s.build()
            return nc

        with ExitStack() as sc:
            SG = mktile(sc, [1, 512], F32)
            with ExitStack() as ps_sc:
                PSG = mkpsum(ps_sc, [1, 512])
                onescol_f = mktile(sc, [128, 1], F32)
                s.emit("sync", lambda e: e.dma_start(
                    out=onescol_f[:], in_=onescolf_c.ap()), inc=d_misc, inc_n=16)
                s.wait("tensor", d_misc)
                s.wait("tensor", c_g)
                s.emit("tensor", lambda e: e.matmul(
                    PSG[:, 0:4], onescol_f[:], gacc[:], start=True, stop=True), inc=c_pe)
                s.wait("scalar", c_pe)
                s.emit("scalar", lambda e: e.activation(
                    out=SG[:, 0:4], in_=PSG[:, 0:4], func=AF.Identity,
                    scale=0.5 / FFN_CNT, bias=epsc[0:1, 1:2]), inc=c_eva)
            with ExitStack() as ps_sc:
                PSG2 = mkpsum(ps_sc, [128, 512])
                s.wait("tensor", c_eva)
                s.emit("tensor", lambda e: e.matmul(
                    PSG2[:, 0:4], onesrow_f[:], SG[:, 0:4], start=True, stop=True), inc=c_pe)
                s.wait("scalar", c_pe)
                s.emit("scalar", lambda e: e.activation(
                    out=thrv[:], in_=PSG2[:, 0:4], func=AF.Identity), inc=c_eva)
                s.emit("scalar", lambda e: e.activation(
                    out=nthrv[:], in_=PSG2[:, 0:4], func=AF.Identity, scale=-1.0), inc=c_eva)
        s.fence()

        def emit_ffn(app, final_to_out):
            with ExitStack() as scF:
                with ExitStack() as sc:
                    STGl = [mktile(sc, [128, 2048], F32) for _ in range(2)]
                    Bmu, Bs, scr = load_res_stats(
                        sc, [Rm[j] for j in range(KD)], "LN", STGl)
                    center_res(Bmu, Bs, scr)
                    if hsp is not None and app == 0:
                        s.wait("sync", c_evv)
                        for j in range(KD):
                            s.emit("sync", lambda e, j=j: e.dma_start(
                                out=hsp[j], in_=RES[:, j, :]), inc=d_stv, inc_n=16)
                s.fence()

                with ExitStack() as sc:
                    PSA = mkpsum(sc, [128, 2048])
                    PSB = mkpsum(sc, [128, 2048])
                    if app == 0:
                        QA = mktile(sc, [128, 2048], F32)
                        QB = mktile(sc, [128, 2048], F32)
                        WQS = [mktile(sc, [128, 1408], F32) for _ in range(2)]
                        WQB = mktile(sc, [128, 1408], BF16)
                        w3cur = [0]
                        w3q_done = []

                        def pool_extra_w3(i):
                            gi = w3cur[0]
                            if gi >= 64:
                                return
                            w3cur[0] += 1
                            gpr = gi % 2
                            g, f = gi // 16, gi % 16
                            if gi >= 2:
                                s.wait("sync", c_g, w3q_done[gi - 2])
                            s.emit("sync", lambda e, g=g, f=f, gpr=gpr: e.dma_start(
                                out=WQS[gpr][:], in_=w3_t[g, f]), inc=d_g[gpr], inc_n=16)
                            s.wait("vector", d_g[gpr])
                            s.wait("vector", d_stv)
                            s.raw("vector", lambda e, gpr=gpr: e.tensor_scalar(
                                out=QA[:, 0:1408], in0=WQS[gpr][:],
                                scalar1=thrv[:, 2:3], scalar2=1.0,
                                op0=ALU.is_gt, op1=ALU.mult))
                            s.raw("vector", lambda e, gpr=gpr: e.tensor_scalar(
                                out=QB[:, 0:1408], in0=WQS[gpr][:],
                                scalar1=nthrv[:, 2:3], scalar2=1.0,
                                op0=ALU.is_ge, op1=ALU.mult))
                            s.emit("vector", lambda e: e.scalar_tensor_tensor(
                                out=WQB[:], in0=QA[:, 0:1408], scalar=-1.0,
                                in1=QB[:, 0:1408], op0=ALU.add, op1=ALU.add), inc=c_g)
                            w3q_done.append(c_g.count)
                            s.wait("sync", c_g, w3q_done[-1])
                            s.emit("sync", lambda e, g=g, f=f: e.dma_start(
                                out=wq3sp[g, f], in_=WQB[:]), inc=d_stv, inc_n=16)
                    else:
                        QA = QB = None
                        pool_extra_w3 = None

                    def evac_f1a(i, ps, pr):
                        s.wait("scalar", c_mm[pr])
                        s.wait("scalar", d_sta)
                        s.wait("scalar", d_ev)
                        s.emit("scalar", lambda e, i=i, ps=ps: e.activation(
                            out=EVB[i % 2][:], in_=ps[:, :], func=AF.Sigmoid), inc=c_eva)
                        free = (c_eva, c_eva.count)
                        s.wait("scalar", c_eva)
                        s.emit("scalar", lambda e, i=i: e.dma_start(
                            out=sgsp[i], in_=EVB[i % 2][:]), inc=d_sta, inc_n=16)
                        return free

                    if app == 0:
                        STGl = [mktile(sc, [128, 2048], F32) for _ in range(2)]
                        PANl = [mktile(sc, [128, 2048], BF16) for _ in range(2)]
                        stream_panels([w1_t[i] for i in range(KF)], 16, PSA, PSB,
                                      evac_f1a, STG=STGl, PAN=PANl, quant=(0, QA, QB),
                                      pool_extra=pool_extra_w3,
                                      spills=[wq1sp[i] for i in range(KF)])
                    else:
                        PANl = [mktile(sc, [128, 2048], BF16) for _ in range(2)]
                        stream_panels([wq1sp[i] for i in range(KF)], 16, PSA, PSB,
                                      evac_f1a, PAN=PANl, bf16_src=True)

                    SGB = [mktile(sc, [128, 2048], BF16) for _ in range(2)]
                    sg_ld = {}

                    def evac_f1b(i, ps, pr):
                        s.wait("sync", c_evv)
                        s.emit("sync", lambda e, i=i: e.dma_start(
                            out=SGB[i % 2][:], in_=sgsp[i]), inc=d_grp[i % 2], inc_n=16)
                        s.wait("vector", d_grp[i % 2])
                        s.wait("vector", c_mm[pr])
                        s.wait("vector", d_ev)
                        s.wait("vector", d_sta)
                        s.emit("vector", lambda e, i=i, ps=ps: e.tensor_mul(
                            out=EVB[i % 2][:], in0=SGB[i % 2][:], in1=ps[:, :]), inc=c_evv)
                        free = (c_evv, c_evv.count)
                        s.wait("scalar", c_evv)
                        s.emit("scalar", lambda e, i=i: e.dma_start(
                            out=hidsp[i], in_=EVB[i % 2][:]), inc=d_ev, inc_n=16)
                        return free

                    if app == 0:
                        STGl = [mktile(sc, [128, 2048], F32) for _ in range(2)]
                        PANl = [mktile(sc, [128, 2048], BF16) for _ in range(2)]
                        stream_panels([w2_t[i] for i in range(KF)], 16, PSA, PSB,
                                      evac_f1b, STG=STGl, PAN=PANl, quant=(1, QA, QB),
                                      pool_extra=pool_extra_w3,
                                      spills=[wq2sp[i] for i in range(KF)])
                    else:
                        PANl = [mktile(sc, [128, 2048], BF16) for _ in range(2)]
                        stream_panels([wq2sp[i] for i in range(KF)], 16, PSA, PSB,
                                      evac_f1b, PAN=PANl, bf16_src=True)
                s.fence()

                with ExitStack() as sc:
                    GRP = [mktile(sc, [128, 11, 512], BF16) for _ in range(2)]
                    ACC = mktile(sc, [128, 16, 512], F32)
                    RB = mktile(sc, [128, 4, 512], F32)

                    def grp_src(q, g):
                        return hidsp[g * 11:(g + 1) * 11, :, q * 512:(q + 1) * 512].rearrange(
                            "c p t -> p c t")

                    def evac_f2(q):
                        add_c = {}
                        for j in range(KD):
                            xslot = j % 4
                            if j >= 4:
                                s.wait("sync", c_evv, add_c[j - 4])
                            s.emit("sync", lambda e, j=j, q=q, xslot=xslot: e.dma_start(
                                out=RB[:, xslot, :], in_=Rm[j][:, q * 512:(q + 1) * 512]),
                                inc=d_xb4[xslot], inc_n=16)
                            s.wait("vector", d_xb4[xslot])
                            s.emit("vector", lambda e, j=j, xslot=xslot: e.tensor_add(
                                out=ACC[:, j, :], in0=ACC[:, j, :], in1=RB[:, xslot, :]),
                                inc=c_evv)
                            add_c[j] = c_evv.count
                            s.wait("scalar", c_evv, add_c[j])
                            dst = outT if final_to_out else Rm
                            s.emit("scalar", lambda e, j=j, q=q, dst=dst: e.dma_start(
                                out=dst[j][:, q * 512:(q + 1) * 512], in_=ACC[:, j, :]),
                                inc=d_acc, inc_n=16)

                    panels = [[[wq3sp[g, f] for f in range(16)] for g in range(4)]
                              for q in range(NQ)]
                    PANl = [mktile(sc, [128, 1408], BF16) for _ in range(2)]
                    acc_stage(panels, 11, 4, grp_src, ACC, GRP, evac_f2, PAN=PANl,
                              bf16_pan=True)
            s.fence()

        emit_ffn(0, False)
        if stop_after == "F1":
            emit_tail(nc, s, Rm, outT)
            s.build()
            return nc

        with ExitStack() as scB:
            ckv = mktile(scB, [128, 4, T], BF16)
            cq = mktile(scB, [128, 8, T], BF16)

            with ExitStack() as sc:
                STGl = [mktile(sc, [128, 2048], F32) for _ in range(2)]
                Bmu, Bs, scr = load_res_stats(sc, [Rm[j] for j in range(KD)], "LN", STGl)
                center_res(Bmu, Bs, scr)
            s.fence()

            with ExitStack() as sc:
                PSA = mkpsum(sc, [128, 2048])
                PSB = mkpsum(sc, [128, 2048])

                def evac_ckv(i, ps, pr):
                    s.wait("scalar", c_mm[pr])
                    s.emit("scalar", lambda e, i=i, ps=ps: e.activation(
                        out=ckv[:, i, :], in_=ps[:, :], func=AF.Identity,
                        bias=bdkv[:, i:i + 1]), inc=c_eva)
                    return (c_eva, c_eva.count)

                def evac_cq(i, ps, pr):
                    s.wait("scalar", c_mm[pr])
                    s.emit("scalar", lambda e, i=i, ps=ps: e.activation(
                        out=cq[:, i, :], in_=ps[:, :], func=AF.Identity,
                        bias=bdq[:, i:i + 1]), inc=c_eva)
                    return (c_eva, c_eva.count)

                STGl = [mktile(sc, [128, 2048], F32) for _ in range(2)]
                PANl = [mktile(sc, [128, 2048], BF16) for _ in range(2)]
                stream_panels([wdkv_t[i] for i in range(4)], 16, PSA, PSB, evac_ckv,
                              STG=STGl, PAN=PANl)
                s.wait("tensor", c_eva)
                stream_panels([wdq_t[i] for i in range(8)], 16, PSA, PSB, evac_cq,
                              STG=STGl, PAN=PANl)
            s.fence()

            with ExitStack() as sc:
                kCb = [mktile(sc, [128, T], BF16) for _ in range(2)]
                kRb = [mktile(sc, [128, T], BF16) for _ in range(2)]
                qCb = [mktile(sc, [128, T], BF16) for _ in range(2)]
                krp = qCb
                qRb = [mktile(sc, [128, T], BF16) for _ in range(2)]
                Vtmb = [mktile(sc, [128, 16, 132], BF16) for _ in range(2)]
                rsc = mktile(sc, [128, 512], BF16)
                ptile_full = mktile(sc, [128, T], BF16)
                pTs_full = mktile(sc, [128, 1024], BF16)
                otm = mktile(sc, [128, 128], BF16)
                ropeC = mktile(sc, [128, T], BF16)
                ropeS = mktile(sc, [128, T], BF16)
                tril = mktile(sc, [128, 128], F32)
                rows = mktile(sc, [128, 8], F32)
                HP = [mktile(sc, [128, 4096], BF16) for _ in range(2)]

                dbgt = None
                PBLD = mkpsum(sc, [128, 1024])
                PSF = mkpsum(sc, [128, 2048])
                PO = mkpsum(sc, [128, 512])
                PT = mkpsum(sc, [128, 1024], BF16)

                s.emit("sync", lambda e: e.dma_start(out=ropeC[:], in_=ropeC_c.ap()),
                       inc=d_misc, inc_n=16)
                s.emit("sync", lambda e: e.dma_start(out=ropeS[:], in_=ropeS_c.ap()),
                       inc=d_misc, inc_n=16)
                s.emit("sync", lambda e: e.dma_start(out=tril[:], in_=tril_c.ap()),
                       inc=d_misc, inc_n=16)
                s.wait("vector", d_misc)
                s.wait("tensor", d_misc)

                build_done = {}

                def load_head(h, prev_attend_pe):
                    pr = h % 2
                    if h >= 2:
                        s.wait("sync", c_pe, build_done[h - 2])
                    s.emit("sync", lambda e, h=h, pr=pr: e.dma_start(
                        out=HP[pr][:, 0:2048], in_=hwbf[h, 0]), inc=d_hw[pr], inc_n=16)
                    s.emit("sync", lambda e, h=h, pr=pr: e.dma_start(
                        out=HP[pr][:, 2048:4096], in_=hwbf[h, 1]), inc=d_hw[pr], inc_n=16)

                def build_head_gen(h, prev_ropeq_evv, out):
                    pr = h % 2
                    W = HP[pr]
                    s.wait("tensor", d_hw[pr])
                    if prev_ropeq_evv:
                        s.wait("tensor", c_evv, prev_ropeq_evv)

                    def proj_gen(dstbuf, w0, nc_chunks, rhs, biascol, scale=1.0):
                        evs = []
                        for t in range(NQ):
                            half = (t % 2) * 512
                            if t >= 2:
                                s.wait("tensor", c_eva, evs[t - 2])
                            for c in range(nc_chunks):
                                fn = lambda e, t=t, c=c, half=half, w0=w0, rhs=rhs: e.matmul(
                                    PBLD[:, half:half + 512],
                                    W[:, w0 + c * 128:w0 + (c + 1) * 128],
                                    rhs[:, c, t * 512:(t + 1) * 512],
                                    start=(c == 0), stop=(c == nc_chunks - 1))
                                if c == nc_chunks - 1:
                                    s.emit("tensor", fn, inc=c_pe)
                                else:
                                    s.raw("tensor", fn)
                            s.wait("scalar", c_pe)
                            s.emit("scalar", lambda e, t=t, half=half, dstbuf=dstbuf,
                                   biascol=biascol, scale=scale: e.activation(
                                out=dstbuf[:, t * 512:(t + 1) * 512],
                                in_=PBLD[:, half:half + 512], func=AF.Identity,
                                scale=scale, bias=biascol), inc=c_eva)
                            evs.append(c_eva.count)
                            yield
                        return evs

                    def rope_gen(srcbuf, dstbuf):
                        s.wait("tensor", c_eva)
                        done = 0
                        for t in range(NQ):
                            half = (t % 2) * 512
                            sl = slice(t * 512, (t + 1) * 512)
                            if t >= 2:
                                s.wait("tensor", c_evv, done)
                            s.emit("tensor", lambda e, sl=sl, half=half, srcbuf=srcbuf: e.matmul(
                                PBLD[:, half:half + 512], swapP_sb[:], srcbuf[:, sl],
                                start=True, stop=True), inc=c_pe)
                            sw = c_pe.count
                            s.wait("vector", c_pe, sw)
                            s.raw("vector", lambda e, sl=sl, srcbuf=srcbuf, dstbuf=dstbuf: e.tensor_mul(
                                out=dstbuf[:, sl], in0=srcbuf[:, sl], in1=ropeC[:, sl]))
                            s.raw("vector", lambda e, sl=sl, half=half: e.tensor_mul(
                                out=rsc[:, 0:512], in0=PBLD[:, half:half + 512],
                                in1=ropeS[:, sl]))
                            s.emit("vector", lambda e, sl=sl, dstbuf=dstbuf: e.tensor_add(
                                out=dstbuf[:, sl], in0=dstbuf[:, sl], in1=rsc[:, 0:512]),
                                inc=c_evv)
                            done = c_evv.count
                            yield
                        return done

                    yield from proj_gen(kCb[pr], 0, 4, ckv, buk[:, h:h + 1])
                    yield from proj_gen(krp[pr], 2048, KD, RES, bkr[:, h:h + 1])
                    rope_k_done = yield from rope_gen(krp[pr], kRb[pr])
                    s.wait("tensor", c_evv, rope_k_done)
                    for g in range(2):
                        if g == 1:
                            s.wait("tensor", c_eva)
                        for ktl in range(8):
                            kt = g * 8 + ktl
                            for c in range(4):
                                s.raw("tensor", lambda e, kt=kt, ktl=ktl, c=c, W=W: e.matmul(
                                    PBLD[:, ktl * 128:(ktl + 1) * 128],
                                    ckv[:, c, kt * 128:(kt + 1) * 128],
                                    W[:, 512 + c * 128:512 + (c + 1) * 128],
                                    start=(c == 0), stop=False))
                            fn = lambda e, kt=kt, ktl=ktl, h=h: e.matmul(
                                PBLD[:, ktl * 128:(ktl + 1) * 128], onesrow_f[:],
                                buv_row[0:1, h * 128:(h + 1) * 128], start=False, stop=True)
                            if ktl == 7:
                                s.emit("tensor", fn, inc=c_pe)
                            else:
                                s.raw("tensor", fn)
                        s.wait("scalar", c_pe)
                        s.emit("scalar", lambda e, g=g, pr=pr: e.activation(
                            out=Vtmb[pr][:, g * 8:(g + 1) * 8, 0:128],
                            in_=PBLD[:, :].rearrange("p (a b) -> p a b", b=128),
                            func=AF.Identity), inc=c_eva)
                        yield
                    v_done = c_eva.count
                    s.wait("tensor", c_eva, v_done)
                    s.wait("scalar", c_evv, rope_k_done)
                    yield from proj_gen(qCb[pr], 1024, 8, cq, buq16[:, h:h + 1],
                                        scale=1.0 / 16.0)
                    ready = yield from rope_gen(qCb[pr], qRb[pr])
                    build_done[h] = c_pe.count
                    out["ready"] = ready

                def attend_head(h, ready_c, gen):
                    pr = h % 2
                    kC, kR, qC, qR, Vtm = kCb[pr], kRb[pr], qCb[pr], qRb[pr], Vtmb[pr]
                    s.wait("tensor", c_evv, ready_c)
                    s.wait("tensor", c_eva)
                    psf_free = 0
                    pt_copy = 0
                    pt2_read = 0
                    po_read = 0
                    for qi in range(16):
                        nk = (qi + 1) * 128
                        nsl = (nk + 511) // 512
                        if psf_free:
                            s.wait("tensor", c_eva, psf_free)
                        for sl in range(nsl):
                            w = min(512, nk - sl * 512)
                            s.raw("tensor", lambda e, sl=sl, w=w, qi=qi, qC=qC, kC=kC: e.matmul(
                                PSF[:, sl * 512:sl * 512 + w],
                                qC[:, qi * 128:(qi + 1) * 128],
                                kC[:, sl * 512:sl * 512 + w], start=True, stop=False))
                            fn = lambda e, sl=sl, w=w, qi=qi, qR=qR, kR=kR: e.matmul(
                                PSF[:, sl * 512:sl * 512 + w],
                                qR[:, qi * 128:(qi + 1) * 128],
                                kR[:, sl * 512:sl * 512 + w], start=False, stop=True)
                            if sl == nsl - 1:
                                s.emit("tensor", fn, inc=c_pe)
                            else:
                                s.raw("tensor", fn)
                        sc_done = c_pe.count
                        if gen is not None:
                            next(gen, None)
                        s.wait("vector", c_pe, sc_done)
                        s.emit("vector", lambda e, nk=nk: e.tensor_add(
                            out=PSF[:, nk - 128:nk], in0=PSF[:, nk - 128:nk],
                            in1=tril[:]), inc=c_evv)
                        s.emit("vector", lambda e, nk=nk: e.reduce_max(
                            out=rows[:, 4:5], in_=PSF[:, 0:nk], axis=AX.X,
                            negate=True), inc=c_evv)
                        s.wait("vector", c_evv)
                        nm_done = c_evv.count
                        s.wait("scalar", c_evv, nm_done)
                        if pt_copy:
                            pass
                        s.emit("scalar", lambda e, nk=nk: e.activation(
                            out=ptile_full[:, 0:nk], in_=PSF[:, 0:nk], func=AF.Exp,
                            bias=rows[:, 4:5]), inc=c_eva)
                        psf_free = c_eva.count
                        exp_done = c_eva.count
                        nkt = nk // 128
                        first_pv = True
                        for g0 in range(0, nkt, 8):
                            gn = min(8, nkt - g0)
                            s.wait("tensor", c_eva, exp_done)
                            if pt_copy:
                                s.wait("tensor", c_evv, pt_copy)
                            if pt2_read:
                                s.wait("tensor", c_eva, pt2_read)
                            for ktl in range(gn):
                                fn = lambda e, g0=g0, ktl=ktl: e.transpose(
                                    PT[:, ktl * 128:(ktl + 1) * 128],
                                    ptile_full[:, (g0 + ktl) * 128:(g0 + ktl + 1) * 128],
                                    ident[:])
                                if ktl == gn - 1:
                                    s.emit("tensor", fn, inc=c_pe)
                                else:
                                    s.raw("tensor", fn)
                            tr_done = c_pe.count
                            s.wait("vector", c_pe, tr_done)
                            if first_pv and po_read:
                                pass
                            s.emit("vector", lambda e, gn=gn: e.tensor_copy(
                                out=pTs_full[:, 0:gn * 128], in_=PT[:, 0:gn * 128]),
                                inc=c_evv)
                            pt_copy = c_evv.count
                            s.wait("tensor", c_evv, pt_copy)
                            if first_pv and po_read:
                                s.wait("tensor", c_evv, po_read)
                            for ktl in range(gn):
                                kt = g0 + ktl
                                fn = lambda e, kt=kt, ktl=ktl, nkt=nkt, Vtm=Vtm: e.matmul(
                                    PO[:, 0:129], pTs_full[:, ktl * 128:(ktl + 1) * 128],
                                    Vtm[:, kt, 0:129],
                                    start=(kt == 0), stop=(kt == nkt - 1))
                                if ktl == gn - 1:
                                    s.emit("tensor", fn, inc=c_pe)
                                else:
                                    s.raw("tensor", fn)
                            first_pv = False
                        pv_done = c_pe.count
                        s.wait("vector", c_pe, pv_done)
                        s.emit("vector", lambda e: e.reciprocal(
                            out=rows[:, 3:4], in_=PO[:, 128:129]), inc=c_evv)
                        s.wait("vector", c_evv)
                        s.emit("vector", lambda e: e.tensor_scalar(
                            out=otm[:], in0=PO[:, 0:128], scalar1=rows[:, 3:4],
                            scalar2=1.0, op0=ALU.mult, op1=ALU.mult), inc=c_evv)
                        po_read = c_evv.count
                        fin = c_evv.count
                        s.wait("tensor", c_evv, fin)
                        if pt_copy:
                            s.wait("tensor", c_evv, pt_copy)
                        if pt2_read:
                            s.wait("tensor", c_eva, pt2_read)
                        s.emit("tensor", lambda e: e.transpose(
                            PT[:, 0:128], otm[:], ident[:]), inc=c_pe)
                        ot_done = c_pe.count
                        s.wait("scalar", c_pe, ot_done)
                        s.wait("scalar", d_ev)
                        s.emit("scalar", lambda e, qi=qi, pr=pr: e.activation(
                            out=EVB[pr][:, qi * 128:(qi + 1) * 128], in_=PT[:, 0:128],
                            func=AF.Identity), inc=c_eva)
                        pt2_read = c_eva.count
                        if gen is not None:
                            next(gen, None)
                    s.wait("scalar", c_eva)
                    s.emit("scalar", lambda e, h=h, pr=pr: e.dma_start(
                        out=osp[h], in_=EVB[pr][:]), inc=d_ev, inc_n=16)
                    return c_pe.count

                swapP_sb = mktile(sc, [128, 128], BF16)
                s.emit("sync", lambda e: e.dma_start(out=swapP_sb[:], in_=swapP_c.ap()),
                       inc=d_misc, inc_n=16)
                s.wait("tensor", d_misc)
                s.wait("vector", d_misc)
                s.emit("vector", lambda e: e.memset(Vtmb[0][:, :, 128:129], 1.0), inc=c_evv)
                s.emit("vector", lambda e: e.memset(Vtmb[1][:, :, 128:129], 1.0), inc=c_evv)
                s.wait("tensor", c_evv)

                att_pe = 0
                load_head(0, 0)
                bst = {}
                g0 = build_head_gen(0, 0, bst)
                for _ in g0:
                    pass
                ready = bst["ready"]
                prev_ropeq = ready
                for h in range(1, H):
                    load_head(h, att_pe)
                    bst = {}
                    gh = build_head_gen(h, prev_ropeq, bst)
                    att_pe = attend_head(h - 1, ready, gh)
                    for _ in gh:
                        pass
                    ready = bst["ready"]
                    prev_ropeq = ready
                att_pe = attend_head(H - 1, ready, None)
            s.fence()

            with ExitStack() as sc:
                EVF = [mktile(sc, [128, 2048], F32) for _ in range(2)]
                RST = [mktile(sc, [128, 2048], F32) for _ in range(2)]
                PSA = mkpsum(sc, [128, 2048])
                PSB = mkpsum(sc, [128, 2048])
                for j in range(16):
                    s.emit("sync", lambda e, j=j: e.dma_start(
                        out=RES[:, j, :], in_=osp[j]), inc=d_res, inc_n=16)
                s.wait("tensor", d_res)
                s.wait("gpsimd", d_res)

                def evac_b3(i, ps, pr):
                    s.wait("sync", c_evv)
                    s.emit("sync", lambda e, i=i, pr=pr: e.dma_start(
                        out=RST[pr][:], in_=Rm[i]), inc=d_xb, inc_n=16)
                    s.wait("scalar", c_mm[pr])
                    s.wait("scalar", c_evv)
                    s.emit("scalar", lambda e, i=i, ps=ps, pr=pr: e.activation(
                        out=EVF[pr][:], in_=ps[:, :], func=AF.Identity,
                        bias=bo[:, i:i + 1]), inc=c_eva)
                    free = (c_eva, c_eva.count)
                    s.wait("vector", c_eva)
                    s.wait("vector", d_xb)
                    s.wait("vector", d_acc)
                    s.emit("vector", lambda e, pr=pr: e.tensor_add(
                        out=EVF[pr][:], in0=EVF[pr][:], in1=RST[pr][:]), inc=c_evv)
                    s.wait("scalar", c_evv)
                    s.emit("scalar", lambda e, i=i, pr=pr: e.dma_start(
                        out=Rm[i], in_=EVF[pr][:]), inc=d_acc, inc_n=16)
                    return free

                STGl = [mktile(sc, [128, 2048], F32) for _ in range(2)]
                PANl = [mktile(sc, [128, 2048], BF16) for _ in range(2)]
                stream_panels([wo_t[i] for i in range(16)], 16, PSA, PSB, evac_b3,
                              STG=STGl, PAN=PANl)
        s.fence()
        if stop_after == "B":
            emit_tail(nc, s, Rm, outT)
            s.build()
            return nc

        emit_ffn(1, True)
        for name, c in list(s.sems.items()):
            if name.startswith("dma_"):
                s.wait("sync", c)
        s.build()
    return nc


def emit_tail(nc, s, Rm, outT, STG=None):
    d_tail = s.sem("dma_tail")
    if STG is None:
        STG = [s.st.enter_context(nc.sbuf_tensor(f"tailstg{i}", [128, 2048], F32))
               for i in range(2)]
    for j in range(KD):
        pr = j % 2
        s.emit("sync", lambda e, j=j, pr=pr: e.dma_start(
            out=STG[pr][:], in_=Rm[j]), inc=d_tail, inc_n=16)
        s.wait("sync", d_tail)
        s.emit("sync", lambda e, j=j, pr=pr: e.dma_start(
            out=outT[j], in_=STG[pr][:]), inc=d_tail, inc_n=16)
    for name, c in list(s.sems.items()):
        if name.startswith("dma_"):
            s.wait("sync", c)


_NC_CACHE = {}


def kernel(**inputs):
    import os
    stop_after = os.environ.get("MK_STOP", "full")
    hp = {
        "win_t": tile_lhsT(np.asarray(inputs["L0_attn_a_W_in"])),
        "wout_t": tile_lhsT(np.asarray(inputs["L0_attn_a_W_out"]), groups=2),
        "wm1_t": tile_lhsT(np.asarray(inputs["L0_attn_a_W_mlp1"])),
        "wm2_t": tile_lhsT(np.asarray(inputs["L0_attn_a_W_mlp2"]), groups=4),
        "w1_t": tile_lhsT(np.asarray(inputs["L0_ffn_w1"])),
        "w2_t": tile_lhsT(np.asarray(inputs["L0_ffn_w2"])),
        "w3_t": tile_lhsT(np.asarray(inputs["L0_ffn_w3"]), groups=4),
        "wdkv_t": tile_lhsT(np.asarray(inputs["L0_attn_b_W_DKV_w"])),
        "wuk_t": tile_lhsT(np.asarray(inputs["L0_attn_b_W_UK_w"])),
        "wuv_t": tile_lhsT(np.asarray(inputs["L0_attn_b_W_UV_w"])),
        "wdq_t": tile_lhsT(np.asarray(inputs["L0_attn_b_W_DQ_w"])),
        "wuq_t": tile_lhsT(np.asarray(inputs["L0_attn_b_W_UQ_w"])),
        "wkr_t": tile_lhsT(np.asarray(inputs["L0_attn_b_W_KR_w"])),
        "wo_t": tile_lhsT(np.asarray(inputs["L0_attn_b_W_O_w"])),
        "convw": np.ascontiguousarray(
            np.asarray(inputs["L0_attn_a_conv_w"]).reshape(32, 128, 4).transpose(1, 0, 2)),
        "convb": chunk_vec(inputs["L0_attn_a_conv_b"]),
        "bdkv": chunk_vec(inputs["L0_attn_b_W_DKV_b"]),
        "buk": chunk_vec(inputs["L0_attn_b_W_UK_b"]),
        "buv_row": np.ascontiguousarray(
            np.asarray(inputs["L0_attn_b_W_UV_b"]).reshape(1, 2048)),
        "bdq": chunk_vec(inputs["L0_attn_b_W_DQ_b"]),
        "buq": chunk_vec(inputs["L0_attn_b_W_UQ_b"]),
        "bkr": chunk_vec(inputs["L0_attn_b_W_KR_b"]),
        "bo": chunk_vec(inputs["L0_attn_b_W_O_b"]),
    }
    hp["hwcat_t"] = np.ascontiguousarray(
        np.concatenate([hp["wuk_t"], hp["wuv_t"], hp["wuq_t"]], axis=2))
    x = np.asarray(inputs["L0_attn_a_x"])
    nc = build_program(stop_after)
    in_maps = []
    for b in range(8):
        m = dict(hp)
        m["xt"] = np.ascontiguousarray(x[b].T)
        in_maps.append(m)
    res = run_bass_kernel_spmd(nc, in_maps, core_ids=list(range(8)))
    out = np.empty((8, T, D), np.float32)
    for b in range(8):
        ot = np.asarray(res.results[b]["outT"])
        out[b] = ot.reshape(D, T).T
    return out


# version 14 (final)
# speedup vs baseline: 1.2458x; 1.0486x over previous
import numpy as np
import ml_dtypes
from contextlib import ExitStack

import concourse.bass as bass
import concourse.mybir as mybir
from concourse.bass_utils import run_bass_kernel_spmd

F32 = mybir.dt.float32
BF16 = mybir.dt.bfloat16
AF = mybir.ActivationFunctionType
ALU = mybir.AluOpType
AX = mybir.AxisListType
BF = ml_dtypes.bfloat16

D = 2048
T = 2048
KD = 16
NQ = 4
H = 16
KF = 44
EPS_LN = 1e-5
NEG_BIG = -3.0e38
FFN_CNT = 5632 * 2048

ENGINES = ("tensor", "vector", "scalar", "gpsimd", "sync")


class Ctr:
    __slots__ = ("h", "count", "name")

    def __init__(self, h, name):
        self.h = h
        self.count = 0
        self.name = name


class Sched:
    def __init__(self, nc, st):
        self.nc = nc
        self.st = st
        self.q = {e: [] for e in ENGINES}
        self.sems = {}

    def sem(self, name):
        if name not in self.sems:
            h = self.st.enter_context(self.nc.semaphore(name))
            self.sems[name] = Ctr(h, name)
        return self.sems[name]

    def wait(self, eng, ctr, val=None):
        v = ctr.count if val is None else val
        if v <= 0:
            return
        self.q[eng].append(lambda e, h=ctr.h, v=v: e.wait_ge(h, v))

    def emit(self, eng, fn, inc=None, inc_n=1):
        if inc is not None:
            inc.count += inc_n

        def run(e, fn=fn, inc=inc, n=inc_n):
            ins = fn(e)
            if inc is not None:
                ins.then_inc(inc.h, n)

        self.q[eng].append(run)

    def raw(self, eng, fn):
        self.q[eng].append(fn)

    def fence(self):
        for name, c in list(self.sems.items()):
            if name.startswith("dma_"):
                self.wait("sync", c)
        for eng in ENGINES:
            f = self.sem("fence_" + eng)
            if eng in ("tensor", "sync"):
                self.emit(eng, lambda e: e.nop(), inc=f)
            else:
                self.emit(eng, lambda e: e.drain(fusable=False), inc=f)
        for eng in ENGINES:
            for other in ENGINES:
                if other != eng:
                    self.wait(eng, self.sem("fence_" + other))

    def build(self):
        block = self.st.enter_context(self.nc.Block())

        def mk(eng):
            def body(e):
                for f in self.q[eng]:
                    f(e)
            return body

        block.tensor(mk("tensor"))
        block.vector(mk("vector"))
        block.scalar(mk("scalar"))
        block.gpsimd(mk("gpsimd"))
        block.sync(mk("sync"))


def tile_lhsT(Wt, groups=1):
    Fd, Cd = Wt.shape
    nf, kc = Fd // 128, Cd // 128
    L = Wt.reshape(nf, 128, kc, 128).transpose(0, 3, 2, 1)
    if groups == 1:
        return np.ascontiguousarray(L.reshape(nf, 128, kc * 128))
    kg = kc // groups
    Lg = L.reshape(nf, 128, groups, kg, 128).transpose(2, 0, 1, 3, 4)
    return np.ascontiguousarray(Lg.reshape(groups, nf, 128, kg * 128))


def chunk_vec(b):
    return np.ascontiguousarray(np.asarray(b).reshape(-1, 128).T)


def rope_tables():
    pos = np.arange(T, dtype=np.float64)
    idx = np.arange(64, dtype=np.float64)
    theta = (10000.0 ** (-2.0 * idx / 128.0))[:, None] * pos[None, :]
    C = np.repeat(np.cos(theta), 2, axis=0).astype(BF)
    S = np.empty((128, T), np.float64)
    S[0::2] = -np.sin(theta)
    S[1::2] = np.sin(theta)
    return C, S.astype(BF)


def build_program(stop_after="full"):
    nc = bass.Bass("TRN2", target_bir_lowering=False)

    def ein(name, shape):
        return nc.dram_tensor(name, shape, F32, kind="ExternalInput").ap()

    xt = ein("xt", [D, T])
    win_t = ein("win_t", [64, 128, 2048])
    wout_t = ein("wout_t", [2, 16, 128, 2048])
    wm1_t = ein("wm1_t", [64, 128, 2048])
    wm2_t = ein("wm2_t", [4, 16, 128, 2048])
    w1_t = ein("w1_t", [KF, 128, 2048])
    w2_t = ein("w2_t", [KF, 128, 2048])
    w3_t = ein("w3_t", [4, 16, 128, 1408])
    wdkv_t = ein("wdkv_t", [4, 128, 2048])
    wuk_t = ein("wuk_t", [16, 128, 512])
    wuv_t = ein("wuv_t", [16, 128, 512])
    wdq_t = ein("wdq_t", [8, 128, 2048])
    wuq_t = ein("wuq_t", [16, 128, 1024])
    hwcat_t = ein("hwcat_t", [16, 128, 2048])
    wkr_t = ein("wkr_t", [16, 128, 2048])
    wo_t = ein("wo_t", [16, 128, 2048])
    convw_d = ein("convw", [128, 32, 4])
    convb_d = ein("convb", [128, 32])
    bdkv_d = ein("bdkv", [128, 4])
    buk_d = ein("buk", [128, 16])
    buv_row_d = ein("buv_row", [1, 2048])
    bdq_d = ein("bdq", [128, 8])
    buq_d = ein("buq", [128, 16])
    bkr_d = ein("bkr", [128, 16])
    bo_d = ein("bo", [128, 16])

    import os as _os
    _dbg = set(filter(None, _os.environ.get("MK_DBG", "").split(",")))

    def _scratch(name, shape, dt):
        kind = "ExternalOutput" if name in _dbg else "Internal"
        return nc.dram_tensor(name, shape, dt, kind=kind).ap()

    outT = nc.dram_tensor("outT", [KD, 128, T], F32, kind="ExternalOutput").ap()
    Rm = _scratch("Rm", [KD, 128, T], F32)
    ysp = _scratch("ysp", [32, 128, T], BF16)
    usp = _scratch("usp", [64, 128, T], BF16)
    sgsp = _scratch("sgsp", [KF, 128, T], BF16)
    hidsp = _scratch("hidsp", [KF, 128, T], BF16)
    osp = _scratch("osp", [16, 128, T], BF16)
    wq1sp = _scratch("wq1sp", [KF, 128, 2048], BF16)
    wq2sp = _scratch("wq2sp", [KF, 128, 2048], BF16)
    wq3sp = _scratch("wq3sp", [4, 16, 128, 1408], BF16)
    woutbf = _scratch("woutbf", [2, 16, 128, 2048], BF16)
    wm2bf = _scratch("wm2bf", [4, 16, 128, 2048], BF16)
    hwbf = _scratch("hwbf", [16, 2, 128, 2048], BF16)
    hsp = _scratch("hsp", [KD, 128, T], BF16) if "hsp" in _dbg else None
    scdbg = _scratch("scdbg", [4, 128, 512], F32) if "scdbg" in _dbg else None

    ident_c = nc.inline_tensor(np.eye(128, dtype=BF), "ident_c")
    onescol_c = nc.inline_tensor(np.ones((128, 1), dtype=BF), "onescol_c")
    onesrow_f_c = nc.inline_tensor(np.ones((1, 128), np.float32), "onesrow_f_c")
    onescolf_c = nc.inline_tensor(np.ones((128, 1), np.float32), "onescolf_c")
    epsc_np = np.zeros((128, 2), np.float32)
    epsc_np[:, 0] = EPS_LN
    epsc_np[:, 1] = 0.5e-8
    epsc_c = nc.inline_tensor(epsc_np, "epsc_c")
    swapP_np = np.zeros((128, 128), dtype=BF)
    for i in range(64):
        swapP_np[2 * i, 2 * i + 1] = 1
        swapP_np[2 * i + 1, 2 * i] = 1
    swapP_c = nc.inline_tensor(swapP_np, "swapP_c")
    tril_np = np.where(np.arange(128)[None, :] <= np.arange(128)[:, None],
                       np.float32(0.0), np.float32(NEG_BIG)).astype(np.float32)
    tril_c = nc.inline_tensor(tril_np, "tril_c")
    ropeC_np, ropeS_np = rope_tables()
    ropeC_c = nc.inline_tensor(ropeC_np, "ropeC_c")
    ropeS_c = nc.inline_tensor(ropeS_np, "ropeS_c")

    st = ExitStack()
    with st:
        RES = st.enter_context(nc.sbuf_tensor("RES", [128, KD, T], BF16))
        EVB = [st.enter_context(nc.sbuf_tensor(f"EVB{i}", [128, 2048], BF16)) for i in range(2)]
        ident = st.enter_context(nc.sbuf_tensor("ident", [128, 128], BF16))
        onescol = st.enter_context(nc.sbuf_tensor("onescol", [128, 1], BF16))
        onesrow_f = st.enter_context(nc.sbuf_tensor("onesrow_f", [1, 128], F32))
        convw = st.enter_context(nc.sbuf_tensor("convw_sb", [128, 32, 4], F32))
        convb = st.enter_context(nc.sbuf_tensor("convb_sb", [128, 32], F32))
        bdkv = st.enter_context(nc.sbuf_tensor("bdkv_sb", [128, 4], F32))
        buk = st.enter_context(nc.sbuf_tensor("buk_sb", [128, 16], F32))
        buv_row = st.enter_context(nc.sbuf_tensor("buv_row_sb", [1, 2048], F32))
        bdq = st.enter_context(nc.sbuf_tensor("bdq_sb", [128, 8], F32))
        buq16 = st.enter_context(nc.sbuf_tensor("buq16", [128, 16], F32))
        bkr = st.enter_context(nc.sbuf_tensor("bkr_sb", [128, 16], F32))
        bo = st.enter_context(nc.sbuf_tensor("bo_sb", [128, 16], F32))
        gacc = st.enter_context(nc.sbuf_tensor("gacc", [128, 4], F32))
        epsc = st.enter_context(nc.sbuf_tensor("epsc", [128, 2], F32))
        gtmp = st.enter_context(nc.sbuf_tensor("gtmp", [128, 1], F32))
        thrv = st.enter_context(nc.sbuf_tensor("thrv", [128, 4], F32))
        nthrv = st.enter_context(nc.sbuf_tensor("nthrv", [128, 4], F32))

        s = Sched(nc, st)
        d_stg = [s.sem("dma_stg0"), s.sem("dma_stg1")]
        d_grp = [s.sem("dma_grp0"), s.sem("dma_grp1")]
        d_res = s.sem("dma_res")
        d_xb = s.sem("dma_xb")
        d_xb4 = [s.sem(f"dma_xb4_{i}") for i in range(4)]
        d_stg4 = [s.sem("dma_stg2"), s.sem("dma_stg3")]
        d_hw = [s.sem("dma_hw0"), s.sem("dma_hw1")]
        d_stv = s.sem("dma_stv")
        d_ev = s.sem("dma_ev")
        d_acc = s.sem("dma_acc")
        d_wq = s.sem("dma_wq")
        d_side = s.sem("dma_side")
        d_sta = s.sem("dma_sta")
        d_misc = s.sem("dma_misc")
        d_g = [s.sem("dma_g0"), s.sem("dma_g1")]
        c_pool = s.sem("c_pool")
        c_pan = [s.sem("c_pan0"), s.sem("c_pan1")]
        c_mm = [s.sem("c_mm0"), s.sem("c_mm1")]
        c_evv = s.sem("c_evv")
        c_eva = s.sem("c_eva")
        c_pe = s.sem("c_pe")
        c_g = s.sem("c_g")

        for dst, src in ((epsc, epsc_c.ap()), (ident, ident_c.ap()), (onescol, onescol_c.ap()),
                         (onesrow_f, onesrow_f_c.ap()), (convw, convw_d),
                         (convb, convb_d), (bdkv, bdkv_d), (buk, buk_d),
                         (buv_row, buv_row_d), (bdq, bdq_d), (bkr, bkr_d),
                         (bo, bo_d), (buq16, buq_d)):
            s.emit("sync", lambda e, dst=dst, src=src: e.dma_start(out=dst[:], in_=src),
                   inc=d_misc, inc_n=16)
        s.wait("vector", d_misc)
        s.emit("vector", lambda e: e.tensor_scalar(
            out=buq16[:], in0=buq16[:], scalar1=1.0 / 16.0, scalar2=None, op0=ALU.mult),
            inc=c_evv)
        s.emit("gpsimd", lambda e: e.memset(gacc[:], 0.0), inc=c_pool)
        s.wait("tensor", d_misc)

        uid = [0]

        def mktile(scope, shape, dt):
            uid[0] += 1
            return scope.enter_context(nc.sbuf_tensor(f"t{uid[0]}", shape, dt))

        def mkpsum(scope, shape, dt=F32):
            uid[0] += 1
            return scope.enter_context(nc.psum_tensor(f"p{uid[0]}", shape, dt))

        def load_res_stats(scope, src_chunks, mode, STG):
            SQ = [mktile(scope, [128, T], BF16) for _ in range(2)]
            SR1 = mktile(scope, [1, T], F32)
            SR2 = mktile(scope, [1, T], F32)
            nbt = 3 if mode == "A" else 2
            BT = [mktile(scope, [128, T], BF16) for _ in range(nbt)]
            BTS = mktile(scope, [128, T], BF16)
            cast_c = []
            sq_c = []
            with ExitStack() as ps_sc:
                PS_S = mkpsum(ps_sc, [1, T])
                PS_Q = mkpsum(ps_sc, [1, T])
                for j in range(KD):
                    pr = j % 2
                    if j >= 2:
                        s.wait("sync", c_pan[pr], cast_c[j - 2])
                    s.emit("sync", lambda e, j=j, pr=pr: e.dma_start(
                        out=STG[pr][:], in_=src_chunks[j]), inc=d_stg[pr], inc_n=16)
                    ceng = "scalar" if pr == 0 else "vector"
                    s.wait(ceng, d_stg[pr])
                    if pr == 0:
                        s.emit("scalar", lambda e, j=j, pr=pr: e.activation(
                            out=RES[:, j, :], in_=STG[pr][:], func=AF.Identity), inc=c_pan[0])
                    else:
                        s.emit("vector", lambda e, j=j, pr=pr: e.tensor_copy(
                            out=RES[:, j, :], in_=STG[pr][:]), inc=c_pan[1])
                    cast_c.append(c_pan[pr].count)
                    s.wait("scalar", c_pan[pr], cast_c[j])
                    if j >= 2:
                        s.wait("scalar", c_pe, sq_c[j - 2])
                    s.emit("scalar", lambda e, j=j, pr=pr: e.activation(
                        out=SQ[pr][:], in_=RES[:, j, :], func=AF.Square), inc=c_eva)
                    sq_done = c_eva.count
                    s.wait("tensor", c_pan[pr], cast_c[j])
                    for q in range(NQ):
                        s.raw("tensor", lambda e, j=j, q=q: e.matmul(
                            PS_S[:, q * 512:(q + 1) * 512], onescol[:],
                            RES[:, j, q * 512:(q + 1) * 512],
                            start=(j == 0), stop=(j == KD - 1)))
                    s.wait("tensor", c_eva, sq_done)
                    for q in range(NQ):
                        fn = lambda e, pr=pr, q=q, j=j: e.matmul(
                            PS_Q[:, q * 512:(q + 1) * 512], onescol[:],
                            SQ[pr][:, q * 512:(q + 1) * 512],
                            start=(j == 0), stop=(j == KD - 1))
                        if q == NQ - 1:
                            s.emit("tensor", fn, inc=c_pe)
                        else:
                            s.raw("tensor", fn)
                    sq_c.append(c_pe.count)
                s.wait("vector", c_pe)
                s.emit("vector", lambda e: e.tensor_copy(out=SR1[:], in_=PS_S[:, :]), inc=c_evv)
                s.emit("vector", lambda e: e.tensor_copy(out=SR2[:], in_=PS_Q[:, :]), inc=c_evv)
                rows_done = c_evv.count
            with ExitStack() as ps_sc:
                PSB = mkpsum(ps_sc, [128, 2048])
                s.wait("tensor", c_evv, rows_done)
                for q in range(NQ):
                    fn = lambda e, q=q: e.matmul(
                        PSB[:, q * 512:(q + 1) * 512], onesrow_f[:],
                        SR1[:, q * 512:(q + 1) * 512], start=True, stop=True)
                    s.emit("tensor", fn, inc=c_pe) if q == NQ - 1 else s.raw("tensor", fn)
                s.wait("scalar", c_pe)
                s.emit("scalar", lambda e: e.activation(
                    out=BT[0][:], in_=PSB[:, :], func=AF.Identity, scale=1.0 / D), inc=c_eva)
                s.wait("tensor", c_eva)
                for q in range(NQ):
                    fn = lambda e, q=q: e.matmul(
                        PSB[:, q * 512:(q + 1) * 512], onesrow_f[:],
                        SR2[:, q * 512:(q + 1) * 512], start=True, stop=True)
                    s.emit("tensor", fn, inc=c_pe) if q == NQ - 1 else s.raw("tensor", fn)
                s.wait("scalar", c_pe)
                s.emit("scalar", lambda e: e.activation(
                    out=BT[1][:], in_=PSB[:, :], func=AF.Identity, scale=1.0 / D), inc=c_eva)
            s.wait("vector", c_eva)
            s.emit("vector", lambda e: e.tensor_mul(out=BTS[:], in0=BT[0][:], in1=BT[0][:]), inc=c_evv)
            s.emit("vector", lambda e: e.tensor_sub(out=BT[1][:], in0=BT[1][:], in1=BTS[:]), inc=c_evv)
            s.wait("scalar", c_evv)
            if mode == "A":
                s.emit("scalar", lambda e: e.activation(
                    out=BT[2][:], in_=BT[1][:], func=AF.Ln, bias=epsc[:, 0:1]), inc=c_eva)
                s.emit("scalar", lambda e: e.activation(
                    out=BT[1][:], in_=BT[1][:], func=AF.Ln), inc=c_eva)
                s.wait("vector", c_eva)
                s.emit("vector", lambda e: e.tensor_sub(out=BT[2][:], in0=BT[1][:], in1=BT[2][:]), inc=c_evv)
                s.wait("scalar", c_evv)
                s.emit("scalar", lambda e: e.activation(
                    out=BT[2][:], in_=BT[2][:], func=AF.Exp, scale=0.5), inc=c_eva)
                s.emit("scalar", lambda e: e.activation(
                    out=BT[1][:], in_=BT[1][:], func=AF.Exp, scale=-0.5), inc=c_eva)
                return BT[0], BT[1], BT[2], BTS
            else:
                s.emit("scalar", lambda e: e.activation(
                    out=BT[1][:], in_=BT[1][:], func=AF.Ln, bias=epsc[:, 0:1]), inc=c_eva)
                s.emit("scalar", lambda e: e.activation(
                    out=BT[1][:], in_=BT[1][:], func=AF.Exp, scale=-0.5), inc=c_eva)
                return BT[0], BT[1], BTS

        def center_res(Bmu, Bscale, scr):
            s.wait("vector", c_eva)
            for j in range(KD):
                s.emit("vector", lambda e, j=j: e.tensor_sub(
                    out=scr[:], in0=RES[:, j, :], in1=Bmu[:]), inc=c_evv)
                s.emit("vector", lambda e, j=j: e.tensor_mul(
                    out=RES[:, j, :], in0=scr[:], in1=Bscale[:]), inc=c_evv)

        def stream_panels(panels, kc, psA, psB, evac_fn, STG=None, PAN=None, quant=None,
                          pool_extra=None, bf16_src=False, rhs=None, rhs_kc=None,
                          spills=None):
            if rhs is None:
                rhs, rhs_kc = RES, KD
            for qq in ("sync", "gpsimd"):
                s.wait(qq, c_mm[0])
                s.wait(qq, c_mm[1])
                s.wait(qq, c_pan[0])
                s.wait(qq, c_pan[1])
            wid = kc * 128
            n = len(panels)
            cast_c = [None] * n
            mm_c = [None] * n
            free_c = [None] * n
            for i in range(n):
                pr = i % 2
                if bf16_src:
                    if i >= 2:
                        s.wait("sync", c_mm[pr], mm_c[i - 2])
                    s.emit("sync", lambda e, i=i, pr=pr: e.dma_start(
                        out=PAN[pr][:, 0:wid], in_=panels[i]), inc=d_stg[pr], inc_n=16)
                    s.wait("tensor", d_stg[pr])
                else:
                    if i >= 2:
                        s.wait("sync", c_pan[pr], cast_c[i - 2])
                    s.emit("sync", lambda e, i=i, pr=pr: e.dma_start(
                        out=STG[pr][:, 0:wid], in_=panels[i]), inc=d_stg[pr], inc_n=16)
                    ceng = ("scalar" if pr == 0 else "vector") if quant is None else "vector"
                    s.wait(ceng, d_stg[pr])
                    if i >= 2:
                        s.wait(ceng, c_mm[pr], mm_c[i - 2])
                    if spills is not None and i >= 2 and spills[i - 2] is not None:
                        s.wait(ceng, d_wq)
                    if quant is None:
                        if ceng == "scalar":
                            s.emit("scalar", lambda e, pr=pr: e.activation(
                                out=PAN[pr][:, 0:wid], in_=STG[pr][:, 0:wid],
                                func=AF.Identity), inc=c_pan[pr])
                        else:
                            s.emit("vector", lambda e, pr=pr: e.tensor_copy(
                                out=PAN[pr][:, 0:wid], in_=STG[pr][:, 0:wid]), inc=c_pan[pr])
                    else:
                        m, QA, QB = quant
                        s.raw("vector", lambda e, pr=pr: e.tensor_scalar(
                            out=QA[:, 0:wid], in0=STG[pr][:, 0:wid],
                            scalar1=thrv[:, m:m + 1], scalar2=1.0,
                            op0=ALU.is_gt, op1=ALU.mult))
                        s.raw("vector", lambda e, pr=pr: e.tensor_scalar(
                            out=QB[:, 0:wid], in0=STG[pr][:, 0:wid],
                            scalar1=nthrv[:, m:m + 1], scalar2=1.0,
                            op0=ALU.is_ge, op1=ALU.mult))
                        s.emit("vector", lambda e, pr=pr: e.scalar_tensor_tensor(
                            out=PAN[pr][:, 0:wid], in0=QA[:, 0:wid], scalar=-1.0,
                            in1=QB[:, 0:wid], op0=ALU.add, op1=ALU.add), inc=c_pan[pr])
                    cast_c[i] = c_pan[pr].count
                    if spills is not None and spills[i] is not None:
                        s.wait("scalar", c_pan[pr], cast_c[i])
                        s.emit("scalar", lambda e, i=i, pr=pr: e.dma_start(
                            out=spills[i], in_=PAN[pr][:, 0:wid]), inc=d_wq, inc_n=16)
                    s.wait("tensor", c_pan[pr], cast_c[i])
                if pool_extra is not None:
                    pool_extra(i)
                if i >= 2:
                    s.wait("tensor", *free_c[i - 2])
                ps = psA if pr == 0 else psB
                for t in range(NQ):
                    for k in range(kc):
                        fn = lambda e, ps=ps, pr=pr, k=k, t=t: e.matmul(
                            ps[:, t * 512:(t + 1) * 512],
                            PAN[pr][:, k * 128:(k + 1) * 128],
                            rhs[:, k, t * 512:(t + 1) * 512],
                            start=(k == 0), stop=(k == kc - 1))
                        if t == NQ - 1 and k == kc - 1:
                            s.emit("tensor", fn, inc=c_mm[pr])
                        else:
                            s.raw("tensor", fn)
                mm_c[i] = c_mm[pr].count
                free_c[i] = evac_fn(i, ps, pr)

        def acc_stage(qg_panels, kc_g, ngrp, grp_src_fn, ACC, GRP, evac_q_fn,
                      STG=None, PAN=None, bf16_pan=False, side_fn=None, deep=True):
            wid = kc_g * 128
            PQ = None
            with ExitStack() as ps_sc:
                PQ = [mkpsum(ps_sc, [128, 512]) for _ in range(2)]
                if bf16_pan:
                    if deep:
                        PANX = [mktile(ps_sc, [128, wid], BF16) for _ in range(2)]
                        pbufs = [PAN[0], PAN[1], PANX[0], PANX[1]]
                        psems = [d_stg[0], d_stg[1], d_stg4[0], d_stg4[1]]
                    else:
                        pbufs = [PAN[0], PAN[1]]
                        psems = [d_stg[0], d_stg[1]]
                    npb = len(pbufs)
                pi = 0
                grp_mm = {}
                pan_cast = {}
                pan_mm = {}
                ps_free = {}
                for q in range(NQ):
                    for g in range(ngrp):
                        gpr = g % 2
                        prev = (q, g - 2) if g >= 2 else ((q - 1, ngrp - 2 + g) if q >= 1 else None)
                        if prev in grp_mm:
                            s.wait("sync", c_mm[0], grp_mm[prev])
                        s.emit("sync", lambda e, q=q, g=g, gpr=gpr: e.dma_start(
                            out=GRP[gpr][:], in_=grp_src_fn(q, g)), inc=d_grp[gpr], inc_n=16)
                        s.wait("tensor", d_grp[gpr])
                        for f in range(16):
                            pr = pi % 2
                            if bf16_pan:
                                pb = pi % npb
                                if pi >= npb:
                                    s.wait("sync", c_mm[0], pan_mm.get(pi - npb, 0))
                                s.emit("sync", lambda e, q=q, g=g, f=f, pb=pb: e.dma_start(
                                    out=pbufs[pb][:, 0:wid], in_=qg_panels[q][g][f]),
                                    inc=psems[pb], inc_n=16)
                                s.wait("tensor", psems[pb])
                            else:
                                if pi >= 2:
                                    s.wait("sync", c_pan[pr], pan_cast.get(pi - 2, 0))
                                s.emit("sync", lambda e, q=q, g=g, f=f, pr=pr: e.dma_start(
                                    out=STG[pr][:, 0:wid], in_=qg_panels[q][g][f]),
                                    inc=d_stg[pr], inc_n=16)
                                s.wait("gpsimd", d_stg[pr])
                                if pi >= 2:
                                    s.wait("gpsimd", c_mm[0], pan_mm.get(pi - 2, 0))
                                s.emit("gpsimd", lambda e, pr=pr: e.tensor_copy(
                                    out=PAN[pr][:, 0:wid], in_=STG[pr][:, 0:wid]),
                                    inc=c_pan[pr])
                                pan_cast[pi] = c_pan[pr].count
                                s.wait("tensor", c_pan[pr], pan_cast[pi])
                            if pi >= 2:
                                s.wait("tensor", c_evv, ps_free.get(pi - 2, 0))
                            ps = PQ[pi % 2]
                            pbuf = pbufs[pi % npb] if bf16_pan else PAN[pr]
                            for k in range(kc_g):
                                fn = lambda e, ps=ps, pbuf=pbuf, k=k, gpr=gpr: e.matmul(
                                    ps[:, :], pbuf[:, k * 128:(k + 1) * 128],
                                    GRP[gpr][:, k, :],
                                    start=(k == 0), stop=(k == kc_g - 1))
                                if k == kc_g - 1:
                                    s.emit("tensor", fn, inc=c_mm[0])
                                else:
                                    s.raw("tensor", fn)
                            pan_mm[pi] = c_mm[0].count
                            if side_fn is not None:
                                side_fn(pi)
                            s.wait("vector", c_mm[0], pan_mm[pi])
                            if g == 0:
                                if q > 0:
                                    s.wait("vector", d_acc)
                                s.emit("vector", lambda e, f=f, ps=ps: e.tensor_copy(
                                    out=ACC[:, f, :], in_=ps[:, :]), inc=c_evv)
                            else:
                                s.emit("vector", lambda e, f=f, ps=ps: e.tensor_add(
                                    out=ACC[:, f, :], in0=ACC[:, f, :], in1=ps[:, :]), inc=c_evv)
                            ps_free[pi] = c_evv.count
                            pi += 1
                        grp_mm[(q, g)] = c_mm[0].count
                    evac_q_fn(q)

        def bcast_const_row(scope, src_row_ap, out_tile, scale=1.0, extra=None):
            with ExitStack() as ps_sc:
                PSB = mkpsum(ps_sc, [128, 512])
                s.wait("tensor", c_evv)
                s.wait("tensor", c_eva)
                s.emit("tensor", lambda e: e.matmul(
                    PSB[:, 0:src_row_ap.shape[-1]], onesrow_f[:], src_row_ap,
                    start=True, stop=True), inc=c_pe)
                s.wait("scalar", c_pe)
                s.emit("scalar", lambda e: e.activation(
                    out=out_tile, in_=PSB[:, 0:src_row_ap.shape[-1]],
                    func=AF.Identity, scale=scale), inc=c_eva)

        side_jobs = ([("c", wout_t[g, f], woutbf[g, f], 2048)
                      for g in range(2) for f in range(16)]
                     + [("g", w1_t[i], 2048, 0) for i in range(KF)]
                     + [("g", w2_t[i], 2048, 1) for i in range(KF)]
                     + [("g", w3_t[g, f], 1408, 2) for g in range(4) for f in range(16)]
                     + [("c", wm2_t[g, f], wm2bf[g, f], 2048)
                        for g in range(4) for f in range(16)]
                     + [j for h in range(16) for j in
                        (("c", hwcat_t[h], hwbf[h, 0], 2048),
                         ("c", wkr_t[h], hwbf[h, 1], 2048))])
        N_WOUT_JOBS = 32
        N_GAMMA_END = 32 + 152
        side_i = [0]
        side_done = {}

        def side_pump(GSTb, SBTb, n=1):
            for _ in range(n):
                gi = side_i[0]
                if gi >= len(side_jobs):
                    return
                side_i[0] += 1
                job = side_jobs[gi]
                gpr = gi % 2
                if gi >= 2:
                    s.wait("sync", c_g, side_done[gi - 2])
                src = job[1]
                wid = job[2] if job[0] == "g" else job[3]
                s.emit("sync", lambda e, src=src, wid=wid, gpr=gpr: e.dma_start(
                    out=GSTb[gpr][:, 0:wid], in_=src), inc=d_g[gpr], inc_n=16)
                if job[0] == "g":
                    m = job[3]
                    s.wait("vector", d_g[gpr])
                    qw = wid // 4
                    for qq in range(4):
                        s.emit("vector", lambda e, qq=qq, qw=qw, gpr=gpr, GSTb=GSTb: e.tensor_reduce(
                            out=gtmp[:], in_=GSTb[gpr][:, qq * qw:(qq + 1) * qw],
                            axis=AX.X, op=ALU.add, apply_absolute_value=True), inc=c_evv)
                        s.wait("vector", c_evv)
                        fn = lambda e, m=m: e.tensor_add(
                            out=gacc[:, m:m + 1], in0=gacc[:, m:m + 1], in1=gtmp[:])
                        if qq == 3:
                            s.emit("vector", fn, inc=c_g)
                        else:
                            s.emit("vector", fn, inc=c_evv)
                        s.wait("vector", c_evv if qq < 3 else c_g)
                else:
                    dst = job[2]
                    ceng = "scalar" if gpr == 0 else "vector"
                    s.wait(ceng, d_g[gpr])
                    s.wait(ceng, d_side)
                    if ceng == "scalar":
                        s.emit("scalar", lambda e, gpr=gpr, GSTb=GSTb, SBTb=SBTb: e.activation(
                            out=SBTb[gpr][:], in_=GSTb[gpr][:], func=AF.Identity), inc=c_g)
                        s.wait("scalar", c_g)
                        s.emit("scalar", lambda e, dst=dst, gpr=gpr, SBTb=SBTb: e.dma_start(
                            out=dst, in_=SBTb[gpr][:]), inc=d_side, inc_n=16)
                    else:
                        s.emit("vector", lambda e, gpr=gpr, GSTb=GSTb, SBTb=SBTb: e.tensor_copy(
                            out=SBTb[gpr][:], in_=GSTb[gpr][:]), inc=c_g)
                        s.wait("scalar", c_g)
                        s.emit("scalar", lambda e, dst=dst, gpr=gpr, SBTb=SBTb: e.dma_start(
                            out=dst, in_=SBTb[gpr][:]), inc=d_side, inc_n=16)
                side_done[gi] = c_g.count

        xt_r = xt.rearrange("(j p) t -> p j t", p=128)
        with ExitStack() as scA:
            Bsig0 = mktile(scA, [128, T], BF16)
            Brho2 = mktile(scA, [128, T], BF16)

            with ExitStack() as sc:
                STGl = [mktile(sc, [128, 2048], F32) for _ in range(2)]
                Bmu, Brho, Bsig, scr = load_res_stats(
                    sc, [xt_r[:, j, :] for j in range(KD)], "A", STGl)
                s.wait("vector", c_eva)
                s.emit("vector", lambda e: e.tensor_copy(out=Bsig0[:], in_=Bsig[:]), inc=c_evv)
                center_res(Bmu, Brho, scr)

            s.fence()
            with ExitStack() as sc:
                XPS = [mktile(sc, [128, 3 + T], F32) for _ in range(2)]
                XC = mktile(sc, [128, T], F32)
                XCS = mktile(sc, [128, T], BF16)
                ZSIL = mktile(sc, [128, T], BF16)
                GST = [mktile(sc, [128, 2048], F32) for _ in range(2)]
                SBT = [mktile(sc, [128, 2048], BF16) for _ in range(2)]
                PSA = mkpsum(sc, [128, 2048])
                PSB = mkpsum(sc, [128, 2048])
                s.wait("gpsimd", c_pool)
                s.emit("gpsimd", lambda e: e.memset(XPS[0][:, 0:3], 0.0), inc=c_pool)
                s.emit("gpsimd", lambda e: e.memset(XPS[1][:, 0:3], 0.0), inc=c_pool)

                def pool_extra(i):
                    side_pump(GST, SBT, 2 if i % 2 == 0 else 1)

                state = {"xc_sil": 0, "ytt": 0, "xpcopy": {}, "zsil": 0}

                def evac_a1(i, ps, pr):
                    pair = i // 2
                    j = pair
                    if pr == 0:
                        s.wait("scalar", c_mm[0])
                        if pair >= 2:
                            s.wait("scalar", c_evv, state["xpcopy"].get(pair - 2, (0, 0))[1])
                        s.emit("scalar", lambda e, pair=pair: e.activation(
                            out=XPS[pair % 2][:, 3:3 + T], in_=ps[:, :],
                            func=AF.Identity), inc=c_eva)
                        free = (c_eva, c_eva.count)
                        s.wait("vector", c_eva)
                        if pair >= 1:
                            s.wait("vector", c_eva, state["xc_sil"])
                        s.raw("vector", lambda e, pair=pair, j=j: e.tensor_scalar(
                            out=XC[:], in0=XPS[pair % 2][:, 0:T],
                            scalar1=convw[:, j, 0:1], scalar2=0.0,
                            op0=ALU.mult, op1=ALU.add))
                        for k in range(1, 3):
                            s.raw("vector", lambda e, pair=pair, j=j, k=k: e.scalar_tensor_tensor(
                                out=XC[:], in0=XPS[pair % 2][:, k:k + T],
                                scalar=convw[:, j, k:k + 1], in1=XC[:],
                                op0=ALU.mult, op1=ALU.add))
                        s.emit("vector", lambda e, pair=pair, j=j: e.scalar_tensor_tensor(
                            out=XC[:], in0=XPS[pair % 2][:, 3:3 + T],
                            scalar=convw[:, j, 3:4], in1=XC[:],
                            op0=ALU.mult, op1=ALU.add), inc=c_evv)
                        state["xpcopy"][pair] = (c_evv, c_evv.count)
                        s.wait("scalar", c_evv)
                        if pair >= 1:
                            s.wait("scalar", c_evv, state["ytt"])
                        s.emit("scalar", lambda e, j=j: e.activation(
                            out=XCS[:], in_=XC[:], func=AF.Silu,
                            bias=convb[:, j:j + 1]), inc=c_eva)
                        state["xc_sil"] = c_eva.count
                        return free
                    else:
                        s.wait("scalar", c_mm[1])
                        if pair >= 1:
                            s.wait("scalar", c_evv, state["ytt"])
                        s.emit("scalar", lambda e: e.activation(
                            out=ZSIL[:], in_=ps[:, :], func=AF.Silu), inc=c_eva)
                        free = (c_eva, c_eva.count)
                        s.wait("vector", c_eva)
                        s.wait("vector", d_ev)
                        s.emit("vector", lambda e, pair=pair: e.tensor_mul(
                            out=EVB[pair % 2][:], in0=XCS[:], in1=ZSIL[:]), inc=c_evv)
                        state["ytt"] = c_evv.count
                        s.wait("scalar", c_evv, state["ytt"])
                        s.emit("scalar", lambda e, j=j, pair=pair: e.dma_start(
                            out=ysp[j], in_=EVB[pair % 2][:]), inc=d_ev, inc_n=16)
                        return free

                order = []
                for j in range(32):
                    order.append(win_t[j])
                    order.append(win_t[j + 32])
                STGl = [mktile(sc, [128, 2048], F32) for _ in range(2)]
                PANl = [mktile(sc, [128, 2048], BF16) for _ in range(2)]
                stream_panels(order, 16, PSA, PSB, evac_a1, STG=STGl, PAN=PANl,
                              pool_extra=pool_extra)

            s.fence()
            with ExitStack() as sc:
                GRP = [mktile(sc, [128, 16, 512], BF16) for _ in range(2)]
                ACC = mktile(sc, [128, 16, 512], F32)
                T1 = mktile(sc, [128, 512], BF16)
                SQ2 = T1
                SR = mktile(sc, [1, T], F32)
                rho2_row_done = [0]

                ps_q2 = mkpsum(sc, [1, T])

                def grp_src(q, g):
                    return ysp[g * 16:(g + 1) * 16, :, q * 512:(q + 1) * 512].rearrange(
                        "c p t -> p c t")

                def evac_a2(q):
                    s.wait("vector", c_evv)
                    for j in range(KD):
                        s.wait("vector", c_pe)
                        s.emit("vector", lambda e, j=j, q=q, T1=T1: e.tensor_mul(
                            out=T1[:], in0=RES[:, j, q * 512:(q + 1) * 512],
                            in1=Bsig0[:, q * 512:(q + 1) * 512]), inc=c_evv)
                        s.emit("vector", lambda e, j=j, q=q, T1=T1, ACC=ACC: e.tensor_add(
                            out=RES[:, j, q * 512:(q + 1) * 512], in0=T1[:],
                            in1=ACC[:, j, :]), inc=c_evv)
                        x1_done = c_evv.count
                        s.wait("scalar", c_evv, x1_done)
                        if j >= 1 or q >= 1:
                            s.wait("scalar", c_pe)
                        s.emit("scalar", lambda e, j=j, q=q, SQ2=SQ2: e.activation(
                            out=SQ2[:], in_=RES[:, j, q * 512:(q + 1) * 512],
                            func=AF.Square), inc=c_eva)
                        s.wait("tensor", c_eva)
                        fn = lambda e, j=j, q=q, SQ2=SQ2, ps_q2=ps_q2: e.matmul(
                            ps_q2[:, q * 512:(q + 1) * 512], onescol[:], SQ2[:],
                            start=(j == 0), stop=(j == KD - 1))
                        s.emit("tensor", fn, inc=c_pe)

                GST2 = [mktile(sc, [128, 2048], F32) for _ in range(2)]
                _sbt2 = mktile(sc, [128, 2048], BF16)
                SBT2 = [_sbt2, _sbt2]

                def side_a2(pi):
                    if pi % 8 < 4:
                        side_pump(GST2, SBT2, 1)

                panels = [[[woutbf[g, f] for f in range(16)] for g in range(2)]
                          for q in range(NQ)]
                PANl = [mktile(sc, [128, 2048], BF16) for _ in range(2)]
                acc_stage(panels, 16, 2, grp_src, ACC, GRP, evac_a2, PAN=PANl,
                          bf16_pan=True, side_fn=side_a2, deep=False)
                s.wait("vector", c_pe)
                s.emit("vector", lambda e, SR=SR, ps_q2=ps_q2: e.tensor_copy(out=SR[:], in_=ps_q2[:, :]), inc=c_evv)
                with ExitStack() as ps_sc:
                    PSB2 = mkpsum(ps_sc, [128, 2048])
                    s.wait("tensor", c_evv)
                    for q in range(NQ):
                        fn = lambda e, q=q, PSB2=PSB2, SR=SR: e.matmul(
                            PSB2[:, q * 512:(q + 1) * 512], onesrow_f[:],
                            SR[:, q * 512:(q + 1) * 512], start=True, stop=True)
                        s.emit("tensor", fn, inc=c_pe) if q == NQ - 1 else s.raw("tensor", fn)
                    s.wait("scalar", c_pe)
                    s.emit("scalar", lambda e, PSB2=PSB2: e.activation(
                        out=Brho2[:], in_=PSB2[:, :], func=AF.Ln), inc=c_eva)
                    s.emit("scalar", lambda e: e.activation(
                        out=Brho2[:], in_=Brho2[:], func=AF.Exp, scale=-0.5), inc=c_eva)
                    s.emit("scalar", lambda e: e.activation(
                        out=Brho2[:], in_=Brho2[:], func=AF.Identity,
                        scale=float(np.sqrt(D))), inc=c_eva)

            s.fence()

            with ExitStack() as sc:
                PSA = mkpsum(sc, [128, 2048])
                PSB = mkpsum(sc, [128, 2048])

                GST3 = [mktile(sc, [128, 2048], F32) for _ in range(2)]
                SBT3 = [mktile(sc, [128, 2048], BF16) for _ in range(2)]

                def pool_extra_a3(i):
                    side_pump(GST3, SBT3, 2)

                def evac_a3(i, ps, pr):
                    s.wait("scalar", c_mm[pr])
                    s.wait("scalar", d_ev)
                    s.emit("scalar", lambda e, i=i, ps=ps: e.activation(
                        out=EVB[i % 2][:], in_=ps[:, :], func=AF.Relu), inc=c_eva)
                    free = (c_eva, c_eva.count)
                    s.wait("vector", c_eva)
                    s.emit("vector", lambda e, i=i: e.tensor_mul(
                        out=EVB[i % 2][:], in0=EVB[i % 2][:], in1=Brho2[:]), inc=c_evv)
                    s.wait("scalar", c_evv)
                    s.emit("scalar", lambda e, i=i: e.dma_start(
                        out=usp[i], in_=EVB[i % 2][:]), inc=d_ev, inc_n=16)
                    return free

                STGl = [mktile(sc, [128, 2048], F32) for _ in range(2)]
                PANl = [mktile(sc, [128, 2048], BF16) for _ in range(2)]
                stream_panels([wm1_t[i] for i in range(64)], 16, PSA, PSB, evac_a3,
                              STG=STGl, PAN=PANl, pool_extra=pool_extra_a3)
                while side_i[0] < len(side_jobs):
                    side_pump(GST3, SBT3, 1)

            s.fence()
            with ExitStack() as sc:
                GRP = [mktile(sc, [128, 16, 512], BF16) for _ in range(2)]
                ACC = mktile(sc, [128, 16, 512], F32)
                XB = mktile(sc, [128, 4, 512], F32)

                def grp_src(q, g):
                    return usp[g * 16:(g + 1) * 16, :, q * 512:(q + 1) * 512].rearrange(
                        "c p t -> p c t")

                def evac_a4(q):
                    add_c = {}
                    for j in range(KD):
                        xslot = j % 4
                        if j >= 4:
                            s.wait("sync", c_evv, add_c[j - 4])
                        s.emit("sync", lambda e, j=j, q=q, xslot=xslot, XB=XB: e.dma_start(
                            out=XB[:, xslot, :], in_=xt_r[:, j, q * 512:(q + 1) * 512]),
                            inc=d_xb4[xslot], inc_n=16)
                        s.wait("vector", d_xb4[xslot])
                        s.emit("vector", lambda e, j=j, xslot=xslot, ACC=ACC, XB=XB: e.tensor_add(
                            out=ACC[:, j, :], in0=ACC[:, j, :], in1=XB[:, xslot, :]),
                            inc=c_evv)
                        s.emit("vector", lambda e, j=j, q=q, ACC=ACC: e.tensor_add(
                            out=ACC[:, j, :], in0=ACC[:, j, :],
                            in1=RES[:, j, q * 512:(q + 1) * 512]), inc=c_evv)
                        add_c[j] = c_evv.count
                        s.wait("scalar", c_evv, add_c[j])
                        s.emit("scalar", lambda e, j=j, q=q, ACC=ACC: e.dma_start(
                            out=Rm[j][:, q * 512:(q + 1) * 512], in_=ACC[:, j, :]),
                            inc=d_acc, inc_n=16)

                assert side_i[0] >= 0
                panels = [[[wm2bf[g, f] for f in range(16)] for g in range(4)]
                          for q in range(NQ)]
                PANl = [mktile(sc, [128, 2048], BF16) for _ in range(2)]
                acc_stage(panels, 16, 4, grp_src, ACC, GRP, evac_a4, PAN=PANl,
                          bf16_pan=True)

        s.fence()
        if stop_after == "A":
            emit_tail(nc, s, Rm, outT)
            s.build()
            return nc

        with ExitStack() as sc:
            SG = mktile(sc, [1, 512], F32)
            with ExitStack() as ps_sc:
                PSG = mkpsum(ps_sc, [1, 512])
                onescol_f = mktile(sc, [128, 1], F32)
                s.emit("sync", lambda e: e.dma_start(
                    out=onescol_f[:], in_=onescolf_c.ap()), inc=d_misc, inc_n=16)
                s.wait("tensor", d_misc)
                s.wait("tensor", c_g)
                s.emit("tensor", lambda e: e.matmul(
                    PSG[:, 0:4], onescol_f[:], gacc[:], start=True, stop=True), inc=c_pe)
                s.wait("scalar", c_pe)
                s.emit("scalar", lambda e: e.activation(
                    out=SG[:, 0:4], in_=PSG[:, 0:4], func=AF.Identity,
                    scale=0.5 / FFN_CNT, bias=epsc[0:1, 1:2]), inc=c_eva)
            with ExitStack() as ps_sc:
                PSG2 = mkpsum(ps_sc, [128, 512])
                s.wait("tensor", c_eva)
                s.emit("tensor", lambda e: e.matmul(
                    PSG2[:, 0:4], onesrow_f[:], SG[:, 0:4], start=True, stop=True), inc=c_pe)
                s.wait("scalar", c_pe)
                s.emit("scalar", lambda e: e.activation(
                    out=thrv[:], in_=PSG2[:, 0:4], func=AF.Identity), inc=c_eva)
                s.emit("scalar", lambda e: e.activation(
                    out=nthrv[:], in_=PSG2[:, 0:4], func=AF.Identity, scale=-1.0), inc=c_eva)
        s.fence()

        def emit_ffn(app, final_to_out):
            with ExitStack() as scF:
                with ExitStack() as sc:
                    STGl = [mktile(sc, [128, 2048], F32) for _ in range(2)]
                    Bmu, Bs, scr = load_res_stats(
                        sc, [Rm[j] for j in range(KD)], "LN", STGl)
                    center_res(Bmu, Bs, scr)
                    if hsp is not None and app == 0:
                        s.wait("sync", c_evv)
                        for j in range(KD):
                            s.emit("sync", lambda e, j=j: e.dma_start(
                                out=hsp[j], in_=RES[:, j, :]), inc=d_stv, inc_n=16)
                s.fence()

                with ExitStack() as sc:
                    PSA = mkpsum(sc, [128, 2048])
                    PSB = mkpsum(sc, [128, 2048])
                    if app == 0:
                        QA = mktile(sc, [128, 2048], F32)
                        QB = mktile(sc, [128, 2048], F32)
                        WQS = [mktile(sc, [128, 1408], F32) for _ in range(2)]
                        WQB = mktile(sc, [128, 1408], BF16)
                        w3cur = [0]
                        w3q_done = []

                        def pool_extra_w3(i):
                            gi = w3cur[0]
                            if gi >= 64:
                                return
                            w3cur[0] += 1
                            gpr = gi % 2
                            g, f = gi // 16, gi % 16
                            if gi >= 2:
                                s.wait("sync", c_g, w3q_done[gi - 2])
                            s.emit("sync", lambda e, g=g, f=f, gpr=gpr: e.dma_start(
                                out=WQS[gpr][:], in_=w3_t[g, f]), inc=d_g[gpr], inc_n=16)
                            s.wait("vector", d_g[gpr])
                            s.wait("vector", d_stv)
                            s.raw("vector", lambda e, gpr=gpr: e.tensor_scalar(
                                out=QA[:, 0:1408], in0=WQS[gpr][:],
                                scalar1=thrv[:, 2:3], scalar2=1.0,
                                op0=ALU.is_gt, op1=ALU.mult))
                            s.raw("vector", lambda e, gpr=gpr: e.tensor_scalar(
                                out=QB[:, 0:1408], in0=WQS[gpr][:],
                                scalar1=nthrv[:, 2:3], scalar2=1.0,
                                op0=ALU.is_ge, op1=ALU.mult))
                            s.emit("vector", lambda e: e.scalar_tensor_tensor(
                                out=WQB[:], in0=QA[:, 0:1408], scalar=-1.0,
                                in1=QB[:, 0:1408], op0=ALU.add, op1=ALU.add), inc=c_g)
                            w3q_done.append(c_g.count)
                            s.wait("sync", c_g, w3q_done[-1])
                            s.emit("sync", lambda e, g=g, f=f: e.dma_start(
                                out=wq3sp[g, f], in_=WQB[:]), inc=d_stv, inc_n=16)
                    else:
                        QA = QB = None
                        pool_extra_w3 = None

                    def evac_f1a(i, ps, pr):
                        s.wait("scalar", c_mm[pr])
                        s.wait("scalar", d_sta)
                        s.wait("scalar", d_ev)
                        s.emit("scalar", lambda e, i=i, ps=ps: e.activation(
                            out=EVB[i % 2][:], in_=ps[:, :], func=AF.Sigmoid), inc=c_eva)
                        free = (c_eva, c_eva.count)
                        s.wait("scalar", c_eva)
                        s.emit("scalar", lambda e, i=i: e.dma_start(
                            out=sgsp[i], in_=EVB[i % 2][:]), inc=d_sta, inc_n=16)
                        return free

                    if app == 0:
                        STGl = [mktile(sc, [128, 2048], F32) for _ in range(2)]
                        PANl = [mktile(sc, [128, 2048], BF16) for _ in range(2)]
                        stream_panels([w1_t[i] for i in range(KF)], 16, PSA, PSB,
                                      evac_f1a, STG=STGl, PAN=PANl, quant=(0, QA, QB),
                                      pool_extra=pool_extra_w3,
                                      spills=[wq1sp[i] for i in range(KF)])
                    else:
                        PANl = [mktile(sc, [128, 2048], BF16) for _ in range(2)]
                        stream_panels([wq1sp[i] for i in range(KF)], 16, PSA, PSB,
                                      evac_f1a, PAN=PANl, bf16_src=True)

                    SGB = [mktile(sc, [128, 2048], BF16) for _ in range(2)]
                    sg_ld = {}

                    def evac_f1b(i, ps, pr):
                        s.wait("sync", c_evv)
                        s.emit("sync", lambda e, i=i: e.dma_start(
                            out=SGB[i % 2][:], in_=sgsp[i]), inc=d_grp[i % 2], inc_n=16)
                        s.wait("vector", d_grp[i % 2])
                        s.wait("vector", c_mm[pr])
                        s.wait("vector", d_ev)
                        s.wait("vector", d_sta)
                        s.emit("vector", lambda e, i=i, ps=ps: e.tensor_mul(
                            out=EVB[i % 2][:], in0=SGB[i % 2][:], in1=ps[:, :]), inc=c_evv)
                        free = (c_evv, c_evv.count)
                        s.wait("scalar", c_evv)
                        s.emit("scalar", lambda e, i=i: e.dma_start(
                            out=hidsp[i], in_=EVB[i % 2][:]), inc=d_ev, inc_n=16)
                        return free

                    if app == 0:
                        STGl = [mktile(sc, [128, 2048], F32) for _ in range(2)]
                        PANl = [mktile(sc, [128, 2048], BF16) for _ in range(2)]
                        stream_panels([w2_t[i] for i in range(KF)], 16, PSA, PSB,
                                      evac_f1b, STG=STGl, PAN=PANl, quant=(1, QA, QB),
                                      pool_extra=pool_extra_w3,
                                      spills=[wq2sp[i] for i in range(KF)])
                    else:
                        PANl = [mktile(sc, [128, 2048], BF16) for _ in range(2)]
                        stream_panels([wq2sp[i] for i in range(KF)], 16, PSA, PSB,
                                      evac_f1b, PAN=PANl, bf16_src=True)
                s.fence()

                with ExitStack() as sc:
                    GRP = [mktile(sc, [128, 11, 512], BF16) for _ in range(2)]
                    ACC = mktile(sc, [128, 16, 512], F32)
                    RB = mktile(sc, [128, 4, 512], F32)

                    def grp_src(q, g):
                        return hidsp[g * 11:(g + 1) * 11, :, q * 512:(q + 1) * 512].rearrange(
                            "c p t -> p c t")

                    def evac_f2(q):
                        add_c = {}
                        for j in range(KD):
                            xslot = j % 4
                            if j >= 4:
                                s.wait("sync", c_evv, add_c[j - 4])
                            s.emit("sync", lambda e, j=j, q=q, xslot=xslot: e.dma_start(
                                out=RB[:, xslot, :], in_=Rm[j][:, q * 512:(q + 1) * 512]),
                                inc=d_xb4[xslot], inc_n=16)
                            s.wait("vector", d_xb4[xslot])
                            s.emit("vector", lambda e, j=j, xslot=xslot: e.tensor_add(
                                out=ACC[:, j, :], in0=ACC[:, j, :], in1=RB[:, xslot, :]),
                                inc=c_evv)
                            add_c[j] = c_evv.count
                            s.wait("scalar", c_evv, add_c[j])
                            dst = outT if final_to_out else Rm
                            s.emit("scalar", lambda e, j=j, q=q, dst=dst: e.dma_start(
                                out=dst[j][:, q * 512:(q + 1) * 512], in_=ACC[:, j, :]),
                                inc=d_acc, inc_n=16)

                    panels = [[[wq3sp[g, f] for f in range(16)] for g in range(4)]
                              for q in range(NQ)]
                    PANl = [mktile(sc, [128, 1408], BF16) for _ in range(2)]
                    acc_stage(panels, 11, 4, grp_src, ACC, GRP, evac_f2, PAN=PANl,
                              bf16_pan=True)
            s.fence()

        emit_ffn(0, False)
        if stop_after == "F1":
            emit_tail(nc, s, Rm, outT)
            s.build()
            return nc

        with ExitStack() as scB:
            ckv = mktile(scB, [128, 4, T], BF16)
            cq = mktile(scB, [128, 8, T], BF16)

            with ExitStack() as sc:
                STGl = [mktile(sc, [128, 2048], F32) for _ in range(2)]
                Bmu, Bs, scr = load_res_stats(sc, [Rm[j] for j in range(KD)], "LN", STGl)
                center_res(Bmu, Bs, scr)
            s.fence()

            with ExitStack() as sc:
                PSA = mkpsum(sc, [128, 2048])
                PSB = mkpsum(sc, [128, 2048])

                def evac_ckv(i, ps, pr):
                    s.wait("scalar", c_mm[pr])
                    s.emit("scalar", lambda e, i=i, ps=ps: e.activation(
                        out=ckv[:, i, :], in_=ps[:, :], func=AF.Identity,
                        bias=bdkv[:, i:i + 1]), inc=c_eva)
                    return (c_eva, c_eva.count)

                def evac_cq(i, ps, pr):
                    s.wait("scalar", c_mm[pr])
                    s.emit("scalar", lambda e, i=i, ps=ps: e.activation(
                        out=cq[:, i, :], in_=ps[:, :], func=AF.Identity,
                        bias=bdq[:, i:i + 1]), inc=c_eva)
                    return (c_eva, c_eva.count)

                STGl = [mktile(sc, [128, 2048], F32) for _ in range(2)]
                PANl = [mktile(sc, [128, 2048], BF16) for _ in range(2)]
                stream_panels([wdkv_t[i] for i in range(4)], 16, PSA, PSB, evac_ckv,
                              STG=STGl, PAN=PANl)
                s.wait("tensor", c_eva)
                stream_panels([wdq_t[i] for i in range(8)], 16, PSA, PSB, evac_cq,
                              STG=STGl, PAN=PANl)
            s.fence()

            with ExitStack() as sc:
                kCb = [mktile(sc, [128, T], BF16) for _ in range(2)]
                kRb = [mktile(sc, [128, T], BF16) for _ in range(2)]
                qCb = [mktile(sc, [128, T], BF16) for _ in range(2)]
                krp = qCb
                qRb = [mktile(sc, [128, T], BF16) for _ in range(2)]
                Vtmb = [mktile(sc, [128, 16, 132], BF16) for _ in range(2)]
                rsc = mktile(sc, [128, 512], BF16)
                ptile_full = mktile(sc, [128, T], BF16)
                pTs_full = mktile(sc, [128, 1024], BF16)
                otm = mktile(sc, [128, 128], BF16)
                ropeC = mktile(sc, [128, T], BF16)
                ropeS = mktile(sc, [128, T], BF16)
                tril = mktile(sc, [128, 128], F32)
                rows = mktile(sc, [128, 8], F32)
                HP = [mktile(sc, [128, 4096], BF16) for _ in range(2)]

                dbgt = None
                PBLD = mkpsum(sc, [128, 1024])
                PSF = mkpsum(sc, [128, 2048])
                PO = mkpsum(sc, [128, 512])
                PT = mkpsum(sc, [128, 1024], BF16)

                s.emit("sync", lambda e: e.dma_start(out=ropeC[:], in_=ropeC_c.ap()),
                       inc=d_misc, inc_n=16)
                s.emit("sync", lambda e: e.dma_start(out=ropeS[:], in_=ropeS_c.ap()),
                       inc=d_misc, inc_n=16)
                s.emit("sync", lambda e: e.dma_start(out=tril[:], in_=tril_c.ap()),
                       inc=d_misc, inc_n=16)
                s.wait("vector", d_misc)
                s.wait("tensor", d_misc)

                build_done = {}

                def load_head(h, prev_attend_pe):
                    pr = h % 2
                    if h >= 2:
                        s.wait("sync", c_pe, build_done[h - 2])
                    s.emit("sync", lambda e, h=h, pr=pr: e.dma_start(
                        out=HP[pr][:, 0:2048], in_=hwbf[h, 0]), inc=d_hw[pr], inc_n=16)
                    s.emit("sync", lambda e, h=h, pr=pr: e.dma_start(
                        out=HP[pr][:, 2048:4096], in_=hwbf[h, 1]), inc=d_hw[pr], inc_n=16)

                def build_head_gen(h, prev_ropeq_evv, out):
                    pr = h % 2
                    W = HP[pr]
                    s.wait("tensor", d_hw[pr])
                    if prev_ropeq_evv:
                        s.wait("tensor", c_evv, prev_ropeq_evv)

                    def proj_gen(dstbuf, w0, nc_chunks, rhs, biascol, scale=1.0):
                        evs = []
                        for t in range(NQ):
                            half = (t % 2) * 512
                            if t >= 2:
                                s.wait("tensor", c_eva, evs[t - 2])
                            for c in range(nc_chunks):
                                fn = lambda e, t=t, c=c, half=half, w0=w0, rhs=rhs: e.matmul(
                                    PBLD[:, half:half + 512],
                                    W[:, w0 + c * 128:w0 + (c + 1) * 128],
                                    rhs[:, c, t * 512:(t + 1) * 512],
                                    start=(c == 0), stop=(c == nc_chunks - 1))
                                if c == nc_chunks - 1:
                                    s.emit("tensor", fn, inc=c_pe)
                                else:
                                    s.raw("tensor", fn)
                            s.wait("scalar", c_pe)
                            s.emit("scalar", lambda e, t=t, half=half, dstbuf=dstbuf,
                                   biascol=biascol, scale=scale: e.activation(
                                out=dstbuf[:, t * 512:(t + 1) * 512],
                                in_=PBLD[:, half:half + 512], func=AF.Identity,
                                scale=scale, bias=biascol), inc=c_eva)
                            evs.append(c_eva.count)
                            yield
                        return evs

                    def rope_gen(srcbuf, dstbuf):
                        s.wait("tensor", c_eva)
                        done = 0
                        for t in range(NQ):
                            half = (t % 2) * 512
                            sl = slice(t * 512, (t + 1) * 512)
                            if t >= 2:
                                s.wait("tensor", c_evv, done)
                            s.emit("tensor", lambda e, sl=sl, half=half, srcbuf=srcbuf: e.matmul(
                                PBLD[:, half:half + 512], swapP_sb[:], srcbuf[:, sl],
                                start=True, stop=True), inc=c_pe)
                            sw = c_pe.count
                            s.wait("vector", c_pe, sw)
                            s.raw("vector", lambda e, sl=sl, srcbuf=srcbuf, dstbuf=dstbuf: e.tensor_mul(
                                out=dstbuf[:, sl], in0=srcbuf[:, sl], in1=ropeC[:, sl]))
                            s.raw("vector", lambda e, sl=sl, half=half: e.tensor_mul(
                                out=rsc[:, 0:512], in0=PBLD[:, half:half + 512],
                                in1=ropeS[:, sl]))
                            s.emit("vector", lambda e, sl=sl, dstbuf=dstbuf: e.tensor_add(
                                out=dstbuf[:, sl], in0=dstbuf[:, sl], in1=rsc[:, 0:512]),
                                inc=c_evv)
                            done = c_evv.count
                            yield
                        return done

                    yield from proj_gen(kCb[pr], 0, 4, ckv, buk[:, h:h + 1])
                    yield from proj_gen(krp[pr], 2048, KD, RES, bkr[:, h:h + 1])
                    rope_k_done = yield from rope_gen(krp[pr], kRb[pr])
                    s.wait("tensor", c_evv, rope_k_done)
                    for g in range(2):
                        if g == 1:
                            s.wait("tensor", c_eva)
                        for ktl in range(8):
                            kt = g * 8 + ktl
                            for c in range(4):
                                s.raw("tensor", lambda e, kt=kt, ktl=ktl, c=c, W=W: e.matmul(
                                    PBLD[:, ktl * 128:(ktl + 1) * 128],
                                    ckv[:, c, kt * 128:(kt + 1) * 128],
                                    W[:, 512 + c * 128:512 + (c + 1) * 128],
                                    start=(c == 0), stop=False))
                            fn = lambda e, kt=kt, ktl=ktl, h=h: e.matmul(
                                PBLD[:, ktl * 128:(ktl + 1) * 128], onesrow_f[:],
                                buv_row[0:1, h * 128:(h + 1) * 128], start=False, stop=True)
                            if ktl == 7:
                                s.emit("tensor", fn, inc=c_pe)
                            else:
                                s.raw("tensor", fn)
                        s.wait("scalar", c_pe)
                        s.emit("scalar", lambda e, g=g, pr=pr: e.activation(
                            out=Vtmb[pr][:, g * 8:(g + 1) * 8, 0:128],
                            in_=PBLD[:, :].rearrange("p (a b) -> p a b", b=128),
                            func=AF.Identity), inc=c_eva)
                        yield
                    v_done = c_eva.count
                    s.wait("tensor", c_eva, v_done)
                    s.wait("scalar", c_evv, rope_k_done)
                    yield from proj_gen(qCb[pr], 1024, 8, cq, buq16[:, h:h + 1],
                                        scale=1.0 / 16.0)
                    ready = yield from rope_gen(qCb[pr], qRb[pr])
                    build_done[h] = c_pe.count
                    out["ready"] = ready

                def attend_head(h, ready_c, gen):
                    pr = h % 2
                    kC, kR, qC, qR, Vtm = kCb[pr], kRb[pr], qCb[pr], qRb[pr], Vtmb[pr]
                    s.wait("tensor", c_evv, ready_c)
                    s.wait("tensor", c_eva)
                    psf_free = 0
                    pt_copy = 0
                    pt2_read = 0
                    po_read = 0
                    for qi in range(16):
                        nk = (qi + 1) * 128
                        nsl = (nk + 511) // 512
                        if psf_free:
                            s.wait("tensor", c_eva, psf_free)
                        for sl in range(nsl):
                            w = min(512, nk - sl * 512)
                            s.raw("tensor", lambda e, sl=sl, w=w, qi=qi, qC=qC, kC=kC: e.matmul(
                                PSF[:, sl * 512:sl * 512 + w],
                                qC[:, qi * 128:(qi + 1) * 128],
                                kC[:, sl * 512:sl * 512 + w], start=True, stop=False))
                            fn = lambda e, sl=sl, w=w, qi=qi, qR=qR, kR=kR: e.matmul(
                                PSF[:, sl * 512:sl * 512 + w],
                                qR[:, qi * 128:(qi + 1) * 128],
                                kR[:, sl * 512:sl * 512 + w], start=False, stop=True)
                            if sl == nsl - 1:
                                s.emit("tensor", fn, inc=c_pe)
                            else:
                                s.raw("tensor", fn)
                        sc_done = c_pe.count
                        s.wait("vector", c_pe, sc_done)
                        s.emit("vector", lambda e, nk=nk: e.tensor_add(
                            out=PSF[:, nk - 128:nk], in0=PSF[:, nk - 128:nk],
                            in1=tril[:]), inc=c_evv)
                        s.emit("vector", lambda e, nk=nk: e.reduce_max(
                            out=rows[:, 4:5], in_=PSF[:, 0:nk], axis=AX.X,
                            negate=True), inc=c_evv)
                        s.wait("vector", c_evv)
                        nm_done = c_evv.count
                        s.wait("scalar", c_evv, nm_done)
                        if pt_copy:
                            pass
                        s.emit("scalar", lambda e, nk=nk: e.activation(
                            out=ptile_full[:, 0:nk], in_=PSF[:, 0:nk], func=AF.Exp,
                            bias=rows[:, 4:5]), inc=c_eva)
                        psf_free = c_eva.count
                        exp_done = c_eva.count
                        if gen is not None:
                            next(gen, None)
                            if qi >= 10:
                                next(gen, None)
                        nkt = nk // 128
                        first_pv = True
                        for g0 in range(0, nkt, 8):
                            gn = min(8, nkt - g0)
                            s.wait("tensor", c_eva, exp_done)
                            if pt_copy:
                                s.wait("tensor", c_evv, pt_copy)
                            if pt2_read:
                                s.wait("tensor", c_eva, pt2_read)
                            for ktl in range(gn):
                                fn = lambda e, g0=g0, ktl=ktl: e.transpose(
                                    PT[:, ktl * 128:(ktl + 1) * 128],
                                    ptile_full[:, (g0 + ktl) * 128:(g0 + ktl + 1) * 128],
                                    ident[:])
                                if ktl == gn - 1:
                                    s.emit("tensor", fn, inc=c_pe)
                                else:
                                    s.raw("tensor", fn)
                            tr_done = c_pe.count
                            s.wait("vector", c_pe, tr_done)
                            if first_pv and po_read:
                                pass
                            s.emit("vector", lambda e, gn=gn: e.tensor_copy(
                                out=pTs_full[:, 0:gn * 128], in_=PT[:, 0:gn * 128]),
                                inc=c_evv)
                            pt_copy = c_evv.count
                            s.wait("tensor", c_evv, pt_copy)
                            if first_pv and po_read:
                                s.wait("tensor", c_evv, po_read)
                            for ktl in range(gn):
                                kt = g0 + ktl
                                fn = lambda e, kt=kt, ktl=ktl, nkt=nkt, Vtm=Vtm: e.matmul(
                                    PO[:, 0:129], pTs_full[:, ktl * 128:(ktl + 1) * 128],
                                    Vtm[:, kt, 0:129],
                                    start=(kt == 0), stop=(kt == nkt - 1))
                                if ktl == gn - 1:
                                    s.emit("tensor", fn, inc=c_pe)
                                else:
                                    s.raw("tensor", fn)
                            first_pv = False
                        pv_done = c_pe.count
                        s.wait("vector", c_pe, pv_done)
                        s.emit("vector", lambda e: e.reciprocal(
                            out=rows[:, 3:4], in_=PO[:, 128:129]), inc=c_evv)
                        s.wait("vector", c_evv)
                        s.emit("vector", lambda e: e.tensor_scalar(
                            out=otm[:], in0=PO[:, 0:128], scalar1=rows[:, 3:4],
                            scalar2=1.0, op0=ALU.mult, op1=ALU.mult), inc=c_evv)
                        po_read = c_evv.count
                        fin = c_evv.count
                        s.wait("tensor", c_evv, fin)
                        if pt_copy:
                            s.wait("tensor", c_evv, pt_copy)
                        if pt2_read:
                            s.wait("tensor", c_eva, pt2_read)
                        s.emit("tensor", lambda e: e.transpose(
                            PT[:, 0:128], otm[:], ident[:]), inc=c_pe)
                        ot_done = c_pe.count
                        s.wait("scalar", c_pe, ot_done)
                        s.wait("scalar", d_ev)
                        s.emit("scalar", lambda e, qi=qi, pr=pr: e.activation(
                            out=EVB[pr][:, qi * 128:(qi + 1) * 128], in_=PT[:, 0:128],
                            func=AF.Identity), inc=c_eva)
                        pt2_read = c_eva.count
                    s.wait("scalar", c_eva)
                    s.emit("scalar", lambda e, h=h, pr=pr: e.dma_start(
                        out=osp[h], in_=EVB[pr][:]), inc=d_ev, inc_n=16)
                    return c_pe.count

                swapP_sb = mktile(sc, [128, 128], BF16)
                s.emit("sync", lambda e: e.dma_start(out=swapP_sb[:], in_=swapP_c.ap()),
                       inc=d_misc, inc_n=16)
                s.wait("tensor", d_misc)
                s.wait("vector", d_misc)
                s.emit("vector", lambda e: e.memset(Vtmb[0][:, :, 128:129], 1.0), inc=c_evv)
                s.emit("vector", lambda e: e.memset(Vtmb[1][:, :, 128:129], 1.0), inc=c_evv)
                s.wait("tensor", c_evv)

                att_pe = 0
                load_head(0, 0)
                bst = {}
                g0 = build_head_gen(0, 0, bst)
                for _ in g0:
                    pass
                ready = bst["ready"]
                prev_ropeq = ready
                for h in range(1, H):
                    load_head(h, att_pe)
                    bst = {}
                    gh = build_head_gen(h, prev_ropeq, bst)
                    att_pe = attend_head(h - 1, ready, gh)
                    for _ in gh:
                        pass
                    ready = bst["ready"]
                    prev_ropeq = ready
                att_pe = attend_head(H - 1, ready, None)
            s.fence()

            with ExitStack() as sc:
                EVF = [mktile(sc, [128, 2048], F32) for _ in range(2)]
                RST = [mktile(sc, [128, 2048], F32) for _ in range(2)]
                PSA = mkpsum(sc, [128, 2048])
                PSB = mkpsum(sc, [128, 2048])
                for j in range(16):
                    s.emit("sync", lambda e, j=j: e.dma_start(
                        out=RES[:, j, :], in_=osp[j]), inc=d_res, inc_n=16)
                s.wait("tensor", d_res)
                s.wait("gpsimd", d_res)

                def evac_b3(i, ps, pr):
                    s.wait("sync", c_evv)
                    s.emit("sync", lambda e, i=i, pr=pr: e.dma_start(
                        out=RST[pr][:], in_=Rm[i]), inc=d_xb, inc_n=16)
                    s.wait("scalar", c_mm[pr])
                    s.wait("scalar", c_evv)
                    s.emit("scalar", lambda e, i=i, ps=ps, pr=pr: e.activation(
                        out=EVF[pr][:], in_=ps[:, :], func=AF.Identity,
                        bias=bo[:, i:i + 1]), inc=c_eva)
                    free = (c_eva, c_eva.count)
                    s.wait("vector", c_eva)
                    s.wait("vector", d_xb)
                    s.wait("vector", d_acc)
                    s.emit("vector", lambda e, pr=pr: e.tensor_add(
                        out=EVF[pr][:], in0=EVF[pr][:], in1=RST[pr][:]), inc=c_evv)
                    s.wait("scalar", c_evv)
                    s.emit("scalar", lambda e, i=i, pr=pr: e.dma_start(
                        out=Rm[i], in_=EVF[pr][:]), inc=d_acc, inc_n=16)
                    return free

                STGl = [mktile(sc, [128, 2048], F32) for _ in range(2)]
                PANl = [mktile(sc, [128, 2048], BF16) for _ in range(2)]
                stream_panels([wo_t[i] for i in range(16)], 16, PSA, PSB, evac_b3,
                              STG=STGl, PAN=PANl)
        s.fence()
        if stop_after == "B":
            emit_tail(nc, s, Rm, outT)
            s.build()
            return nc

        emit_ffn(1, True)
        for name, c in list(s.sems.items()):
            if name.startswith("dma_"):
                s.wait("sync", c)
        s.build()
    return nc


def emit_tail(nc, s, Rm, outT, STG=None):
    d_tail = s.sem("dma_tail")
    if STG is None:
        STG = [s.st.enter_context(nc.sbuf_tensor(f"tailstg{i}", [128, 2048], F32))
               for i in range(2)]
    for j in range(KD):
        pr = j % 2
        s.emit("sync", lambda e, j=j, pr=pr: e.dma_start(
            out=STG[pr][:], in_=Rm[j]), inc=d_tail, inc_n=16)
        s.wait("sync", d_tail)
        s.emit("sync", lambda e, j=j, pr=pr: e.dma_start(
            out=outT[j], in_=STG[pr][:]), inc=d_tail, inc_n=16)
    for name, c in list(s.sems.items()):
        if name.startswith("dma_"):
            s.wait("sync", c)


_NC_CACHE = {}


def kernel(**inputs):
    import os
    stop_after = os.environ.get("MK_STOP", "full")
    hp = {
        "win_t": tile_lhsT(np.asarray(inputs["L0_attn_a_W_in"])),
        "wout_t": tile_lhsT(np.asarray(inputs["L0_attn_a_W_out"]), groups=2),
        "wm1_t": tile_lhsT(np.asarray(inputs["L0_attn_a_W_mlp1"])),
        "wm2_t": tile_lhsT(np.asarray(inputs["L0_attn_a_W_mlp2"]), groups=4),
        "w1_t": tile_lhsT(np.asarray(inputs["L0_ffn_w1"])),
        "w2_t": tile_lhsT(np.asarray(inputs["L0_ffn_w2"])),
        "w3_t": tile_lhsT(np.asarray(inputs["L0_ffn_w3"]), groups=4),
        "wdkv_t": tile_lhsT(np.asarray(inputs["L0_attn_b_W_DKV_w"])),
        "wuk_t": tile_lhsT(np.asarray(inputs["L0_attn_b_W_UK_w"])),
        "wuv_t": tile_lhsT(np.asarray(inputs["L0_attn_b_W_UV_w"])),
        "wdq_t": tile_lhsT(np.asarray(inputs["L0_attn_b_W_DQ_w"])),
        "wuq_t": tile_lhsT(np.asarray(inputs["L0_attn_b_W_UQ_w"])),
        "wkr_t": tile_lhsT(np.asarray(inputs["L0_attn_b_W_KR_w"])),
        "wo_t": tile_lhsT(np.asarray(inputs["L0_attn_b_W_O_w"])),
        "convw": np.ascontiguousarray(
            np.asarray(inputs["L0_attn_a_conv_w"]).reshape(32, 128, 4).transpose(1, 0, 2)),
        "convb": chunk_vec(inputs["L0_attn_a_conv_b"]),
        "bdkv": chunk_vec(inputs["L0_attn_b_W_DKV_b"]),
        "buk": chunk_vec(inputs["L0_attn_b_W_UK_b"]),
        "buv_row": np.ascontiguousarray(
            np.asarray(inputs["L0_attn_b_W_UV_b"]).reshape(1, 2048)),
        "bdq": chunk_vec(inputs["L0_attn_b_W_DQ_b"]),
        "buq": chunk_vec(inputs["L0_attn_b_W_UQ_b"]),
        "bkr": chunk_vec(inputs["L0_attn_b_W_KR_b"]),
        "bo": chunk_vec(inputs["L0_attn_b_W_O_b"]),
    }
    hp["hwcat_t"] = np.ascontiguousarray(
        np.concatenate([hp["wuk_t"], hp["wuv_t"], hp["wuq_t"]], axis=2))
    x = np.asarray(inputs["L0_attn_a_x"])
    nc = build_program(stop_after)
    in_maps = []
    for b in range(8):
        m = dict(hp)
        m["xt"] = np.ascontiguousarray(x[b].T)
        in_maps.append(m)
    res = run_bass_kernel_spmd(nc, in_maps, core_ids=list(range(8)))
    out = np.empty((8, T, D), np.float32)
    for b in range(8):
        ot = np.asarray(res.results[b]["outT"])
        out[b] = ot.reshape(D, T).T
    return out
